# version 03
# baseline (speedup 1.0000x reference)
import contextlib

import numpy as np
import concourse.bass as bass
import concourse.mybir as mybir
from concourse.bass_utils import run_bass_kernel_spmd

F32 = mybir.dt.float32
BF16 = mybir.dt.bfloat16
I32 = mybir.dt.int32
AF = mybir.ActivationFunctionType
ALU = mybir.AluOpType
AX = mybir.AxisListType

ENGINES = ("sync", "scalar", "vector", "tensor", "gpsimd")
DMA_QUEUES = {"sync": 8, "scalar": 6, "gpsimd": 8}

T = 4096
TC = 512
NTC = T // TC
D = 1024
NDC = D // 128
DI = 2048
NIC = DI // 128
NST = 16
DTR = 64
H0 = 3072
NH0 = H0 // 128
HM = 4096
NHM = HM // 128
NE = 8
HR = 4096
NHR = HR // 128
EPS = 1e-5
ST_TOK = 1024
NSUP = T // ST_TOK
PAIRS = [[0, 1], [2, 3], [4, 5], [6, 7]]


class Reg:
    __slots__ = ("name", "writer", "readers")

    def __init__(self, name):
        self.name = name
        self.writer = None
        self.readers = []


class Prog:
    def __init__(self, nc, stack):
        self.nc = nc
        self.stack = stack
        self.ops = {e: [] for e in ENGINES}
        self.psem = {e: self._mk_sem("p_" + e) for e in ENGINES}
        self.pcount = {e: 0 for e in ENGINES}
        self.dsem = {}
        self.dcount = {}
        for q, n in DMA_QUEUES.items():
            self.dsem[q] = [self._mk_sem(f"d_{q}{i}") for i in range(n)]
            self.dcount[q] = [0] * n
        self.drr = {q: 0 for q in DMA_QUEUES}
        self.frontier = {e: {} for e in ENGINES}
        self.csem = self._mk_sem("coll")
        self.ccount = 0
        self.n_ops = 0

    def _mk_sem(self, name):
        return self.stack.enter_context(self.nc.semaphore(name))

    def region(self, name):
        return Reg(name)

    def _waits_for(self, ename, reads, writes):
        waits = []
        fr = self.frontier[ename]

        def need(dep, skip_same_engine=False):
            if dep is None:
                return
            sem, val, eng = dep
            if skip_same_engine and eng == ename:
                return
            if fr.get(sem, 0) < val:
                waits.append((sem, val))
                fr[sem] = val

        for r in reads:
            need(r.writer)
        for w in writes:
            need(w.writer, skip_same_engine=True)
            for rd in w.readers:
                need(rd, skip_same_engine=True)
        return waits

    def op(self, ename, fn, reads=(), writes=(), tag=""):
        waits = self._waits_for(ename, reads, writes)
        self.pcount[ename] += 1
        comp = (self.psem[ename], self.pcount[ename], ename)
        for r in reads:
            r.readers.append(comp)
        for w in writes:
            w.writer = comp
            w.readers = []
        self.ops[ename].append((waits, fn, comp[0], 1, tag))
        self.n_ops += 1

    def dma(self, q, out, in_, reads=(), writes=(), tag="", transpose=False):
        i = self.drr[q]
        self.drr[q] = (i + 1) % len(self.dsem[q])
        sem = self.dsem[q][i]
        prev = self.dcount[q][i]
        waits = self._waits_for(q, reads, writes)
        fr = self.frontier[q]
        if prev and fr.get(sem, 0) < prev:
            waits.append((sem, prev))
            fr[sem] = prev
        self.dcount[q][i] = prev + 16
        comp = (sem, prev + 16, "@dma")
        for r in reads:
            r.readers.append(comp)
        for w in writes:
            w.writer = comp
            w.readers = []
        if transpose:
            def fn(eng, out=out, in_=in_):
                return eng.dma_start_transpose(out=out, in_=in_)
        else:
            def fn(eng, out=out, in_=in_):
                return eng.dma_start(out=out, in_=in_)
        self.ops[q].append((waits, fn, sem, 16, tag))
        self.n_ops += 1

    def collective(self, kind, op, ins_, outs_, reads=(), writes=(), tag="coll"):
        waits = self._waits_for("gpsimd", reads, writes)
        self.ccount += 1
        comp = (self.csem, self.ccount, "@coll")
        for r in reads:
            r.readers.append(comp)
        for w in writes:
            w.writer = comp
            w.readers = []

        def fn(eng, kind=kind, op=op, ins_=ins_, outs_=outs_):
            return eng.collective_compute(kind, op, PAIRS,
                                          [a.opt() for a in ins_],
                                          [a.opt() for a in outs_])

        self.ops["gpsimd"].append((waits, fn, self.csem, 1, tag))
        self.n_ops += 1

    def barrier(self):
        deps = []
        for e in ENGINES:
            if self.pcount[e]:
                deps.append((self.psem[e], self.pcount[e]))
        for q in DMA_QUEUES:
            for i, c in enumerate(self.dcount[q]):
                if c:
                    deps.append((self.dsem[q][i], c))
        if self.ccount:
            deps.append((self.csem, self.ccount))
        for e in ENGINES:
            fr = self.frontier[e]
            ws = [(s, v) for (s, v) in deps if fr.get(s, 0) < v]
            for s, v in ws:
                fr[s] = v
            if ws:
                self.ops[e].append((ws, None, None, 0, "barrier"))

    def build(self):
        nc = self.nc
        with nc.Block() as block:
            for ename in ENGINES:
                ops = self.ops[ename]

                def mk(ops):
                    def run(eng):
                        for waits, fn, sem, inc, tag in ops:
                            for (s, v) in waits:
                                eng.wait_ge(s, v)
                            if fn is not None:
                                instr = fn(eng)
                                if sem is not None and instr is not None:
                                    instr.then_inc(sem, inc)
                    return run

                getattr(block, ename)(mk(ops))


class Arena:
    def __init__(self, p, base_ap, cols, name):
        self.p = p
        self.base = base_ap
        self.cols = cols
        self.off = 0
        self.name = name
        self.buddy = None

    def alloc(self, name, c):
        assert self.off + c <= self.cols, \
            f"{self.name}/{name}: overflow {self.off}+{c}>{self.cols}"
        apv = self.base[:, self.off:self.off + c]
        self.off += c
        if self.buddy is not None:
            assert self.off + 2 * self.buddy.off <= self.cols, \
                f"{self.name}/{name}: collides with top arena"
        return apv

    def reset(self):
        self.off = 0


class ArenaTop:

    def __init__(self, p, base_bf16, bf16_cols, name, buddy):
        self.p = p
        self.base = base_bf16
        self.cols = bf16_cols
        self.off = 0
        self.name = name
        self.buddy = buddy
        buddy.buddy = self

    def alloc(self, name, c):
        self.off += c
        lo = self.cols - 2 * self.off
        assert lo >= self.buddy.off, \
            f"{self.name}/{name}: collides with bf16 arena"
        return self.base[:, lo:lo + 2 * c].bitcast(F32)

    def reset(self):
        self.off = 0


def build_program(upto=None, dumps=(), debug_r2b=False):
    nc = bass.Bass("TRN2", target_bir_lowering=False)
    stack = contextlib.ExitStack()
    p = Prog(nc, stack)

    ins = {}

    def din(name, shape, dt=F32):
        ins[name] = nc.dram_tensor(name, shape, dt, kind="ExternalInput").ap()
        return ins[name]

    din("x_in", [T, D])
    for _v in ("L0_attn_ln_g", "L0_attn_ln_b", "L0_attn_sq_b", "L0_attn_srw_b",
               "L1_attn_ln1_w", "L1_attn_ln1_b", "L1_attn_ln2_w",
               "L1_attn_ln2_b", "L1_attn_time_decay", "L1_attn_time_first",
               "L1_attn_time_mix_k", "L1_attn_time_mix_v",
               "L1_attn_time_mix_r", "L1_attn_f_time_mix_k",
               "L1_attn_f_time_mix_r"):
        din(_v, [128, NDC])
    for _v in ("L0_attn_conv_b", "L0_attn_dt_b", "L0_attn_D"):
        din(_v, [128, NIC])
    din("L0_attn_sq_w", [D, D])
    din("L0_attn_srw_w", [D, D])
    din("L0_attn_in_proj_w", [D, 2 * DI])
    din("L0_attn_conv_w", [128, NIC * 4])
    din("L0_attn_x_proj_w", [DI, DTR + 2 * NST])
    din("L0_attn_dt_w", [DTR, DI])
    din("L0_attn_A_log", [128, NIC * NST])
    din("L0_attn_out_proj_w", [DI, D])
    din("L0_ffn_w_gate", [H0, D], I32)
    din("L0_ffn_w_up", [H0, D], I32)
    din("L0_ffn_w_down", [D, H0], I32)
    din("L1_attn_w_key", [D, D]); din("L1_attn_w_value", [D, D])
    din("L1_attn_w_recept", [D, D]); din("L1_attn_w_output", [D, D])
    din("L1_attn_f_w_key", [D, HR]); din("L1_attn_f_w_recept", [D, D])
    din("L1_attn_f_w_value", [HR, D])
    din("L1_ffn_router_w", [NE, D])
    din("L1_ffn_w1", [NE * HM, D])
    din("L1_ffn_w2", [NE * D, HM])
    din("parity", [128, 1])
    din("iota_row", [1, T])
    din("identity", [128, 128])
    din("res_scale_col", [128, 1])

    out_ap = nc.dram_tensor("out", [T, D], F32, kind="ExternalOutput").ap()

    scr_tensors = {}

    def scr(name, shape, dt=F32):
        t_ = nc.dram_tensor(name, shape, dt, kind="Internal").ap()
        scr_tensors[name] = t_
        return t_

    xfA = scr("xfA", [D, T])
    xfB = scr("xfB", [D, T])
    h_scr = scr("h_scr", [D, T], BF16)
    u_pre = scr("u_pre", [DI, T])
    z_scr = scr("z_scr", [DI, T])
    P_scr = scr("P_scr", [DI, T])
    B_scr = scr("B_scr", [NST, T])
    C_scr = scr("C_scr", [NST, T])
    y_scr = scr("y_scr", [DI, T])
    yg_scr = scr("yg_scr", [DI, T], BF16)
    xn1_scr = scr("xn1_scr", [D, T])
    xn2_scr = scr("xn2_scr", [D, T])
    k_scr = scr("k_scr", [D, T])
    v_scr = scr("v_scr", [D, T])
    sr_scr = scr("sr_scr", [D, T])
    al_scr = scr("al_scr", [D, T])
    bl_scr = scr("bl_scr", [D, T])
    xb_scr = scr("xb_scr", [D, T])
    sg_scr = scr("sg_scr", [D, T])
    xnm_scr = scr("xnm_scr", [D, T], BF16)
    gate_scr = scr("gate_scr", [NE, T])
    wg_bf = scr("wg_bf", [H0, D], BF16)
    wu_bf = scr("wu_bf", [H0, D], BF16)
    wd_bf = scr("wd_bf", [D, H0], BF16)
    w1_bf = scr("w1_bf", [NE * HM, D], BF16)
    w2_bf = scr("w2_bf", [NE * D, HM], BF16)
    wgT_scr = scr("wgT_scr", [D, H0], BF16)
    wuT_scr = scr("wuT_scr", [D, H0], BF16)
    wdT_scr = scr("wdT_scr", [H0, D], BF16)
    w1T_scr = scr("w1T_scr", [NE * D, HM], BF16)
    w2T_scr = scr("w2T_scr", [NE * HM, D], BF16)
    opw_bf = scr("opw_bf", [DI, D], BF16)
    cc_in = {}
    cc_out = {}
    for nm, cols in [("c1", 48), ("c2", 256), ("c3", 8), ("c4", 16), ("c5", 8)]:
        cc_in[nm] = scr("cin_" + nm, [128, cols])
        cc_out[nm] = scr("cout_" + nm, [256, cols])

    sreg = {nm: p.region(nm) for nm in list(scr_tensors)}
    sreg["out"] = p.region("out")

    def sb(name, shape, dt=F32):
        return stack.enter_context(nc.sbuf_tensor(name, shape, dt))[:]

    AF_t = sb("AFa", [128, 10240], F32)
    BF_t = sb("BFa", [128, 9216], F32)
    WA_t = sb("WAa", [128, 33280], BF16)
    PR_t = sb("PRa", [128, 1024], F32)
    PRB_t = sb("PRBa", [128, 128], BF16)
    psb = [stack.enter_context(nc.psum_tensor(f"pb{i}", [128, 512], F32))[:]
           for i in range(8)]
    rps = [p.region(f"ps{i}") for i in range(8)]

    af = Arena(p, AF_t, 10240, "af")
    bfa = Arena(p, BF_t, 9216, "bf")
    wa = Arena(p, WA_t, 33280, "wa")
    waf = ArenaTop(p, WA_t, 33280, "waf", wa)
    pers = Arena(p, PR_t, 1024, "pers")
    persb = Arena(p, PRB_t, 128, "persb")

    def phase_reset():
        p.barrier()
        af.reset()
        bfa.reset()
        wa.reset()
        waf.reset()

    def v_op(fn, reads, writes, tag=""):
        p.op("vector", fn, reads=reads, writes=writes, tag=tag)

    def s_op(fn, reads, writes, tag=""):
        p.op("scalar", fn, reads=reads, writes=writes, tag=tag)

    def t_op(fn, reads, writes, tag=""):
        p.op("tensor", fn, reads=reads, writes=writes, tag=tag)

    def g_op(fn, reads, writes, tag=""):
        p.op("gpsimd", fn, reads=reads, writes=writes, tag=tag)

    def mm_acc(ps_ap, pairs, reads, psreg, tag=""):
        def fn(e, pairs=tuple(pairs), ps_ap=ps_ap):
            last = None
            n = len(pairs)
            for i, (lh, rh) in enumerate(pairs):
                last = e.matmul(ps_ap, lh, rh, start=(i == 0), stop=(i == n - 1))
            return last
        t_op(fn, reads, [psreg], tag)

    def mm_banks(pairs_per_bank, reads, psregs, first, last_, tag=""):
        def fn(e, ppb=tuple(pairs_per_bank), first=first, last_=last_):
            last = None
            for (ps_ap, lh, rh) in ppb:
                last = e.matmul(ps_ap, lh, rh, start=first, stop=last_)
            return last
        t_op(fn, reads, psregs, tag)

    def pe_tr(ps_ap, in_ap, reads, psreg, tag="tr"):
        k = in_ap.shape[0]
        t_op(lambda e, ps_ap=ps_ap, in_ap=in_ap, k=k: e.transpose(
            ps_ap, in_ap, s_ident[0:k, 0:k]),
             list(reads) + [r_ident], [psreg], tag)

    def vcopy(out, in_, reads, writes, tag="cp"):
        v_op(lambda e, out=out, in_=in_: e.tensor_copy(out, in_), reads, writes, tag)

    def scopy(out, in_, reads, writes, scale=None, tag="scp"):
        if scale is None:
            s_op(lambda e, out=out, in_=in_: e.copy(out, in_), reads, writes, tag)
        else:
            s_op(lambda e, out=out, in_=in_, scale=scale: e.activation(
                out, in_, AF.Copy, scale=scale), reads, writes, tag)

    def tt(op, out, a, b, reads, writes, tag="tt"):
        v_op(lambda e, out=out, a=a, b=b, op=op: e.tensor_tensor(out, a, b, op),
             reads, writes, tag)

    def stt(out, in0, scalar, in1, op0, op1, reads, writes, tag="stt"):
        v_op(lambda e, out=out, in0=in0, scalar=scalar, in1=in1, op0=op0, op1=op1:
             e.scalar_tensor_tensor(out, in0, scalar, in1, op0, op1),
             reads, writes, tag)

    def tss(out, in_, scalar, op, reads, writes, tag="tss"):
        v_op(lambda e, out=out, in_=in_, scalar=scalar, op=op:
             e.tensor_single_scalar(out, in_, scalar, op), reads, writes, tag)

    def tsc2(out, in_, s1, s2, op0, op1, reads, writes, tag="ts2"):
        v_op(lambda e, out=out, in_=in_, s1=s1, s2=s2, op0=op0, op1=op1:
             e.tensor_scalar(out, in_, s1, s2, op0, op1), reads, writes, tag)

    def act(out, in_, func, reads, writes, bias=0.0, scale=1.0, tag="act"):
        s_op(lambda e, out=out, in_=in_, func=func, bias=bias, scale=scale:
             e.activation(out, in_, func, bias=bias, scale=scale),
             reads, writes, tag)

    def scan(out, a, b, init, reads, writes, tag="scan"):
        v_op(lambda e, out=out, a=a, b=b, init=init: e.tensor_tensor_scan(
            out, a, b, init, ALU.mult, ALU.add), reads, writes, tag)

    s_ident = pers.alloc("ident", 128)
    r_ident = p.region("ident")
    p.dma("sync", s_ident, ins["identity"], writes=[r_ident])
    s_parity = pers.alloc("parity", 1)
    r_parity = p.region("parity")
    p.dma("sync", s_parity, ins["parity"], writes=[r_parity])
    s_rs = pers.alloc("res_scale", 1)
    r_rs = p.region("res_scale")
    p.dma("sync", s_rs, ins["res_scale_col"], writes=[r_rs])

    def load_vec(name):
        src = ins[name]
        n = src.shape[1]
        tile = pers.alloc(name, n)
        reg = p.region(name)
        p.dma("sync", tile, src, writes=[reg])
        return tile, reg

    s_ln_g, r_ln_g = load_vec("L0_attn_ln_g")
    s_ln_b, r_ln_b = load_vec("L0_attn_ln_b")
    s_sq_b, r_sq_b = load_vec("L0_attn_sq_b")
    s_srw_b, r_srw_b = load_vec("L0_attn_srw_b")
    s_conv_b, r_conv_b = load_vec("L0_attn_conv_b")
    s_dt_b, r_dt_b = load_vec("L0_attn_dt_b")
    s_Dsk, r_Dsk = load_vec("L0_attn_D")
    s_ln1_w, r_ln1_w = load_vec("L1_attn_ln1_w")
    s_ln1_b, r_ln1_b = load_vec("L1_attn_ln1_b")
    s_ln2_w, r_ln2_w = load_vec("L1_attn_ln2_w")
    s_ln2_b, r_ln2_b = load_vec("L1_attn_ln2_b")
    s_td, r_td = load_vec("L1_attn_time_decay")
    s_tf, r_tf = load_vec("L1_attn_time_first")
    s_tmk, r_tmk = load_vec("L1_attn_time_mix_k")
    s_tmv, r_tmv = load_vec("L1_attn_time_mix_v")
    s_tmr, r_tmr = load_vec("L1_attn_time_mix_r")
    s_ftmk, r_ftmk = load_vec("L1_attn_f_time_mix_k")
    s_ftmr, r_ftmr = load_vec("L1_attn_f_time_mix_r")

    s_convw = pers.alloc("convw", NIC * 4)
    r_convw = p.region("convw")
    p.dma("sync", s_convw, ins["L0_attn_conv_w"], writes=[r_convw])
    s_A = pers.alloc("A", NIC * NST)
    r_A = p.region("A")
    p.dma("sync", s_A, ins["L0_attn_A_log"], writes=[r_A])
    act(s_A, s_A, AF.Exp, [r_A], [r_A], tag="Aexp")
    tss(s_A, s_A, -1.0, ALU.mult, [r_A], [r_A], tag="Aneg")
    s_negw = pers.alloc("negw", NDC)
    r_negw = p.region("negw")
    s_lam = pers.alloc("lam", NDC)
    r_lam = p.region("lam")
    s_Eu = pers.alloc("Eu", NDC)
    r_Eu = p.region("Eu")
    act(s_negw, s_td, AF.Exp, [r_td], [r_negw])
    tss(s_negw, s_negw, -1.0, ALU.mult, [r_negw], [r_negw])
    act(s_lam, s_negw, AF.Exp, [r_negw], [r_lam])
    act(s_Eu, s_tf, AF.Exp, [r_tf], [r_Eu])

    s_iorow = sb("s_iorow", [1, TC], F32)
    r_iorow = p.region("iorow")
    s_ones_row = sb("s_ones_row", [1, 128], F32)
    r_ones_row = p.region("ones_row")
    v_op(lambda e: e.memset(s_ones_row, 1.0), [], [r_ones_row])
    s_ones_col = sb("s_ones_col", [128, 1], F32)
    r_ones_col = p.region("ones_col")
    v_op(lambda e: e.memset(s_ones_col, 1.0), [], [r_ones_col])
    s_onesT = sb("s_onesT", [128, TC], F32)
    r_onesT = p.region("onesT")
    v_op(lambda e: e.memset(s_onesT, 1.0), [], [r_onesT])
    s_wrow = sb("s_wrow", [1, TC], F32)
    r_wrow = p.region("wrow")
    s_eps = sb("s_eps", [128, 1], F32)
    r_eps = p.region("eps")
    v_op(lambda e: e.memset(s_eps, EPS), [], [r_eps])
    s_i127 = sb("s_i127", [128, 1], F32)
    r_i127 = p.region("i127")
    v_op(lambda e: e.memset(s_i127, 1.0 / 127.0), [], [r_i127])

    s_hcar = sb("s_hcar", [128, NIC * NST], F32)
    r_hcar = p.region("hcar")
    s_Pcar = sb("s_Pcar", [128, NIC], F32)
    r_Pcar = p.region("Pcar")
    s_h0eff = sb("s_h0eff", [128, NIC * NST], F32)
    r_h0eff = p.region("h0eff")
    s_acar = sb("s_acar", [128, NDC], F32)
    r_acar = p.region("acar")
    s_bcar = sb("s_bcar", [128, NDC], F32)
    r_bcar = p.region("bcar")
    s_ab0 = sb("s_ab0", [128, 2 * NDC], F32)
    r_ab0 = p.region("ab0")
    s_cc = sb("s_cc", [128, 256], F32)
    r_cc = p.region("cc")
    v_op(lambda e: e.memset(s_hcar, 0.0), [], [r_hcar])
    v_op(lambda e: e.memset(s_Pcar, 0.0), [], [r_Pcar])
    v_op(lambda e: e.memset(s_acar, 0.0), [], [r_acar])
    v_op(lambda e: e.memset(s_bcar, 0.0), [], [r_bcar])

    s_routT = persb.alloc("routT", NDC * NE)
    r_routT = p.region("routT")
    s_rout_raw = sb("s_rout_raw", [8, D], F32)
    r_rout_raw = p.region("rout_raw")
    p.dma("sync", s_rout_raw, ins["L1_ffn_router_w"], writes=[r_rout_raw])
    for dc in range(NDC):
        t_op(lambda e, dc=dc: e.transpose(psb[7][0:128, 0:8],
                                          s_rout_raw[0:8, dc * 128:(dc + 1) * 128],
                                          s_ident[0:8, 0:8]),
             [r_rout_raw, r_ident], [rps[7]], "routT")
        vcopy(s_routT[:, dc * NE:(dc + 1) * NE], psb[7][0:128, 0:8],
              [rps[7]], [r_routT])

    s_prep = sb("s_prep", [128, 4096], BF16)
    r_prep = p.region("prep")
    s_prep2 = sb("s_prep2", [128, 4096], BF16)
    r_prep2 = [p.region("prep2a")]

    def prep_cast(src, dst, rows, cols, rdst, tag):
        if cols <= 4096:
            per = 4096 // cols
            for r0 in range(0, rows, 128 * per):
                nr = min(128 * per, rows - r0)
                npt = nr // 128
                view = s_prep[:, 0:npt * cols].rearrange(
                    "p (c d) -> p c d", d=cols)
                p.dma("gpsimd", view,
                      src[r0:r0 + nr, :].rearrange("(c p) d -> p c d", p=128),
                      reads=[], writes=[r_prep], tag=tag + "l")
                p.dma("gpsimd",
                      dst[r0:r0 + nr, :].rearrange("(c p) d -> p c d", p=128),
                      view, reads=[r_prep], writes=[rdst], tag=tag + "s")
        else:
            half = cols // 2
            for r0 in range(0, rows, 128):
                for c0 in (0, half):
                    view = s_prep[:, 0:half]
                    p.dma("gpsimd", view, src[r0:r0 + 128, c0:c0 + half],
                          reads=[], writes=[r_prep], tag=tag + "l")
                    p.dma("gpsimd", dst[r0:r0 + 128, c0:c0 + half], view,
                          reads=[r_prep], writes=[rdst], tag=tag + "s")

    prep_cast(ins["L0_ffn_w_gate"], wg_bf, H0, D, sreg["wg_bf"], "pg")
    prep_cast(ins["L0_ffn_w_up"], wu_bf, H0, D, sreg["wu_bf"], "pu")
    prep_cast(ins["L0_ffn_w_down"], wd_bf, D, H0, sreg["wd_bf"], "pd")
    prep_cast(ins["L0_attn_out_proj_w"], opw_bf, DI, D, sreg["opw_bf"], "pop")
    prep_cast(ins["L1_ffn_w1"], w1_bf, NE * HM, D, sreg["w1_bf"], "pw1")
    prep_cast(ins["L1_ffn_w2"], w2_bf, NE * D, HM, sreg["w2_bf"], "pw2")

    _prep_i = [0]

    def prep_transpose(src, rsrc, dst, rdst, rows, cols, tag):
        for c0 in range(0, cols, 128):
            for r0 in range(0, rows, 4096):
                nr = min(4096, rows - r0)
                i = 0
                view = s_prep2[:, 0:nr]
                p.dma("scalar", view, src[r0:r0 + nr, c0:c0 + 128],
                      reads=[rsrc], writes=[r_prep2[i]], tag=tag + "t",
                      transpose=True)
                p.dma("scalar", dst[c0:c0 + 128, r0:r0 + nr], view,
                      reads=[r_prep2[i]], writes=[rdst], tag=tag + "s")

    prep_transpose(wg_bf, sreg["wg_bf"], wgT_scr, sreg["wgT_scr"], H0, D, "tg")
    prep_transpose(wu_bf, sreg["wu_bf"], wuT_scr, sreg["wuT_scr"], H0, D, "tu")
    prep_transpose(wd_bf, sreg["wd_bf"], wdT_scr, sreg["wdT_scr"], D, H0, "td")

    def prep_moe_transpose():
        for e_ in range(NE):
            prep_transpose(w1_bf[e_ * HM:(e_ + 1) * HM, :], sreg["w1_bf"],
                           w1T_scr[e_ * D:(e_ + 1) * D, :], sreg["w1T_scr"],
                           HM, D, "t1")
            prep_transpose(w2_bf[e_ * D:(e_ + 1) * D, :], sreg["w2_bf"],
                           w2T_scr[e_ * HM:(e_ + 1) * HM, :], sreg["w2T_scr"],
                           D, HM, "t2")

    ln_sqt = [sb(f"ln_sqt{i}", [128, TC], F32) for i in range(4)]
    ln_rsqt = [p.region(f"ln_sqt{i}") for i in range(4)]
    r_lnsq = p.region("ln_sq_all")
    ln_st = sb("ln_st", [1, 4 * TC], F32)
    r_lst = p.region("ln_st")

    def emit_ln(x_tiles, rx, out_tiles, rout, g=None, b=None, rgb=(),
                banks=(6, 7)):
        bs, bb = psb[banks[0]], psb[banks[1]]
        rbs, rbb = rps[banks[0]], rps[banks[1]]
        rxl = list(rx)

        def msum(e, tiles=tuple(x_tiles), bs=bs):
            last = None
            for i, ti in enumerate(tiles):
                last = e.matmul(bs[0:1, :], s_ones_col, ti,
                                start=(i == 0), stop=(i == NDC - 1))
            return last
        t_op(msum, rxl + [r_ones_col], [rbs], "ln_sum")
        for d0 in range(0, NDC, 4):
            def sq4(e, d0=d0):
                last = None
                for j in range(4):
                    last = e.activation(ln_sqt[j], x_tiles[d0 + j], AF.Square)
                return last
            s_op(sq4, rxl, [r_lnsq], "ln_sq")
            def msq4(e, d0=d0):
                last = None
                for j in range(4):
                    last = e.matmul(bb[0:1, :], s_ones_col, ln_sqt[j],
                                    start=(d0 + j == 0),
                                    stop=(d0 + j == NDC - 1))
                return last
            t_op(msq4, [r_lnsq, r_ones_col], [rbb], "ln_sumsq")
        mean = ln_st[0:1, 0:TC]
        tmp = ln_st[0:1, TC:2 * TC]
        rstd = ln_st[0:1, 2 * TC:3 * TC]
        mr = ln_st[0:1, 3 * TC:4 * TC]
        tss(mean, bs[0:1, :], 1.0 / D, ALU.mult, [rbs], [r_lst], "ln_mean")
        tss(tmp, bb[0:1, :], 1.0 / D, ALU.mult, [rbb], [r_lst], "ln_ms")
        act(mr, mean, AF.Square, [r_lst], [r_lst], tag="ln_msq")
        tt(ALU.subtract, tmp, tmp, mr, [r_lst], [r_lst], "ln_var")
        act(tmp, tmp, AF.Sqrt, [r_lst, r_eps], [r_lst], bias=s_eps[0:1, 0:1],
            tag="ln_std")
        v_op(lambda e: e.reciprocal(rstd, tmp), [r_lst], [r_lst], "ln_rstd")
        tt(ALU.mult, mr, mean, rstd, [r_lst], [r_lst], "ln_mr")
        t_op(lambda e: e.matmul(bs, s_ones_row, rstd, start=True, stop=True),
             [r_lst, r_ones_row], [rbs], "ln_bc_r")
        t_op(lambda e: e.matmul(bb, s_ones_row, mr, start=True, stop=True),
             [r_lst, r_ones_row], [rbb], "ln_bc_m")
        for d0 in range(0, NDC, 4):
            def a14(e, d0=d0):
                last = None
                for j in range(4):
                    last = e.tensor_tensor(ln_sqt[j], x_tiles[d0 + j], bs,
                                           ALU.mult)
                return last
            v_op(a14, rxl + [rbs], [r_lnsq], "ln_a1")
            if g is None:
                def a24(e, d0=d0):
                    last = None
                    for j in range(4):
                        last = e.tensor_tensor(out_tiles[d0 + j], ln_sqt[j],
                                               bb, ALU.subtract)
                    return last
                v_op(a24, [r_lnsq, rbb], [rout], "ln_a2")
            else:
                def a24(e, d0=d0):
                    last = None
                    for j in range(4):
                        last = e.tensor_tensor(ln_sqt[j], ln_sqt[j], bb,
                                               ALU.subtract)
                    return last
                v_op(a24, [r_lnsq, rbb], [r_lnsq], "ln_a2")
                def af4(e, d0=d0):
                    last = None
                    for j in range(4):
                        last = e.tensor_scalar(out_tiles[d0 + j], ln_sqt[j],
                                               g[:, d0 + j:d0 + j + 1],
                                               b[:, d0 + j:d0 + j + 1],
                                               ALU.mult, ALU.add)
                    return last
                v_op(af4, [r_lnsq] + list(rgb), [rout], "ln_aff")

    def load_xtiles(src, rsrc, tiles, rt, t, q="sync", tag="xld"):
        for dc in range(NDC):
            p.dma(q, tiles[dc], src[dc * 128:(dc + 1) * 128, t * TC:(t + 1) * TC],
                  reads=[rsrc], writes=[rt], tag=tag)

    def load_shifted(src, rsrc, tiles, rt, t, bnd, r_bnd, tag="sld"):
        for dc in range(NDC):
            if t == 0:
                p.dma("sync", tiles[dc][:, 1:TC + 1],
                      src[dc * 128:(dc + 1) * 128, 0:TC],
                      reads=[rsrc], writes=[rt], tag=tag)
                vcopy(tiles[dc][:, 0:1], bnd[:, dc:dc + 1], [r_bnd], [rt],
                      tag + "b")
            else:
                p.dma("sync", tiles[dc],
                      src[dc * 128:(dc + 1) * 128, t * TC - 1:t * TC + TC],
                      reads=[rsrc], writes=[rt], tag=tag)

    def phase_P0():
        ing = [af.alloc(f"ing{i}", 128) for i in range(4)]
        r_ing = [p.region(f"ing{i}") for i in range(4)]
        xrow = [af.alloc(f"P0x{i}", TC) for i in range(2)]
        r_xrow = [p.region(f"P0x{i}") for i in range(2)]
        for t in range(NTC):
            for dc in range(NDC):
                xb_, rxb = xrow[dc % 2], r_xrow[dc % 2]
                for tb in range(TC // 128):
                    i = (dc * 4 + tb) % 4
                    bank = 4 + i % 2
                    p.dma("sync", ing[i],
                          ins["x_in"][t * TC + tb * 128:t * TC + (tb + 1) * 128,
                                      dc * 128:(dc + 1) * 128],
                          writes=[r_ing[i]], tag="P0ld")
                    pe_tr(psb[bank][:, 0:128], ing[i], [r_ing[i]], rps[bank])
                    vcopy(xb_[:, tb * 128:(tb + 1) * 128], psb[bank][:, 0:128],
                          [rps[bank]], [rxb])
                p.dma("sync", xfA[dc * 128:(dc + 1) * 128, t * TC:(t + 1) * TC],
                      xb_, reads=[rxb], writes=[sreg["xfA"]], tag="P0st")

    def phase_ffn0(src, rsrc, dst, rdst, pn):
        HHALF = H0 // 2
        NHH = NH0 // 2
        wsl = wa.alloc(pn + "w", NDC * HHALF)
        r_wsl = p.region(pn + "w")
        sgb = [wa.alloc(f"{pn}s{hc}", TC) for hc in range(NH0)]
        r_sgb = p.region(pn + "s")
        xn = [wa.alloc(f"{pn}xn{dc}", TC) for dc in range(NDC)]
        r_xn = p.region(pn + "xn")
        wdT = [wa.alloc(f"{pn}wd{i}", 2048) for i in range(2)]
        r_wdT = [p.region(f"{pn}wd{i}") for i in range(2)]
        xt = [af.alloc(f"{pn}x{dc}", TC) for dc in range(NDC)]
        r_xt = p.region(pn + "x")
        upev = [af.alloc(f"{pn}g{i}", TC) for i in range(2)]
        r_upev = [p.region(f"{pn}g{i}") for i in range(2)]
        xo = [bfa.alloc(f"{pn}o{i}", TC) for i in range(2)]
        r_xo = [p.region(f"{pn}o{i}") for i in range(2)]

        def load_half(w_scr, rw, half, tag):
            p.dma("scalar",
                  wsl.rearrange("p (c h) -> p c h", h=HHALF),
                  w_scr[:, half * HHALF:(half + 1) * HHALF].rearrange(
                      "(c p) h -> p c h", p=128),
                  reads=[rw], writes=[r_wsl], tag=tag)

        for t in range(NTC):
            load_xtiles(src, rsrc, xt, r_xt, t, tag=pn + "xld")
            emit_ln(xt, [r_xt], xn, r_xn)
            for half in range(2):
                load_half(wgT_scr, sreg["wgT_scr"], half, pn + "wg")
                for hj in range(0, NHH, 2):
                    hc = half * NHH + hj
                    def s1(e, hj=hj):
                        last = None
                        for j in range(2):
                            for dc in range(NDC):
                                last = e.matmul(
                                    psb[j],
                                    wsl[:, dc * HHALF + (hj + j) * 128:
                                        dc * HHALF + (hj + j + 1) * 128],
                                    xn[dc], start=(dc == 0), stop=(dc == NDC - 1))
                        return last
                    t_op(s1, [r_wsl, r_xn], [rps[0], rps[1]], pn + "gmm")
                    def g1(e, hc=hc):
                        last = None
                        for j in range(2):
                            last = e.activation(sgb[hc + j], psb[j], AF.Silu,
                                                scale=s_i127[:, 0:1])
                        return last
                    s_op(g1, [rps[0], rps[1], r_i127], [r_sgb], pn + "silu")
            for half in range(2):
                load_half(wuT_scr, sreg["wuT_scr"], half, pn + "wu")
                for hj in range(0, NHH, 2):
                    hc = half * NHH + hj
                    def s1(e, hj=hj):
                        last = None
                        for j in range(2):
                            for dc in range(NDC):
                                last = e.matmul(
                                    psb[j],
                                    wsl[:, dc * HHALF + (hj + j) * 128:
                                        dc * HHALF + (hj + j + 1) * 128],
                                    xn[dc], start=(dc == 0), stop=(dc == NDC - 1))
                        return last
                    t_op(s1, [r_wsl, r_xn], [rps[0], rps[1]], pn + "umm")
                    def u1(e):
                        e.activation(upev[0], psb[0], AF.Copy, scale=s_i127[:, 0:1])
                        return e.activation(upev[1], psb[1], AF.Copy,
                                            scale=s_i127[:, 0:1])
                    s_op(u1, [rps[0], rps[1], r_i127],
                         [r_upev[0], r_upev[1]], pn + "ue")
                    def m1(e, hc=hc):
                        e.tensor_tensor(sgb[hc], upev[0], sgb[hc], ALU.mult)
                        return e.tensor_tensor(sgb[hc + 1], upev[1], sgb[hc + 1],
                                               ALU.mult)
                    v_op(m1, [r_upev[0], r_upev[1], r_sgb], [r_sgb], pn + "hm")
            for hc in range(0, NH0, 2):
                sl = (hc // 2) % 2
                p.dma("scalar",
                      wdT[sl].rearrange("p (c d) -> p c d", d=D),
                      wdT_scr[hc * 128:(hc + 2) * 128, :].rearrange(
                          "(c p) d -> p c d", p=128),
                      reads=[sreg["wdT_scr"]], writes=[r_wdT[sl]], tag=pn + "wd")
                def s2(e, hc=hc, sl=sl):
                    last = None
                    for j in range(2):
                        for dc in range(NDC):
                            last = e.matmul(
                                psb[dc],
                                wdT[sl][:, j * D + dc * 128:j * D + (dc + 1) * 128],
                                sgb[hc + j], start=(hc + j == 0),
                                stop=(hc + j == NH0 - 1))
                    return last
                t_op(s2, [r_wdT[sl], r_sgb], [rps[d] for d in range(NDC)],
                     pn + "dmm")
            for dc in range(NDC):
                i = dc % 2
                v_op(lambda e, dc=dc, i=i: e.scalar_tensor_tensor(
                    xo[i], psb[dc], 0.5 / 127.0, xt[dc], ALU.mult, ALU.add),
                    [rps[dc], r_xt], [r_xo[i]], pn + "res")
                p.dma("sync", dst[dc * 128:(dc + 1) * 128, t * TC:(t + 1) * TC],
                      xo[i], reads=[r_xo[i]], writes=[rdst], tag=pn + "st")

    def phase_P2a1(src, rsrc):
        sqw = [wa.alloc(f"A1q{dc}", D) for dc in range(NDC)]
        r_sqw = p.region("A1q")
        for dc in range(NDC):
            p.dma("gpsimd", sqw[dc],
                  ins["L0_attn_sq_w"][dc * 128:(dc + 1) * 128, :],
                  writes=[r_sqw], tag="sqw")
        h1b = [wa.alloc(f"A1hb{dc}", TC) for dc in range(NDC)]
        r_h1b = p.region("A1hb")
        ho = [wa.alloc(f"A1ho{i}", TC) for i in range(2)]
        r_ho = [p.region(f"A1ho{i}") for i in range(2)]
        xt = [af.alloc(f"A1x{dc}", TC) for dc in range(NDC)]
        r_xt = p.region("A1x")
        xa = [af.alloc(f"A1a{dc}", TC) for dc in range(NDC)]
        r_xa = p.region("A1a")
        h1 = [bfa.alloc(f"A1h{dc}", TC) for dc in range(NDC)]
        r_h1 = p.region("A1h")
        atmp = [af.alloc(f"A1t{i}", TC) for i in range(2)]
        r_atmp = p.region("A1t")

        for t in range(NTC):
            load_xtiles(src, rsrc, xt, r_xt, t, tag="A1ld")
            emit_ln(xt, [r_xt], xa, r_xa)
            emit_ln(xa, [r_xa], h1, r_h1, g=s_ln_g, b=s_ln_b,
                    rgb=[r_ln_g, r_ln_b])
            for dc in range(NDC):
                vcopy(h1b[dc], h1[dc], [r_h1], [r_h1b])
            for dc2 in range(NDC):
                bank = dc2 % 2
                mm_acc(psb[bank],
                       [(sqw[dc][:, dc2 * 128:(dc2 + 1) * 128], h1b[dc])
                        for dc in range(NDC)],
                       [r_sqw, r_h1b], rps[bank], "sqmm")
                i = dc2 % 2
                v_op(lambda e, dc2=dc2, bank=bank: e.scalar_tensor_tensor(
                    atmp[bank], psb[bank], s_sq_b[:, dc2:dc2 + 1], h1[dc2],
                    ALU.add, ALU.add),
                    [rps[bank], r_sq_b, r_h1], [r_atmp], "sqadd")
                vcopy(ho[i], atmp[bank], [r_atmp], [r_ho[i]])
                p.dma("sync", h_scr[dc2 * 128:(dc2 + 1) * 128,
                                      t * TC:(t + 1) * TC],
                      ho[i], reads=[r_ho[i]], writes=[sreg["h_scr"]],
                      tag="A1st")

    def phase_P2a2():
        ipw = [wa.alloc(f"A2w{dc}", DI) for dc in range(NDC)]
        r_ipw = p.region("A2w")
        hb = [wa.alloc(f"A2h{dc}", TC) for dc in range(NDC)]
        r_hb = p.region("A2h")
        uo = [af.alloc(f"A2u{i}", TC) for i in range(4)]
        r_uo = [p.region(f"A2u{i}") for i in range(4)]
        for half in range(2):
            for dc in range(NDC):
                p.dma("gpsimd", ipw[dc],
                      ins["L0_attn_in_proj_w"][dc * 128:(dc + 1) * 128,
                                               half * DI:(half + 1) * DI],
                      writes=[r_ipw], tag="ipw")
            for t in range(NTC):
                load_xtiles(h_scr, sreg["h_scr"], hb, r_hb, t, tag="A2hld")
                for oc in range(NIC):
                    i = oc % 4
                    bank = oc % 4
                    mm_acc(psb[bank],
                           [(ipw[dc][:, oc * 128:(oc + 1) * 128], hb[dc])
                            for dc in range(NDC)],
                           [r_ipw, r_hb], rps[bank], "ipmm")
                    vcopy(uo[i], psb[bank], [rps[bank]], [r_uo[i]])
                    if half == 0:
                        dap = u_pre[oc * 128:(oc + 1) * 128,
                                    t * TC:(t + 1) * TC]
                        rd = sreg["u_pre"]
                        if t == NTC - 1:
                            vcopy(s_tail[:, oc * 3:(oc + 1) * 3],
                                  uo[i][:, TC - 3:TC], [r_uo[i]], [r_tail],
                                  "A2tl")
                    else:
                        dap = z_scr[oc * 128:(oc + 1) * 128, t * TC:(t + 1) * TC]
                        rd = sreg["z_scr"]
                    p.dma("sync", dap, uo[i], reads=[r_uo[i]], writes=[rd],
                          tag="A2st")

    def phase_C1():
        p.dma("sync", cc_in["c1"], s_tail, reads=[r_tail],
              writes=[sreg["cin_c1"]], tag="c1a")
        p.collective("AllGather", ALU.bypass, [cc_in["c1"]], [cc_out["c1"]],
                     reads=[sreg["cin_c1"]], writes=[sreg["cout_c1"]])
        p.dma("sync", s_cc[:, 0:48], cc_out["c1"][0:128, :],
              reads=[sreg["cout_c1"]], writes=[r_cc], tag="c1l")
        tss(s_halo[:, 0:48], s_cc[:, 0:48], s_parity[:, 0:1], ALU.mult,
            [r_cc, r_parity], [r_halo], "c1m")

    def phase_P2b_real():
        delta = [af.alloc(f"Bd{ic}", TC) for ic in range(NIC)]
        r_dl = p.region("Bd")
        Pt = [af.alloc(f"BP{i}", TC) for i in range(2)]
        r_Pt = [p.region(f"BP{i}") for i in range(2)]
        spl = [af.alloc(f"Bs{i}", TC) for i in range(2)]
        r_spl = [p.region(f"Bs{i}") for i in range(2)]
        du = [bfa.alloc(f"Bu{ic}", TC) for ic in range(NIC)]
        r_du = p.region("Bu")
        Bb = bfa.alloc("Bbb", TC)
        r_Bb = p.region("Bbb")
        Cb = bfa.alloc("Bcb", TC)
        r_Cb = p.region("Bcb")
        yac = [waf.alloc(f"By{ic}", TC) for ic in range(NIC)]
        r_y = p.region("By")
        upt = [waf.alloc(f"Bt{i}", TC + 3) for i in range(2)]
        r_upt = [p.region(f"Bt{i}") for i in range(2)]
        tA = [waf.alloc(f"BA{i}", TC) for i in range(6)]
        r_tA = [p.region("BAa"), p.region("BAb")]
        tB = [waf.alloc(f"BB{i}", TC) for i in range(6)]
        r_tB = [p.region("BBa"), p.region("BBb")]
        xprj = [s_xprj[:, ic * 96:(ic + 1) * 96] for ic in range(NIC)]
        r_xprj = p.region("Bxp")
        dtw = s_dtw
        r_dtw = p.region("Bdtw")
        dtv_t = waf.alloc("Bdtv", TC)
        dtv = dtv_t[0:64, :]
        r_dtv = p.region("Bdtv")
        bc16_t = waf.alloc("Bbc16", TC)
        bc16 = bc16_t[0:16, :]
        r_bc16 = p.region("Bbc16")

        for ic in range(NIC):
            p.dma("sync", xprj[ic],
                  ins["L0_attn_x_proj_w"][ic * 128:(ic + 1) * 128, :],
                  writes=[r_xprj], tag="xprj")
        p.dma("sync", dtw, ins["L0_attn_dt_w"], writes=[r_dtw], tag="dtw")

        for t in range(NTC):
            for ic in range(NIC):
                iu = ic % 2
                ub, rub = upt[iu], r_upt[iu]
                if t == 0:
                    p.dma("sync", ub[:, 3:TC + 3],
                          u_pre[ic * 128:(ic + 1) * 128, 0:TC],
                          reads=[sreg["u_pre"]], writes=[rub], tag="Buld")
                    vcopy(ub[:, 0:3], s_halo[:, ic * 3:(ic + 1) * 3],
                          [r_halo], [rub], "Bhal")
                else:
                    p.dma("sync", ub,
                          u_pre[ic * 128:(ic + 1) * 128,
                                t * TC - 3:t * TC + TC],
                          reads=[sreg["u_pre"]], writes=[rub], tag="Buld")
                cv = tA[iu * 3]
                rcv = r_tA[iu]
                tss(cv, ub[:, 0:TC], s_convw[:, ic * 4:ic * 4 + 1], ALU.mult,
                    [rub, r_convw], [rcv], "cv0")
                for j in range(1, 4):
                    v_op(lambda e, ub=ub, cv=cv, ic=ic, j=j: e.scalar_tensor_tensor(
                        cv, ub[:, j:j + TC], s_convw[:, ic * 4 + j:ic * 4 + j + 1],
                        cv, ALU.mult, ALU.add),
                        [rub, r_convw, rcv], [rcv], "cvj")
                s_op(lambda e, ic=ic, cv=cv: e.activation(
                    du[ic], cv, AF.Silu, bias=s_conv_b[:, ic:ic + 1]),
                    [rcv, r_conv_b], [r_du], "usilu")
                tss(yac[ic], du[ic], s_Dsk[:, ic:ic + 1], ALU.mult,
                    [r_du, r_Dsk], [r_y], "yD")
            mm_acc(psb[0][0:64, :], [(xprj[ic][:, 0:64], du[ic])
                                     for ic in range(NIC)],
                   [r_xprj, r_du], rps[0], "xdt")
            mm_acc(psb[1][0:16, :], [(xprj[ic][:, 64:80], du[ic])
                                     for ic in range(NIC)],
                   [r_xprj, r_du], rps[1], "xB")
            mm_acc(psb[2][0:16, :], [(xprj[ic][:, 80:96], du[ic])
                                     for ic in range(NIC)],
                   [r_xprj, r_du], rps[2], "xC")
            vcopy(dtv, psb[0][0:64, :], [rps[0]], [r_dtv])
            vcopy(bc16, psb[1][0:16, :], [rps[1]], [r_bc16])
            p.dma("sync", B_scr[:, t * TC:(t + 1) * TC], bc16,
                  reads=[r_bc16], writes=[sreg["B_scr"]], tag="Bst")
            scopy(bc16, psb[2][0:16, :], [rps[2]], [r_bc16], tag="Cev")
            p.dma("sync", C_scr[:, t * TC:(t + 1) * TC], bc16,
                  reads=[r_bc16], writes=[sreg["C_scr"]], tag="Cst")
            for ic in range(NIC):
                bank = 3 + ic % 2
                t_op(lambda e, ic=ic, bank=bank: e.matmul(
                    psb[bank], dtw[:, ic * 128:(ic + 1) * 128], dtv,
                    start=True, stop=True),
                    [r_dtw, r_dtv], [rps[bank]], "dtmm")
                sa_, sb_ = spl[0], spl[1]
                s_op(lambda e, ic=ic, bank=bank, sa_=sa_: e.activation(
                    sa_, psb[bank], AF.Abs, bias=s_dt_b[:, ic:ic + 1]),
                    [rps[bank], r_dt_b], [r_spl[0]], "sp_abs")
                act(sa_, sa_, AF.Exp, [r_spl[0]], [r_spl[0]], scale=-1.0,
                    tag="sp_exp")
                act(sa_, sa_, AF.Ln, [r_spl[0]], [r_spl[0]], bias=1.0,
                    tag="sp_ln")
                s_op(lambda e, ic=ic, bank=bank, sb_=sb_: e.activation(
                    sb_, psb[bank], AF.Relu, bias=s_dt_b[:, ic:ic + 1]),
                    [rps[bank], r_dt_b], [r_spl[1]], "sp_relu")
                tt(ALU.add, delta[ic], sa_, sb_, [r_spl[0], r_spl[1]], [r_dl],
                   "sp_add")
                pt = Pt[ic % 2]
                rpt = r_Pt[ic % 2]
                scan(pt, s_onesT, delta[ic], s_Pcar[:, ic:ic + 1],
                     [r_onesT, r_dl, r_Pcar], [rpt], "Pscan")
                vcopy(s_Pcar[:, ic:ic + 1], pt[:, TC - 1:TC], [rpt], [r_Pcar],
                      "Pcar")
                p.dma("sync", P_scr[ic * 128:(ic + 1) * 128,
                                      t * TC:(t + 1) * TC],
                      pt, reads=[rpt], writes=[sreg["P_scr"]], tag="Pst")
                tt(ALU.mult, du[ic], delta[ic], du[ic], [r_dl, r_du], [r_du],
                   "du")
            for n in range(NST):
                p.dma("sync", Bb,
                      B_scr[n:n + 1, t * TC:(t + 1) * TC].partition_broadcast(128),
                      reads=[sreg["B_scr"]], writes=[r_Bb], tag="Bbc")
                p.dma("sync", Cb,
                      C_scr[n:n + 1, t * TC:(t + 1) * TC].partition_broadcast(128),
                      reads=[sreg["C_scr"]], writes=[r_Cb], tag="Cbc")
                groups = [(0, 3), (3, 3), (6, 3), (9, 3), (12, 3), (15, 1)]
                for gi, (ic0, gn) in enumerate(groups):
                    st_ = gi % 2
                    A = tA[st_ * 3:st_ * 3 + gn]
                    B = tB[st_ * 3:st_ * 3 + gn]
                    rA, rB = r_tA[st_], r_tB[st_]
                    def dAg(e, ic0=ic0, gn=gn, n=n, A=tuple(A)):
                        last = None
                        for j in range(gn):
                            col = (ic0 + j) * NST + n
                            last = e.activation(A[j], delta[ic0 + j], AF.Exp,
                                                scale=s_A[:, col:col + 1])
                        return last
                    s_op(dAg, [r_dl, r_A], [rA], "dA")
                    def dBg(e, ic0=ic0, gn=gn, B=tuple(B)):
                        last = None
                        for j in range(gn):
                            last = e.tensor_tensor(B[j], du[ic0 + j], Bb,
                                                   ALU.mult)
                        return last
                    v_op(dBg, [r_du, r_Bb], [rB], "dBu")
                    def scg(e, ic0=ic0, gn=gn, n=n, A=tuple(A), B=tuple(B)):
                        last = None
                        for j in range(gn):
                            col = (ic0 + j) * NST + n
                            last = e.tensor_tensor_scan(
                                A[j], A[j], B[j], s_hcar[:, col:col + 1],
                                ALU.mult, ALU.add)
                        for j in range(gn):
                            col = (ic0 + j) * NST + n
                            last = e.tensor_copy(s_hcar[:, col:col + 1],
                                                 A[j][:, TC - 1:TC])
                        return last
                    v_op(scg, [rA, rB, r_hcar], [rA, r_hcar], "hscan")
                    def hCg(e, gn=gn, A=tuple(A), B=tuple(B)):
                        last = None
                        for j in range(gn):
                            last = e.tensor_tensor(B[j], A[j], Cb, ALU.mult)
                        return last
                    v_op(hCg, [rA, r_Cb], [rB], "hC")
                    def yag(e, ic0=ic0, gn=gn, B=tuple(B)):
                        last = None
                        for j in range(gn):
                            last = e.tensor_tensor(yac[ic0 + j], yac[ic0 + j],
                                                   B[j], ALU.add)
                        return last
                    g_op(yag, [rB, r_y], [r_y], "yadd")
            for ic in range(NIC):
                p.dma("sync", y_scr[ic * 128:(ic + 1) * 128,
                                      t * TC:(t + 1) * TC],
                      yac[ic], reads=[r_y], writes=[sreg["y_scr"]], tag="yst")

    def phase_C2():
        p.dma("gpsimd", cc_in["c2"], s_hcar, reads=[r_hcar],
              writes=[sreg["cin_c2"]], tag="c2a")
        p.collective("AllGather", ALU.bypass, [cc_in["c2"]], [cc_out["c2"]],
                     reads=[sreg["cin_c2"]], writes=[sreg["cout_c2"]])
        p.dma("sync", s_h0eff, cc_out["c2"][0:128, :],
              reads=[sreg["cout_c2"]], writes=[r_h0eff], tag="c2l")
        tss(s_h0eff, s_h0eff, s_parity[:, 0:1], ALU.mult,
            [r_h0eff, r_parity], [r_h0eff], "c2m")

    s_xprj = sb("s_xprj", [128, NIC * 96], F32)
    s_dtw = sb("s_dtw", [64, DI], F32)
    s_tail = sb("s_tail", [128, 48], F32)
    r_tail = p.region("tail")
    s_halo = sb("s_halo", [128, 48], F32)
    r_halo = p.region("halo")
    s_bcc = sb("s_bcc", [128, 8], F32)
    r_bcc = p.region("bcc")
    s_bnd1 = sb("s_bnd1", [128, 8], F32)
    r_bnd1 = p.region("bnd1")
    s_bnd2 = sb("s_bnd2", [128, 8], F32)
    r_bnd2 = p.region("bnd2")

    def phase_P2c(src, rsrc, dst, rdst):
        Ptl = [af.alloc(f"Cp{ic}", TC) for ic in range(NIC)]
        r_P = p.region("Cp")
        tE = [af.alloc(f"Ce{i}", TC) for i in range(4)]
        r_tE = p.region("Ce")
        ytl = [bfa.alloc(f"Cy{ic}", TC) for ic in range(NIC)]
        r_y = p.region("Cy")
        zt = [bfa.alloc(f"Cz{i}", TC) for i in range(2)]
        r_zt = [p.region(f"Cz{i}") for i in range(2)]
        opw = [wa.alloc(f"Do{i}", NIC * 128) for i in range(2)]
        r_opw = [p.region(f"Do{i}") for i in range(2)]
        srw = [wa.alloc(f"Dw{dc}", D) for dc in range(NDC)]
        r_srw = p.region("Dw")
        ygb = [wa.alloc(f"Dg{ic}", TC) for ic in range(NIC)]
        r_ygb = p.region("Dg")
        tM = [waf.alloc(f"Cm{i}", TC) for i in range(8)]
        r_tM = [p.region("Cma"), p.region("Cmb")]
        Cb = waf.alloc("Ccb", TC)
        r_Cb = p.region("Ccb")
        gtmp = [waf.alloc(f"Dt{i}", TC) for i in range(2)]
        r_gtmp = p.region("Dt")
        ybf = Ptl[0:NDC]
        r_ybf = r_P
        xt2 = ytl[0:NDC]
        r_xt2 = r_y
        xa2 = ytl[NDC:2 * NDC]
        r_xa2 = r_y
        xo = [tE[2], tE[3]]
        r_xo = [r_tE, r_tE]

        for dc in range(NDC):
            p.dma("gpsimd", srw[dc],
                  ins["L0_attn_srw_w"][dc * 128:(dc + 1) * 128, :],
                  writes=[r_srw], tag="srw")

        for t in range(NTC):
            for ic in range(NIC):
                p.dma("sync", Ptl[ic],
                      P_scr[ic * 128:(ic + 1) * 128, t * TC:(t + 1) * TC],
                      reads=[sreg["P_scr"]], writes=[r_P], tag="CPld")
                p.dma("sync", ytl[ic],
                      y_scr[ic * 128:(ic + 1) * 128, t * TC:(t + 1) * TC],
                      reads=[sreg["y_scr"]], writes=[r_y], tag="Cyld")
            for n in range(NST):
                p.dma("sync", Cb,
                      C_scr[n:n + 1, t * TC:(t + 1) * TC].partition_broadcast(128),
                      reads=[sreg["C_scr"]], writes=[r_Cb], tag="CCbc")
                for gi, ic0 in enumerate(range(0, NIC, 4)):
                    st_ = gi % 2
                    M = tM[st_ * 4:st_ * 4 + 4]
                    rM = r_tM[st_]
                    def ce4(e, ic0=ic0, n=n):
                        last = None
                        for j in range(4):
                            col = (ic0 + j) * NST + n
                            last = e.activation(tE[j], Ptl[ic0 + j], AF.Exp,
                                                scale=s_A[:, col:col + 1])
                        return last
                    s_op(ce4, [r_P, r_A], [r_tE], "Cexp")
                    def cm4(e, ic0=ic0, n=n, M=tuple(M)):
                        last = None
                        for j in range(4):
                            col = (ic0 + j) * NST + n
                            last = e.scalar_tensor_tensor(
                                M[j], tE[j], s_h0eff[:, col:col + 1], Cb,
                                ALU.mult, ALU.mult)
                        return last
                    v_op(cm4, [r_tE, r_h0eff, r_Cb], [rM], "Cm")
                    if gi % 2 == 0:
                        def ca4(e, ic0=ic0, M=tuple(M)):
                            last = None
                            for j in range(4):
                                last = e.tensor_tensor(ytl[ic0 + j],
                                                       ytl[ic0 + j], M[j],
                                                       ALU.add)
                            return last
                        g_op(ca4, [rM, r_y], [r_y], "Cadd")
                    else:
                        def ca4(e, ic0=ic0, M=tuple(M)):
                            last = None
                            for j in range(4):
                                last = e.tensor_tensor(ytl[ic0 + j],
                                                       ytl[ic0 + j], M[j],
                                                       ALU.add)
                            return last
                        v_op(ca4, [rM, r_y], [r_y], "Cadd")
            for ic in range(NIC):
                i = ic % 2
                p.dma("sync", zt[i],
                      z_scr[ic * 128:(ic + 1) * 128, t * TC:(t + 1) * TC],
                      reads=[sreg["z_scr"]], writes=[r_zt[i]], tag="Czld")
                act(tE[i], zt[i], AF.Silu, [r_zt[i]], [r_tE], tag="Csz")
                tt(ALU.mult, ygb[ic], ytl[ic], tE[i], [r_y, r_tE], [r_ygb],
                   "Cyg")
            for dc in range(NDC):
                sl = dc % 2
                p.dma("scalar",
                      opw[sl].rearrange("p (c q) -> p c q", q=128),
                      opw_bf[:, dc * 128:(dc + 1) * 128]
                      .rearrange("(c p) q -> p c q", p=128),
                      reads=[sreg["opw_bf"]], writes=[r_opw[sl]], tag="Dow")
                bank = dc % 4
                mm_acc(psb[bank],
                       [(opw[sl][:, ic * 128:(ic + 1) * 128], ygb[ic])
                        for ic in range(NIC)],
                       [r_opw[sl], r_ygb], rps[bank], "Dop")
                scopy(ybf[dc], psb[bank], [rps[bank]], [r_ybf], tag="Dybe")
            def ybbc(e):
                last = None
                for dc in range(NDC):
                    last = e.tensor_copy(ygb[dc], ybf[dc])
                return last
            v_op(ybbc, [r_ybf, r_ygb], [r_ygb], "Dybb")
            load_xtiles(src, rsrc, xt2, r_xt2, t, tag="Dxld")
            emit_ln(xt2, [r_xt2], xa2, r_xa2)
            for dc2 in range(NDC):
                bank = 4 + dc2 % 2
                mm_acc(psb[bank],
                       [(srw[dc][:, dc2 * 128:(dc2 + 1) * 128], ygb[dc])
                        for dc in range(NDC)],
                       [r_srw, r_ygb], rps[bank], "Dsrw")
                tsc2(gtmp[dc2 % 2], psb[bank], s_srw_b[:, dc2:dc2 + 1], None,
                     ALU.add, ALU.bypass, [rps[bank], r_srw_b],
                     [r_gtmp], "Dgb")
                tt(ALU.mult, gtmp[dc2 % 2], ybf[dc2], gtmp[dc2 % 2],
                   [r_ybf, r_gtmp], [r_gtmp], "Dy2")
                xoo, rxo = xo[dc2 % 2], r_xo[dc2 % 2]
                tt(ALU.add, xoo, xt2[dc2], xa2[dc2], [r_xt2, r_xa2], [rxo],
                   "Dxa")
                v_op(lambda e, dc2=dc2, xoo=xoo: e.scalar_tensor_tensor(
                    xoo, gtmp[dc2 % 2], s_rs[:, 0:1], xoo, ALU.mult, ALU.add),
                    [r_gtmp, r_rs, rxo], [rxo], "Dres")
                p.dma("sync", dst[dc2 * 128:(dc2 + 1) * 128,
                                  t * TC:(t + 1) * TC],
                      xoo, reads=[rxo], writes=[rdst], tag="Dst")

    def phase_moepre(src, rsrc, pn):
        xn = [wa.alloc(f"{pn}n{dc}", TC) for dc in range(NDC)]
        r_xn = p.region(pn + "n")
        xt = [af.alloc(f"{pn}x{dc}", TC) for dc in range(NDC)]
        r_xt = p.region(pn + "x")
        rr = af.alloc(pn + "rr", TC)
        r_rr = p.region(pn + "rr")
        lt = af.alloc(pn + "lt", 8)
        r_lt = p.region(pn + "lt")
        st8 = af.alloc(pn + "s8", 8)
        r_st8 = p.region(pn + "s8")
        w8 = af.alloc(pn + "w8", 8)
        r_w8 = p.region(pn + "w8")
        stat = af.alloc(pn + "st", 4)
        r_stat = p.region(pn + "st")

        for t in range(NTC):
            load_xtiles(src, rsrc, xt, r_xt, t, tag=pn + "xld")
            emit_ln(xt, [r_xt], xn, r_xn)
            for dc in range(NDC):
                p.dma("sync", xnm_scr[dc * 128:(dc + 1) * 128,
                                        t * TC:(t + 1) * TC],
                      xn[dc], reads=[r_xn], writes=[sreg["xnm_scr"]],
                      tag=pn + "nst")
            def mr_(e):
                last = None
                for dc in range(NDC):
                    last = e.matmul(psb[2][0:8, :],
                                    s_routT[:, dc * NE:(dc + 1) * NE], xn[dc],
                                    start=(dc == 0), stop=(dc == NDC - 1))
                return last
            t_op(mr_, [r_routT, r_xn], [rps[2]], pn + "rt")
            vcopy(rr[0:8, :], psb[2][0:8, :], [rps[2]], [r_rr])
            for tb in range(TC // 128):
                pe_tr(psb[3][0:128, 0:8], rr[0:8, tb * 128:(tb + 1) * 128],
                      [r_rr], rps[3])
                vcopy(lt, psb[3][0:128, 0:8], [rps[3]], [r_lt])
                m1n = stat[:, 0:1]
                ssum = stat[:, 1:2]
                m2 = stat[:, 2:3]
                v_op(lambda e, m1n=m1n: e.reduce_max(m1n, lt, AX.X, negate=True),
                     [r_lt], [r_stat], "m1")
                s_op(lambda e, m1n=m1n: e.activation(st8, lt, AF.Exp,
                                                     bias=m1n[:, 0:1]),
                     [r_lt, r_stat], [r_st8], "smexp")
                v_op(lambda e, ssum=ssum: e.reduce_sum(ssum, st8, AX.X),
                     [r_st8], [r_stat], "ssum")
                v_op(lambda e, ssum=ssum: e.reciprocal(ssum, ssum),
                     [r_stat], [r_stat], "rsum")
                tss(st8, st8, ssum[:, 0:1], ALU.mult, [r_st8, r_stat], [r_st8],
                    "probs")
                v_op(lambda e, m1n=m1n: e.reduce_max(m1n, st8, AX.X, negate=True),
                     [r_st8], [r_stat], "m1p")
                tss(m1n, m1n, -1.0, ALU.mult, [r_stat], [r_stat], "m1pp")
                tss(w8, st8, m1n[:, 0:1], ALU.is_ge, [r_st8, r_stat], [r_w8],
                    "mask1")
                v_op(lambda e: e.scalar_tensor_tensor(
                    w8, w8, -1e9, st8, ALU.mult, ALU.add),
                    [r_w8, r_st8], [r_w8], "s4")
                v_op(lambda e, m2=m2: e.reduce_max(m2, w8, AX.X),
                     [r_w8], [r_stat], "m2")
                tss(w8, st8, m2[:, 0:1], ALU.is_ge, [r_st8, r_stat], [r_w8],
                    "mask2")
                tt(ALU.mult, w8, w8, st8, [r_w8, r_st8], [r_w8], "wsel")
                tss(w8, w8, 0.5, ALU.mult, [r_w8], [r_w8], "whalf")
                pe_tr(psb[3][0:8, 0:128], w8, [r_w8], rps[3])
                vcopy(rr[0:8, tb * 128:(tb + 1) * 128], psb[3][0:8, 0:128],
                      [rps[3]], [r_rr])
            p.dma("sync", gate_scr[:, t * TC:(t + 1) * TC], rr[0:8, :],
                  reads=[r_rr], writes=[sreg["gate_scr"]], tag=pn + "gst")

    def phase_moe(src, rsrc, dst, rdst, pn):
        HMH = HM // 2
        NHH = NHM // 2
        w1T = wa.alloc(pn + "1", NDC * HMH)
        r_w1T = p.region(pn + "1")
        hp = [wa.alloc(f"{pn}h{hh}", TC) for hh in range(NHH)]
        r_hp = p.region(pn + "h")
        xn = [wa.alloc(f"{pn}n{dc}", TC) for dc in range(NDC)]
        r_xn = p.region(pn + "n")
        w2T = [wa.alloc(f"{pn}2{i}", 2048) for i in range(2)]
        r_w2T = [p.region(f"{pn}2{i}") for i in range(2)]
        osb = [af.alloc(f"{pn}o{dc}", TC) for dc in range(NDC)]
        r_osb = p.region(pn + "o")
        gl = [af.alloc(f"{pn}g{i}", TC) for i in range(2)]
        r_gl = [p.region(f"{pn}g{i}") for i in range(2)]
        wb = af.alloc(pn + "wb", TC)
        r_wb = p.region(pn + "wb")
        xt = [bfa.alloc(f"{pn}x{i}", TC) for i in range(2)]
        r_xtl = [p.region(f"{pn}x{i}") for i in range(2)]
        xo = [bfa.alloc(f"{pn}xo{i}", TC) for i in range(2)]
        r_xo = [p.region(f"{pn}xo{i}") for i in range(2)]

        for sup in range(NTC):
            t0 = sup * TC
            def oz(e):
                last = None
                for dc in range(NDC):
                    last = e.memset(osb[dc], 0.0)
                return last
            v_op(oz, [r_osb], [r_osb], pn + "oz")
            for dc in range(NDC):
                p.dma("sync", xn[dc],
                      xnm_scr[dc * 128:(dc + 1) * 128, t0:t0 + TC],
                      reads=[sreg["xnm_scr"]], writes=[r_xn], tag=pn + "nld")
            for e_ in range(NE):
                p.dma("sync", s_wrow, gate_scr[e_:e_ + 1, t0:t0 + TC],
                      reads=[sreg["gate_scr"]], writes=[r_wrow], tag=pn + "wr")
                t_op(lambda e: e.matmul(psb[6], s_ones_row, s_wrow,
                                        start=True, stop=True),
                     [r_wrow, r_ones_row], [rps[6]], pn + "wbc")
                vcopy(wb, psb[6], [rps[6]], [r_wb])
                for half in range(2):
                    for q in range(2):
                        p.dma("scalar",
                              w1T[:, q * NDC * 1024:(q + 1) * NDC * 1024]
                              .rearrange("p (c h) -> p c h", h=1024),
                              w1T_scr[e_ * D:(e_ + 1) * D,
                                      half * HMH + q * 1024:
                                      half * HMH + (q + 1) * 1024]
                              .rearrange("(c p) h -> p c h", p=128),
                              reads=[sreg["w1T_scr"]], writes=[r_w1T],
                              tag=pn + "w1")
                    for hj in range(0, NHH, 2):
                        q, hq = divmod(hj, 8)
                        def s1(e, q=q, hq=hq):
                            last = None
                            for j in range(2):
                                for dc in range(NDC):
                                    base = q * NDC * 1024 + dc * 1024
                                    last = e.matmul(
                                        psb[j],
                                        w1T[:, base + (hq + j) * 128:
                                            base + (hq + j + 1) * 128],
                                        xn[dc], start=(dc == 0),
                                        stop=(dc == NDC - 1))
                            return last
                        t_op(s1, [r_w1T, r_xn], [rps[0], rps[1]], pn + "s1")
                        def g1(e, hj=hj):
                            last = None
                            for j in range(2):
                                last = e.activation(gl[j], psb[j], AF.Gelu)
                            return last
                        s_op(g1, [rps[0], rps[1]], [r_gl[0], r_gl[1]],
                             pn + "gel")
                        def m1(e, hj=hj):
                            e.tensor_tensor(hp[hj], gl[0], wb, ALU.mult)
                            return e.tensor_tensor(hp[hj + 1], gl[1], wb,
                                                   ALU.mult)
                        v_op(m1, [r_gl[0], r_gl[1], r_wb], [r_hp], pn + "gw")
                    for hj in range(0, NHH, 2):
                        hc = half * NHH + hj
                        sl = (hj // 2) % 2
                        p.dma("scalar",
                              w2T[sl].rearrange("p (c d) -> p c d", d=D),
                              w2T_scr[e_ * HM + hc * 128:
                                      e_ * HM + (hc + 2) * 128, :]
                              .rearrange("(c p) d -> p c d", p=128),
                              reads=[sreg["w2T_scr"]], writes=[r_w2T[sl]],
                              tag=pn + "w2")
                        def s2(e, hj=hj, sl=sl):
                            last = None
                            for j in range(2):
                                for dc in range(NDC):
                                    last = e.matmul(
                                        psb[dc],
                                        w2T[sl][:, j * D + dc * 128:
                                                j * D + (dc + 1) * 128],
                                        hp[hj + j], start=(hj + j == 0),
                                        stop=(hj + j == NHH - 1))
                            return last
                        t_op(s2, [r_w2T[sl], r_hp],
                             [rps[d] for d in range(NDC)], pn + "s2")
                    for d0 in range(0, NDC, 4):
                        def oa4(e, d0=d0):
                            last = None
                            for j in range(4):
                                last = e.scalar_tensor_tensor(
                                    osb[d0 + j], psb[d0 + j], 1.0,
                                    osb[d0 + j], ALU.mult, ALU.add)
                            return last
                        v_op(oa4, [rps[d] for d in range(d0, d0 + 4)] + [r_osb],
                             [r_osb], pn + "oa")
            for dc in range(NDC):
                i = dc % 2
                p.dma("sync", xt[i], src[dc * 128:(dc + 1) * 128, t0:t0 + TC],
                      reads=[rsrc], writes=[r_xtl[i]], tag=pn + "xld")
                tt(ALU.add, xo[i], xt[i], osb[dc], [r_xtl[i], r_osb],
                   [r_xo[i]], pn + "res")
                p.dma("sync", dst[dc * 128:(dc + 1) * 128, t0:t0 + TC],
                      xo[i], reads=[r_xo[i]], writes=[rdst], tag=pn + "st")

    def phase_r1(src, rsrc):
        xt = [af.alloc(f"R1x{dc}", TC) for dc in range(NDC)]
        r_xt = p.region("R1x")
        xa = [af.alloc(f"R1a{dc}", TC) for dc in range(NDC)]
        r_xa = p.region("R1a")
        xn1 = [bfa.alloc(f"R1n{dc}", TC) for dc in range(NDC)]
        r_xn1 = p.region("R1n")
        for t in range(NTC):
            load_xtiles(src, rsrc, xt, r_xt, t, tag="R1ld")
            emit_ln(xt, [r_xt], xa, r_xa)
            emit_ln(xa, [r_xa], xn1, r_xn1, g=s_ln1_w, b=s_ln1_b,
                    rgb=[r_ln1_w, r_ln1_b])
            for dc in range(NDC):
                if t == NTC - 1:
                    vcopy(s_bcc[:, dc:dc + 1], xn1[dc][:, TC - 1:TC],
                          [r_xn1], [r_bcc], "R1bc")
                p.dma("sync", xn1_scr[dc * 128:(dc + 1) * 128,
                                        t * TC:(t + 1) * TC],
                      xn1[dc], reads=[r_xn1], writes=[sreg["xn1_scr"]],
                      tag="R1st")

    def phase_cshift(bnd_tile, r_bnd, ccname):
        p.dma("sync", cc_in[ccname], s_bcc, reads=[r_bcc],
              writes=[sreg["cin_" + ccname]], tag=ccname + "a")
        p.collective("AllGather", ALU.bypass, [cc_in[ccname]], [cc_out[ccname]],
                     reads=[sreg["cin_" + ccname]],
                     writes=[sreg["cout_" + ccname]])
        p.dma("sync", s_cc[:, 0:NDC], cc_out[ccname][0:128, :],
              reads=[sreg["cout_" + ccname]], writes=[r_cc], tag=ccname + "l")
        tss(bnd_tile, s_cc[:, 0:NDC], s_parity[:, 0:1], ALU.mult,
            [r_cc, r_parity], [r_bnd], ccname + "m")

    def phase_r1b():
        wsl = [wa.alloc(f"Kw{dc}", D) for dc in range(NDC)]
        r_wsl = p.region("Kw")
        mx = [wa.alloc(f"Km{dc}", TC) for dc in range(NDC)]
        r_mx = p.region("Km")
        xnn = [af.alloc(f"Kx{dc}", TC + 1) for dc in range(NDC)]
        r_xn = p.region("Kx")
        dd = af.alloc("Kd", TC)
        r_dd = p.region("Kd")
        ko = [af.alloc(f"Ko{i}", TC) for i in range(4)]
        r_ko = [p.region(f"Ko{i}") for i in range(4)]

        for (wname, sv, rv, dst_s, dname, sig) in [
                ("L1_attn_w_key", s_tmk, r_tmk, k_scr, "k_scr", False),
                ("L1_attn_w_value", s_tmv, r_tmv, v_scr, "v_scr", False),
                ("L1_attn_w_recept", s_tmr, r_tmr, sr_scr, "sr_scr", True)]:
            for dc in range(NDC):
                p.dma("gpsimd", wsl[dc],
                      ins[wname][dc * 128:(dc + 1) * 128, :],
                      writes=[r_wsl], tag="Kwl")
            for t in range(NTC):
                load_shifted(xn1_scr, sreg["xn1_scr"], xnn, r_xn, t,
                             s_bnd1, r_bnd1, "Kxl")
                for dc in range(NDC):
                    cur = xnn[dc][:, 1:TC + 1]
                    lx = xnn[dc][:, 0:TC]
                    tt(ALU.subtract, dd, cur, lx, [r_xn], [r_dd], "Kmd")
                    v_op(lambda e, dc=dc, lx=lx, sv=sv: e.scalar_tensor_tensor(
                        mx[dc], dd, sv[:, dc:dc + 1], lx, ALU.mult, ALU.add),
                        [r_dd, rv, r_xn], [r_mx], "Kmx")
                for dc2 in range(NDC):
                    i = dc2 % 4
                    bank = dc2 % 4
                    mm_acc(psb[bank],
                           [(wsl[dc][:, dc2 * 128:(dc2 + 1) * 128], mx[dc])
                            for dc in range(NDC)],
                           [r_wsl, r_mx], rps[bank], "Kmm")
                    if sig:
                        act(ko[i], psb[bank], AF.Sigmoid, [rps[bank]],
                            [r_ko[i]], tag="Ksg")
                    else:
                        scopy(ko[i], psb[bank], [rps[bank]], [r_ko[i]],
                              tag="Kev")
                    p.dma("sync", dst_s[dc2 * 128:(dc2 + 1) * 128,
                                          t * TC:(t + 1) * TC],
                          ko[i], reads=[r_ko[i]], writes=[sreg[dname]],
                          tag="Kst")

    def phase_r2a():
        kt = [af.alloc(f"Qk{i}", TC) for i in range(2)]
        r_kt = [p.region(f"Qk{i}") for i in range(2)]
        vt = [af.alloc(f"Qv{i}", TC) for i in range(2)]
        r_vt = [p.region(f"Qv{i}") for i in range(2)]
        gt = [af.alloc(f"Qg{i}", TC) for i in range(2)]
        r_gt = [p.region(f"Qg{i}") for i in range(2)]
        gv = [af.alloc(f"Qgv{i}", TC) for i in range(2)]
        r_gv = [p.region(f"Qgv{i}") for i in range(2)]
        lamb = af.alloc("Ql", TC)
        r_lamb = p.region("Ql")
        ao = [af.alloc(f"Qa{i}", TC) for i in range(2)]
        r_ao = [p.region(f"Qa{i}") for i in range(2)]
        bo = [af.alloc(f"Qb{i}", TC) for i in range(2)]
        r_bo = [p.region(f"Qb{i}") for i in range(2)]
        for t in range(NTC):
            for dc in range(NDC):
                i = dc % 2
                p.dma("sync", kt[i],
                      k_scr[dc * 128:(dc + 1) * 128, t * TC:(t + 1) * TC],
                      reads=[sreg["k_scr"]], writes=[r_kt[i]], tag="Qkl")
                p.dma("sync", vt[i],
                      v_scr[dc * 128:(dc + 1) * 128, t * TC:(t + 1) * TC],
                      reads=[sreg["v_scr"]], writes=[r_vt[i]], tag="Qvl")
                act(gt[i], kt[i], AF.Exp, [r_kt[i]], [r_gt[i]], tag="Qg")
                tt(ALU.mult, gv[i], gt[i], vt[i], [r_gt[i], r_vt[i]],
                   [r_gv[i]], "Qgv")
                vcopy(lamb, s_lam[:, dc:dc + 1].broadcast_to((128, TC)),
                      [r_lam], [r_lamb], "Qlb")
                scan(ao[i], lamb, gv[i], s_acar[:, dc:dc + 1],
                     [r_lamb, r_gv[i], r_acar], [r_ao[i]], "Qas")
                vcopy(s_acar[:, dc:dc + 1], ao[i][:, TC - 1:TC], [r_ao[i]],
                      [r_acar], "Qac")
                scan(bo[i], lamb, gt[i], s_bcar[:, dc:dc + 1],
                     [r_lamb, r_gt[i], r_bcar], [r_bo[i]], "Qbs")
                vcopy(s_bcar[:, dc:dc + 1], bo[i][:, TC - 1:TC], [r_bo[i]],
                      [r_bcar], "Qbc")
                p.dma("sync", al_scr[dc * 128:(dc + 1) * 128,
                                       t * TC:(t + 1) * TC],
                      ao[i], reads=[r_ao[i]], writes=[sreg["al_scr"]],
                      tag="Qast")
                p.dma("sync", bl_scr[dc * 128:(dc + 1) * 128,
                                       t * TC:(t + 1) * TC],
                      bo[i], reads=[r_bo[i]], writes=[sreg["bl_scr"]],
                      tag="Qbst")

    def phase_c4():
        p.dma("gpsimd", cc_in["c4"][:, 0:NDC], s_acar, reads=[r_acar],
              writes=[sreg["cin_c4"]], tag="c4a")
        p.dma("gpsimd", cc_in["c4"][:, NDC:2 * NDC], s_bcar, reads=[r_bcar],
              writes=[sreg["cin_c4"]], tag="c4b")
        p.collective("AllGather", ALU.bypass, [cc_in["c4"]], [cc_out["c4"]],
                     reads=[sreg["cin_c4"]], writes=[sreg["cout_c4"]])
        p.dma("sync", s_ab0, cc_out["c4"][0:128, :],
              reads=[sreg["cout_c4"]], writes=[r_ab0], tag="c4l")
        tss(s_ab0, s_ab0, s_parity[:, 0:1], ALU.mult, [r_ab0, r_parity],
            [r_ab0], "c4m")

    def phase_r2b(src, rsrc):
        wo = [wa.alloc(f"Ww{dc}", D) for dc in range(NDC)]
        r_wo = p.region("Ww")
        for dc in range(NDC):
            p.dma("gpsimd", wo[dc],
                  ins["L1_attn_w_output"][dc * 128:(dc + 1) * 128, :],
                  writes=[r_wo], tag="Wwl")
        wkb = [wa.alloc(f"Wk{dc}", TC) for dc in range(NDC)]
        r_wkb = p.region("Wk")
        xn2 = [bfa.alloc(f"Wn{dc}", TC) for dc in range(NDC)]
        r_xn2 = p.region("Wn")
        kt = [af.alloc(f"Wkt{i}", TC) for i in range(2)]
        r_kt = [p.region(f"Wkt{i}") for i in range(2)]
        vt = [af.alloc(f"Wvt{i}", TC) for i in range(2)]
        r_vt = [p.region(f"Wvt{i}") for i in range(2)]
        alt = [af.alloc(f"Wa{i}", TC) for i in range(2)]
        r_alt = [p.region(f"Wa{i}") for i in range(2)]
        blt = [af.alloc(f"Wb{i}", TC) for i in range(2)]
        r_blt = [p.region(f"Wb{i}") for i in range(2)]
        srt = [af.alloc(f"Wr{i}", TC) for i in range(2)]
        r_srt = [p.region(f"Wr{i}") for i in range(2)]
        iob = af.alloc("Wio", TC)
        r_iob = p.region("Wio")
        lamt = af.alloc("Wlm", TC)
        r_lamt = p.region("Wlm")
        gt = af.alloc("Wg", TC)
        r_gt = p.region("Wg")
        gv = af.alloc("Wgv", TC)
        r_gv = p.region("Wgv")
        num = af.alloc("Wnu", TC)
        r_num = p.region("Wnu")
        den = af.alloc("Wde", TC)
        r_den = p.region("Wde")
        xbt = [bfa.alloc(f"Wxb{dc}", TC) for dc in range(NDC)]
        r_xb = p.region("Wxb")
        xa = [waf.alloc(f"Wxa{dc}", TC) for dc in range(NDC)]
        r_xa = p.region("Wxa")
        xt8 = [waf.alloc(f"Wx8{dc}", TC) for dc in range(NDC)]
        r_xt8 = p.region("Wx8")

        for t in range(NTC):
            p.dma("sync", s_iorow, ins["iota_row"][0:1, t * TC:(t + 1) * TC],
                  writes=[r_iorow], tag="Wio")
            t_op(lambda e: e.matmul(psb[6], s_ones_row, s_iorow,
                                    start=True, stop=True),
                 [r_iorow, r_ones_row], [rps[6]], "Wioc")
            vcopy(iob, psb[6], [rps[6]], [r_iob])
            for dc in range(NDC):
                i = dc % 2
                p.dma("sync", kt[i],
                      k_scr[dc * 128:(dc + 1) * 128, t * TC:(t + 1) * TC],
                      reads=[sreg["k_scr"]], writes=[r_kt[i]], tag="Wkl")
                p.dma("sync", vt[i],
                      v_scr[dc * 128:(dc + 1) * 128, t * TC:(t + 1) * TC],
                      reads=[sreg["v_scr"]], writes=[r_vt[i]], tag="Wvl")
                if t == 0:
                    v_op(lambda e, i=i: e.memset(alt[i][:, 0:1], 0.0),
                         [], [r_alt[i]], "Waz")
                    p.dma("sync", alt[i][:, 1:TC],
                          al_scr[dc * 128:(dc + 1) * 128, 0:TC - 1],
                          reads=[sreg["al_scr"]], writes=[r_alt[i]], tag="Wal")
                    v_op(lambda e, i=i: e.memset(blt[i][:, 0:1], 0.0),
                         [], [r_blt[i]], "Wbz")
                    p.dma("sync", blt[i][:, 1:TC],
                          bl_scr[dc * 128:(dc + 1) * 128, 0:TC - 1],
                          reads=[sreg["bl_scr"]], writes=[r_blt[i]], tag="Wbl")
                else:
                    p.dma("sync", alt[i],
                          al_scr[dc * 128:(dc + 1) * 128,
                                 t * TC - 1:t * TC + TC - 1],
                          reads=[sreg["al_scr"]], writes=[r_alt[i]], tag="Wal")
                    p.dma("sync", blt[i],
                          bl_scr[dc * 128:(dc + 1) * 128,
                                 t * TC - 1:t * TC + TC - 1],
                          reads=[sreg["bl_scr"]], writes=[r_blt[i]], tag="Wbl")
                p.dma("sync", srt[i],
                      sr_scr[dc * 128:(dc + 1) * 128, t * TC:(t + 1) * TC],
                      reads=[sreg["sr_scr"]], writes=[r_srt[i]], tag="Wsl")
                act(gt, kt[i], AF.Exp, [r_kt[i]], [r_gt], tag="Wge")
                tt(ALU.mult, gv, gt, vt[i], [r_gt, r_vt[i]], [r_gv], "Wgv")
                s_op(lambda e, dc=dc: e.activation(
                    lamt, iob, AF.Exp, scale=s_negw[:, dc:dc + 1]),
                    [r_iob, r_negw], [r_lamt], "Wlam")
                v_op(lambda e, dc=dc, i=i: e.scalar_tensor_tensor(
                    alt[i], lamt, s_ab0[:, dc:dc + 1], alt[i], ALU.mult, ALU.add),
                    [r_lamt, r_ab0, r_alt[i]], [r_alt[i]], "Wat")
                v_op(lambda e, dc=dc, i=i: e.scalar_tensor_tensor(
                    blt[i], lamt, s_ab0[:, NDC + dc:NDC + dc + 1], blt[i],
                    ALU.mult, ALU.add),
                    [r_lamt, r_ab0, r_blt[i]], [r_blt[i]], "Wbt")
                v_op(lambda e, dc=dc, i=i: e.scalar_tensor_tensor(
                    num, gv, s_Eu[:, dc:dc + 1], alt[i], ALU.mult, ALU.add),
                    [r_gv, r_Eu, r_alt[i]], [r_num], "Wnum")
                v_op(lambda e, dc=dc, i=i: e.scalar_tensor_tensor(
                    den, gt, s_Eu[:, dc:dc + 1], blt[i], ALU.mult, ALU.add),
                    [r_gt, r_Eu, r_blt[i]], [r_den], "Wden")
                v_op(lambda e: e.reciprocal(den, den), [r_den], [r_den], "Wrd")
                tt(ALU.mult, num, num, den, [r_num, r_den], [r_num], "Wwkv")
                if debug_r2b:
                    p.dma("gpsimd", P_scr[dc * 128:(dc + 1) * 128,
                                          t * TC:(t + 1) * TC],
                          alt[i], reads=[r_alt[i]], writes=[sreg["P_scr"]],
                          tag="DBGa")
                    p.dma("gpsimd", P_scr[1024 + dc * 128:1024 + (dc + 1) * 128,
                                          t * TC:(t + 1) * TC],
                          blt[i], reads=[r_blt[i]], writes=[sreg["P_scr"]],
                          tag="DBGb")
                    p.dma("gpsimd", y_scr[dc * 128:(dc + 1) * 128,
                                          t * TC:(t + 1) * TC],
                          num, reads=[r_num], writes=[sreg["y_scr"]],
                          tag="DBGn")
                    p.dma("gpsimd", y_scr[1024 + dc * 128:1024 + (dc + 1) * 128,
                                          t * TC:(t + 1) * TC],
                          lamt, reads=[r_lamt], writes=[sreg["y_scr"]],
                          tag="DBGl")
                tt(ALU.mult, wkb[dc], num, srt[i], [r_num, r_srt[i]], [r_wkb],
                   "Wsr")
            load_xtiles(src, rsrc, xt8, r_xt8, t, tag="Wxl")
            emit_ln(xt8, [r_xt8], xa, r_xa)
            for dc2 in range(NDC):
                bank = dc2 % 4
                mm_acc(psb[bank],
                       [(wo[dc][:, dc2 * 128:(dc2 + 1) * 128], wkb[dc])
                        for dc in range(NDC)],
                       [r_wo, r_wkb], rps[bank], "Wout")
                v_op(lambda e, dc2=dc2, bank=bank: e.tensor_tensor(
                    xbt[dc2], xa[dc2], psb[bank], ALU.add),
                    [r_xa, rps[bank]], [r_xb], "Wxb")
                p.dma("sync", xb_scr[dc2 * 128:(dc2 + 1) * 128,
                                       t * TC:(t + 1) * TC],
                      xbt[dc2], reads=[r_xb], writes=[sreg["xb_scr"]],
                      tag="Wxbs")
            emit_ln(xbt, [r_xb], xn2, r_xn2, g=s_ln2_w, b=s_ln2_b,
                    rgb=[r_ln2_w, r_ln2_b])
            for dc in range(NDC):
                if t == NTC - 1:
                    vcopy(s_bcc[:, dc:dc + 1], xn2[dc][:, TC - 1:TC],
                          [r_xn2], [r_bcc], "Wn2b")
                p.dma("sync", xn2_scr[dc * 128:(dc + 1) * 128,
                                        t * TC:(t + 1) * TC],
                      xn2[dc], reads=[r_xn2], writes=[sreg["xn2_scr"]],
                      tag="Wn2s")

    def phase_r3a():
        frc = [wa.alloc(f"Ff{dc}", D) for dc in range(NDC)]
        r_frc = p.region("Ff")
        for dc in range(NDC):
            p.dma("gpsimd", frc[dc],
                  ins["L1_attn_f_w_recept"][dc * 128:(dc + 1) * 128, :],
                  writes=[r_frc], tag="Ffl")
        mfr = [wa.alloc(f"Fm{dc}", TC) for dc in range(NDC)]
        r_mfr = p.region("Fm")
        xnn = [af.alloc(f"Fx{dc}", TC + 1) for dc in range(NDC)]
        r_xn = p.region("Fx")
        dd = af.alloc("Fd", TC)
        r_dd = p.region("Fd")
        so = [af.alloc(f"Fo{i}", TC) for i in range(4)]
        r_so = [p.region(f"Fo{i}") for i in range(4)]
        for t in range(NTC):
            load_shifted(xn2_scr, sreg["xn2_scr"], xnn, r_xn, t,
                         s_bnd2, r_bnd2, "Fxl")
            for dc in range(NDC):
                cur = xnn[dc][:, 1:TC + 1]
                lx = xnn[dc][:, 0:TC]
                tt(ALU.subtract, dd, cur, lx, [r_xn], [r_dd], "Fmd")
                v_op(lambda e, dc=dc, lx=lx: e.scalar_tensor_tensor(
                    mfr[dc], dd, s_ftmr[:, dc:dc + 1], lx, ALU.mult, ALU.add),
                    [r_dd, r_ftmr, r_xn], [r_mfr], "Fmx")
            for dc2 in range(NDC):
                i = dc2 % 4
                bank = dc2 % 4
                mm_acc(psb[bank],
                       [(frc[dc][:, dc2 * 128:(dc2 + 1) * 128], mfr[dc])
                        for dc in range(NDC)],
                       [r_frc, r_mfr], rps[bank], "Fmm")
                act(so[i], psb[bank], AF.Sigmoid, [rps[bank]], [r_so[i]],
                    tag="Fsg")
                p.dma("sync", sg_scr[dc2 * 128:(dc2 + 1) * 128,
                                       t * TC:(t + 1) * TC],
                      so[i], reads=[r_so[i]], writes=[sreg["sg_scr"]],
                      tag="Fst")

    def phase_r3b(src, rsrc, dst, rdst):
        HRH = HR // 2
        NHH2 = NHR // 2
        fwk = [wa.alloc(f"Gk{dc}", HRH) for dc in range(NDC)]
        r_fwk = p.region("Gk")
        fq = [wa.alloc(f"Gq{hh}", TC) for hh in range(NHH2)]
        r_fq = p.region("Gq")
        mfk = [wa.alloc(f"Gm{dc}", TC) for dc in range(NDC)]
        r_mfk = p.region("Gm")
        fwv = [wa.alloc(f"Gv{i}", D) for i in range(2)]
        r_fwv = [p.region(f"Gv{i}") for i in range(2)]
        xnn = [af.alloc(f"Gx{dc}", TC + 1) for dc in range(NDC)]
        r_xn = p.region("Gx")
        dd = af.alloc("Gd", TC)
        r_dd = p.region("Gd")
        vk = [af.alloc(f"Gw{dc}", TC) for dc in range(NDC)]
        r_vk = p.region("Gw")
        sgt = [af.alloc(f"Gs{i}", TC) for i in range(2)]
        r_sg = [p.region(f"Gs{i}") for i in range(2)]
        xt = [bfa.alloc(f"Gxt{i}", TC) for i in range(2)]
        r_xt = [p.region(f"Gxt{i}") for i in range(2)]
        xbt = [bfa.alloc(f"Gxb{i}", TC) for i in range(2)]
        r_xbt = [p.region(f"Gxb{i}") for i in range(2)]
        xo = [bfa.alloc(f"Go{i}", TC) for i in range(2)]
        r_xo = [p.region(f"Go{i}") for i in range(2)]
        rtmp = [waf.alloc(f"Gt{i}", TC) for i in range(2)]
        r_rtmp = p.region("Gt")

        for t in range(NTC):
            load_shifted(xn2_scr, sreg["xn2_scr"], xnn, r_xn, t,
                         s_bnd2, r_bnd2, "Gxl")
            for dc in range(NDC):
                cur = xnn[dc][:, 1:TC + 1]
                lx = xnn[dc][:, 0:TC]
                tt(ALU.subtract, dd, cur, lx, [r_xn], [r_dd], "Gmd")
                v_op(lambda e, dc=dc, lx=lx: e.scalar_tensor_tensor(
                    mfk[dc], dd, s_ftmk[:, dc:dc + 1], lx, ALU.mult, ALU.add),
                    [r_dd, r_ftmk, r_xn], [r_mfk], "Gmx")
            for half in range(2):
                for dc in range(NDC):
                    p.dma("gpsimd", fwk[dc],
                          ins["L1_attn_f_w_key"][dc * 128:(dc + 1) * 128,
                                                 half * HRH:(half + 1) * HRH],
                          writes=[r_fwk], tag="Gkl")
                for hh in range(NHH2):
                    bank = hh % 2
                    mm_acc(psb[bank],
                           [(fwk[dc][:, hh * 128:(hh + 1) * 128], mfk[dc])
                            for dc in range(NDC)],
                           [r_fwk, r_mfk], rps[bank], "Gfk")
                    tss(rtmp[bank], psb[bank], 0.0, ALU.max,
                        [rps[bank]], [r_rtmp], "Grl")
                    s_op(lambda e, hh=hh, bank=bank: e.activation(
                        fq[hh], rtmp[bank], AF.Square),
                        [r_rtmp], [r_fq], "Gsq")
                for hh in range(NHH2):
                    hc = half * NHH2 + hh
                    sl = hh % 2
                    p.dma("gpsimd", fwv[sl],
                          ins["L1_attn_f_w_value"][hc * 128:(hc + 1) * 128, :],
                          writes=[r_fwv[sl]], tag="Gvl")
                    mm_banks([(psb[dc], fwv[sl][:, dc * 128:(dc + 1) * 128],
                               fq[hh]) for dc in range(NDC)],
                             [r_fwv[sl], r_fq], [rps[d] for d in range(NDC)],
                             hh == 0, hh == NHH2 - 1, "Gvk")
                for dc in range(NDC):
                    if half == 0:
                        scopy(vk[dc], psb[dc], [rps[dc]], [r_vk], tag="Gve")
                    else:
                        v_op(lambda e, dc=dc: e.scalar_tensor_tensor(
                            vk[dc], psb[dc], 1.0, vk[dc], ALU.mult, ALU.add),
                            [rps[dc], r_vk], [r_vk], "Gva")
            for dc in range(NDC):
                i = dc % 2
                p.dma("sync", sgt[i],
                      sg_scr[dc * 128:(dc + 1) * 128, t * TC:(t + 1) * TC],
                      reads=[sreg["sg_scr"]], writes=[r_sg[i]], tag="Gsl")
                p.dma("sync", xt[i],
                      src[dc * 128:(dc + 1) * 128, t * TC:(t + 1) * TC],
                      reads=[rsrc], writes=[r_xt[i]], tag="Gxtl")
                p.dma("sync", xbt[i],
                      xb_scr[dc * 128:(dc + 1) * 128, t * TC:(t + 1) * TC],
                      reads=[sreg["xb_scr"]], writes=[r_xbt[i]], tag="Gxbl")
                tt(ALU.mult, rtmp[i], sgt[i], vk[dc], [r_sg[i], r_vk],
                   [r_rtmp], "Gc")
                tt(ALU.add, rtmp[i], rtmp[i], xbt[i],
                   [r_rtmp, r_xbt[i]], [r_rtmp], "Ga1")
                tt(ALU.add, xo[i], rtmp[i], xt[i], [r_rtmp, r_xt[i]],
                   [r_xo[i]], "Ga2")
                p.dma("sync", dst[dc * 128:(dc + 1) * 128, t * TC:(t + 1) * TC],
                      xo[i], reads=[r_xo[i]], writes=[rdst], tag="Gst")

    def phase_PZ(src, rsrc):
        xr = [af.alloc(f"Zx{i}", TC) for i in range(2)]
        r_xr = [p.region(f"Zx{i}") for i in range(2)]
        eg = [af.alloc(f"Ze{i}", 128) for i in range(4)]
        r_eg = [p.region(f"Ze{i}") for i in range(4)]
        for t in range(NTC):
            for dc in range(NDC):
                i = dc % 2
                p.dma("sync", xr[i],
                      src[dc * 128:(dc + 1) * 128, t * TC:(t + 1) * TC],
                      reads=[rsrc], writes=[r_xr[i]], tag="Zld")
                for tb in range(TC // 128):
                    j = (dc * 4 + tb) % 4
                    bank = 4 + j % 2
                    pe_tr(psb[bank][:, 0:128], xr[i][:, tb * 128:(tb + 1) * 128],
                          [r_xr[i]], rps[bank])
                    vcopy(eg[j], psb[bank][:, 0:128], [rps[bank]], [r_eg[j]])
                    p.dma("sync",
                          out_ap[t * TC + tb * 128:t * TC + (tb + 1) * 128,
                                 dc * 128:(dc + 1) * 128],
                          eg[j], reads=[r_eg[j]], writes=[sreg["out"]],
                          tag="Zst")

    schedule = [
        ("P0", phase_P0, "A"),
        ("P1", lambda: phase_ffn0(xfA, sreg["xfA"], xfB, sreg["xfB"], "P1"), "B"),
        ("prepMoE", prep_moe_transpose, "B"),
        ("P2a1", lambda: phase_P2a1(xfB, sreg["xfB"]), "B"),
        ("P2a2", phase_P2a2, "B"),
        ("C1", phase_C1, "B"),
        ("P2b", phase_P2b_real, "B"),
        ("C2", phase_C2, "B"),
        ("P2c2", lambda: phase_P2c(xfB, sreg["xfB"], xfA, sreg["xfA"]), "A"),
        ("P3", lambda: phase_ffn0(xfA, sreg["xfA"], xfB, sreg["xfB"], "P3"), "B"),
        ("P4pre", lambda: phase_moepre(xfB, sreg["xfB"], "M4"), "B"),
        ("P4", lambda: phase_moe(xfB, sreg["xfB"], xfA, sreg["xfA"], "Q4"), "A"),
        ("P5r1", lambda: phase_r1(xfA, sreg["xfA"]), "A"),
        ("C3", lambda: phase_cshift(s_bnd1, r_bnd1, "c3"), "A"),
        ("P5r1b", phase_r1b, "A"),
        ("P5r2a", phase_r2a, "A"),
        ("C4", phase_c4, "A"),
        ("P5r2b", lambda: phase_r2b(xfA, sreg["xfA"]), "A"),
        ("C5", lambda: phase_cshift(s_bnd2, r_bnd2, "c5"), "A"),
        ("P5r3a", phase_r3a, "A"),
        ("P5r3b", lambda: phase_r3b(xfA, sreg["xfA"], xfB, sreg["xfB"]), "B"),
        ("P6pre", lambda: phase_moepre(xfB, sreg["xfB"], "M6"), "B"),
        ("P6", lambda: phase_moe(xfB, sreg["xfB"], xfA, sreg["xfA"], "Q6"), "A"),
    ]

    cur = "A"
    for name, fn, after in schedule:
        fn()
        phase_reset()
        cur = after
        if upto is not None and name == upto:
            break

    phase_PZ(xfA if cur == "A" else xfB, sreg["xfA" if cur == "A" else "xfB"])
    phase_reset()

    dump_names = []
    for nm in dumps:
        tsrc = scr_tensors[nm]
        shp = list(tsrc.shape)
        dap = nc.dram_tensor("dbg_" + nm, shp, tsrc.dtype,
                             kind="ExternalOutput").ap()
        rows = shp[0]
        for r0 in range(0, rows, 128):
            nr = min(128, rows - r0)
            p.dma("gpsimd", dap[r0:r0 + nr, :], tsrc[r0:r0 + nr, :],
                  reads=[sreg[nm]], writes=[], tag="dump")
        dump_names.append("dbg_" + nm)

    p.barrier()
    p.build()
    return nc, stack, dump_names


def make_in_maps(inputs):
    x = np.ascontiguousarray(inputs["L0_attn_x"])
    weights = {}
    for k, v in inputs.items():
        if k in ("L0_attn_x", "L0_attn_res_scale"):
            continue
        a = np.ascontiguousarray(v)
        weights[k] = a
    def col128(v):
        v = np.asarray(v).reshape(-1)
        n = v.shape[0] // 128
        return np.ascontiguousarray(v.reshape(n, 128).T)

    for k in ("L0_attn_ln_g", "L0_attn_ln_b", "L0_attn_sq_b", "L0_attn_srw_b",
              "L1_attn_ln1_w", "L1_attn_ln1_b", "L1_attn_ln2_w",
              "L1_attn_ln2_b", "L1_attn_time_decay", "L1_attn_time_first",
              "L1_attn_time_mix_k", "L1_attn_time_mix_v", "L1_attn_time_mix_r",
              "L1_attn_f_time_mix_k", "L1_attn_f_time_mix_r",
              "L0_attn_conv_b", "L0_attn_dt_b", "L0_attn_D"):
        weights[k] = col128(weights[k])
    cw = np.asarray(weights["L0_attn_conv_w"])
    weights["L0_attn_conv_w"] = np.ascontiguousarray(
        cw.reshape(NIC, 128, 4).transpose(1, 0, 2).reshape(128, NIC * 4))
    al = np.asarray(weights["L0_attn_A_log"])
    weights["L0_attn_A_log"] = np.ascontiguousarray(
        al.reshape(NIC, 128, NST).transpose(1, 0, 2).reshape(128, NIC * NST))
    weights["L1_ffn_w1"] = np.ascontiguousarray(
        inputs["L1_ffn_w1"]).reshape(NE * HM, D)
    weights["L1_ffn_w2"] = np.ascontiguousarray(
        inputs["L1_ffn_w2"]).reshape(NE * D, HM)
    ident = np.eye(128, dtype=np.float32)
    iota = np.ascontiguousarray(np.arange(T, dtype=np.float32)[None, :])
    rs = np.ascontiguousarray(np.broadcast_to(
        np.asarray(inputs["L0_attn_res_scale"], np.float32).reshape(1, 1),
        (128, 1)))
    maps = []
    for c in range(8):
        b, h = c // 2, c % 2
        m = dict(weights)
        m["x_in"] = np.ascontiguousarray(x[b, h * T:(h + 1) * T, :])
        m["parity"] = np.full((128, 1), float(h), np.float32)
        m["iota_row"] = iota
        m["identity"] = ident
        m["res_scale_col"] = rs
        maps.append(m)
    return maps


_CACHE = {}


def kernel(**inputs):
    if "full" not in _CACHE:
        _CACHE["full"] = build_program()
    nc, stack, _ = _CACHE["full"]
    maps = make_in_maps(inputs)
    res = run_bass_kernel_spmd(nc, maps, core_ids=list(range(8)))
    out = np.empty((4, 2 * T, D), dtype=np.float32)
    for c in range(8):
        b, h = c // 2, c % 2
        out[b, h * T:(h + 1) * T, :] = res.results[c]["out"]
    return out


# version 04
# speedup vs baseline: 1.1464x; 1.1464x over previous
import contextlib

import numpy as np
import concourse.bass as bass
import concourse.mybir as mybir
from concourse.bass_utils import run_bass_kernel_spmd

F32 = mybir.dt.float32
BF16 = mybir.dt.bfloat16
I32 = mybir.dt.int32
AF = mybir.ActivationFunctionType
ALU = mybir.AluOpType
AX = mybir.AxisListType

ENGINES = ("sync", "scalar", "vector", "tensor", "gpsimd")
DMA_QUEUES = {"sync": 8, "scalar": 6, "gpsimd": 8}

T = 4096
TC = 512
NTC = T // TC
D = 1024
NDC = D // 128
DI = 2048
NIC = DI // 128
NST = 16
DTR = 64
H0 = 3072
NH0 = H0 // 128
HM = 4096
NHM = HM // 128
NE = 8
HR = 4096
NHR = HR // 128
EPS = 1e-5
ST_TOK = 1024
NSUP = T // ST_TOK
PAIRS = [[0, 1], [2, 3], [4, 5], [6, 7]]


class Reg:
    __slots__ = ("name", "writer", "readers")

    def __init__(self, name):
        self.name = name
        self.writer = None
        self.readers = []


class Prog:
    def __init__(self, nc, stack):
        self.nc = nc
        self.stack = stack
        self.ops = {e: [] for e in ENGINES}
        self.psem = {e: self._mk_sem("p_" + e) for e in ENGINES}
        self.pcount = {e: 0 for e in ENGINES}
        self.dsem = {}
        self.dcount = {}
        for q, n in DMA_QUEUES.items():
            self.dsem[q] = [self._mk_sem(f"d_{q}{i}") for i in range(n)]
            self.dcount[q] = [0] * n
        self.drr = {q: 0 for q in DMA_QUEUES}
        self.frontier = {e: {} for e in ENGINES}
        self.csem = self._mk_sem("coll")
        self.ccount = 0
        self.n_ops = 0

    def _mk_sem(self, name):
        return self.stack.enter_context(self.nc.semaphore(name))

    def region(self, name):
        return Reg(name)

    def _waits_for(self, ename, reads, writes):
        waits = []
        fr = self.frontier[ename]

        def need(dep, skip_same_engine=False):
            if dep is None:
                return
            sem, val, eng = dep
            if skip_same_engine and eng == ename:
                return
            if fr.get(sem, 0) < val:
                waits.append((sem, val))
                fr[sem] = val

        for r in reads:
            need(r.writer)
        for w in writes:
            need(w.writer, skip_same_engine=True)
            for rd in w.readers:
                need(rd, skip_same_engine=True)
        return waits

    def op(self, ename, fn, reads=(), writes=(), tag=""):
        waits = self._waits_for(ename, reads, writes)
        self.pcount[ename] += 1
        comp = (self.psem[ename], self.pcount[ename], ename)
        for r in reads:
            r.readers.append(comp)
        for w in writes:
            w.writer = comp
            w.readers = []
        self.ops[ename].append((waits, fn, comp[0], 1, tag))
        self.n_ops += 1

    def dma(self, q, out, in_, reads=(), writes=(), tag="", transpose=False):
        i = self.drr[q]
        self.drr[q] = (i + 1) % len(self.dsem[q])
        sem = self.dsem[q][i]
        prev = self.dcount[q][i]
        waits = self._waits_for(q, reads, writes)
        fr = self.frontier[q]
        if prev and fr.get(sem, 0) < prev:
            waits.append((sem, prev))
            fr[sem] = prev
        self.dcount[q][i] = prev + 16
        comp = (sem, prev + 16, "@dma")
        for r in reads:
            r.readers.append(comp)
        for w in writes:
            w.writer = comp
            w.readers = []
        if transpose:
            def fn(eng, out=out, in_=in_):
                return eng.dma_start_transpose(out=out, in_=in_)
        else:
            def fn(eng, out=out, in_=in_):
                return eng.dma_start(out=out, in_=in_)
        self.ops[q].append((waits, fn, sem, 16, tag))
        self.n_ops += 1

    def collective(self, kind, op, ins_, outs_, reads=(), writes=(), tag="coll"):
        waits = self._waits_for("gpsimd", reads, writes)
        self.ccount += 1
        comp = (self.csem, self.ccount, "@coll")
        for r in reads:
            r.readers.append(comp)
        for w in writes:
            w.writer = comp
            w.readers = []

        def fn(eng, kind=kind, op=op, ins_=ins_, outs_=outs_):
            return eng.collective_compute(kind, op, PAIRS,
                                          [a.opt() for a in ins_],
                                          [a.opt() for a in outs_])

        self.ops["gpsimd"].append((waits, fn, self.csem, 1, tag))
        self.n_ops += 1

    def barrier(self):
        deps = []
        for e in ENGINES:
            if self.pcount[e]:
                deps.append((self.psem[e], self.pcount[e]))
        for q in DMA_QUEUES:
            for i, c in enumerate(self.dcount[q]):
                if c:
                    deps.append((self.dsem[q][i], c))
        if self.ccount:
            deps.append((self.csem, self.ccount))
        for e in ENGINES:
            fr = self.frontier[e]
            ws = [(s, v) for (s, v) in deps if fr.get(s, 0) < v]
            for s, v in ws:
                fr[s] = v
            if ws:
                self.ops[e].append((ws, None, None, 0, "barrier"))

    def build(self):
        nc = self.nc
        with nc.Block() as block:
            for ename in ENGINES:
                ops = self.ops[ename]

                def mk(ops):
                    def run(eng):
                        for waits, fn, sem, inc, tag in ops:
                            for (s, v) in waits:
                                eng.wait_ge(s, v)
                            if fn is not None:
                                instr = fn(eng)
                                if sem is not None and instr is not None:
                                    instr.then_inc(sem, inc)
                    return run

                getattr(block, ename)(mk(ops))


class Arena:
    def __init__(self, p, base_ap, cols, name):
        self.p = p
        self.base = base_ap
        self.cols = cols
        self.off = 0
        self.name = name
        self.buddy = None

    def alloc(self, name, c):
        assert self.off + c <= self.cols, \
            f"{self.name}/{name}: overflow {self.off}+{c}>{self.cols}"
        apv = self.base[:, self.off:self.off + c]
        self.off += c
        if self.buddy is not None:
            assert self.off + 2 * self.buddy.off <= self.cols, \
                f"{self.name}/{name}: collides with top arena"
        return apv

    def reset(self):
        self.off = 0


class ArenaTop:

    def __init__(self, p, base_bf16, bf16_cols, name, buddy):
        self.p = p
        self.base = base_bf16
        self.cols = bf16_cols
        self.off = 0
        self.name = name
        self.buddy = buddy
        buddy.buddy = self

    def alloc(self, name, c):
        self.off += c
        lo = self.cols - 2 * self.off
        assert lo >= self.buddy.off, \
            f"{self.name}/{name}: collides with bf16 arena"
        return self.base[:, lo:lo + 2 * c].bitcast(F32)

    def reset(self):
        self.off = 0


def build_program(upto=None, dumps=(), debug_r2b=False):
    nc = bass.Bass("TRN2", target_bir_lowering=False)
    stack = contextlib.ExitStack()
    p = Prog(nc, stack)

    ins = {}

    def din(name, shape, dt=F32):
        ins[name] = nc.dram_tensor(name, shape, dt, kind="ExternalInput").ap()
        return ins[name]

    din("x_in", [T, D])
    for _v in ("L0_attn_ln_g", "L0_attn_ln_b", "L0_attn_sq_b", "L0_attn_srw_b",
               "L1_attn_ln1_w", "L1_attn_ln1_b", "L1_attn_ln2_w",
               "L1_attn_ln2_b", "L1_attn_time_decay", "L1_attn_time_first",
               "L1_attn_time_mix_k", "L1_attn_time_mix_v",
               "L1_attn_time_mix_r", "L1_attn_f_time_mix_k",
               "L1_attn_f_time_mix_r"):
        din(_v, [128, NDC])
    for _v in ("L0_attn_conv_b", "L0_attn_dt_b", "L0_attn_D"):
        din(_v, [128, NIC])
    din("L0_attn_sq_w", [D, D])
    din("L0_attn_srw_w", [D, D])
    din("L0_attn_in_proj_w", [D, 2 * DI])
    din("L0_attn_conv_w", [128, NIC * 4])
    din("L0_attn_x_proj_w", [DI, DTR + 2 * NST])
    din("L0_attn_dt_w", [DTR, DI])
    din("L0_attn_A_log", [128, NIC * NST])
    din("L0_attn_out_proj_w", [DI, D])
    din("L0_ffn_w_gate", [H0, D], I32)
    din("L0_ffn_w_up", [H0, D], I32)
    din("L0_ffn_w_down", [D, H0], I32)
    din("L1_attn_w_key", [D, D]); din("L1_attn_w_value", [D, D])
    din("L1_attn_w_recept", [D, D]); din("L1_attn_w_output", [D, D])
    din("L1_attn_f_w_key", [D, HR]); din("L1_attn_f_w_recept", [D, D])
    din("L1_attn_f_w_value", [HR, D])
    din("L1_ffn_router_w", [NE, D])
    din("L1_ffn_w1", [NE * HM, D])
    din("L1_ffn_w2", [NE * D, HM])
    din("parity", [128, 1])
    din("iota_row", [1, T])
    din("identity", [128, 128])
    din("res_scale_col", [128, 1])

    out_ap = nc.dram_tensor("out", [T, D], F32, kind="ExternalOutput").ap()

    scr_tensors = {}

    def scr(name, shape, dt=F32):
        t_ = nc.dram_tensor(name, shape, dt, kind="Internal").ap()
        scr_tensors[name] = t_
        return t_

    xfA = scr("xfA", [D, T])
    xfB = scr("xfB", [D, T])
    h_scr = scr("h_scr", [D, T], BF16)
    u_pre = scr("u_pre", [DI, T])
    z_scr = scr("z_scr", [DI, T])
    P_scr = scr("P_scr", [DI, T])
    B_scr = scr("B_scr", [NST, T])
    C_scr = scr("C_scr", [NST, T])
    y_scr = scr("y_scr", [DI, T])
    yg_scr = scr("yg_scr", [DI, T], BF16)
    xn1_scr = scr("xn1_scr", [D, T])
    xn2_scr = scr("xn2_scr", [D, T])
    k_scr = scr("k_scr", [D, T])
    v_scr = scr("v_scr", [D, T])
    sr_scr = scr("sr_scr", [D, T])
    al_scr = scr("al_scr", [D, T])
    bl_scr = scr("bl_scr", [D, T])
    xb_scr = scr("xb_scr", [D, T])
    sg_scr = scr("sg_scr", [D, T])
    xnm_scr = scr("xnm_scr", [D, T], BF16)
    gate_scr = scr("gate_scr", [NE, T])
    wg_bf = scr("wg_bf", [H0, D], BF16)
    wu_bf = scr("wu_bf", [H0, D], BF16)
    wd_bf = scr("wd_bf", [D, H0], BF16)
    w1_bf = scr("w1_bf", [NE * HM, D], BF16)
    w2_bf = scr("w2_bf", [NE * D, HM], BF16)
    wgT_scr = scr("wgT_scr", [D, H0], BF16)
    wuT_scr = scr("wuT_scr", [D, H0], BF16)
    wdT_scr = scr("wdT_scr", [H0, D], BF16)
    w1T_scr = scr("w1T_scr", [NE * D, HM], BF16)
    w2T_scr = scr("w2T_scr", [NE * HM, D], BF16)
    opw_bf = scr("opw_bf", [DI, D], BF16)
    cc_in = {}
    cc_out = {}
    for nm, cols in [("c1", 48), ("c2", 256), ("c3", 8), ("c4", 16), ("c5", 8)]:
        cc_in[nm] = scr("cin_" + nm, [128, cols])
        cc_out[nm] = scr("cout_" + nm, [256, cols])

    sreg = {nm: p.region(nm) for nm in list(scr_tensors)}
    sreg["out"] = p.region("out")

    def sb(name, shape, dt=F32):
        return stack.enter_context(nc.sbuf_tensor(name, shape, dt))[:]

    AF_t = sb("AFa", [128, 10240], F32)
    BF_t = sb("BFa", [128, 9216], F32)
    WA_t = sb("WAa", [128, 33280], BF16)
    PR_t = sb("PRa", [128, 1024], F32)
    PRB_t = sb("PRBa", [128, 128], BF16)
    psb = [stack.enter_context(nc.psum_tensor(f"pb{i}", [128, 512], F32))[:]
           for i in range(8)]
    rps = [p.region(f"ps{i}") for i in range(8)]

    af = Arena(p, AF_t, 10240, "af")
    bfa = Arena(p, BF_t, 9216, "bf")
    wa = Arena(p, WA_t, 33280, "wa")
    waf = ArenaTop(p, WA_t, 33280, "waf", wa)
    pers = Arena(p, PR_t, 1024, "pers")
    persb = Arena(p, PRB_t, 128, "persb")

    def phase_reset():
        p.barrier()
        af.reset()
        bfa.reset()
        wa.reset()
        waf.reset()

    def v_op(fn, reads, writes, tag=""):
        p.op("vector", fn, reads=reads, writes=writes, tag=tag)

    def s_op(fn, reads, writes, tag=""):
        p.op("scalar", fn, reads=reads, writes=writes, tag=tag)

    def t_op(fn, reads, writes, tag=""):
        p.op("tensor", fn, reads=reads, writes=writes, tag=tag)

    def g_op(fn, reads, writes, tag=""):
        p.op("gpsimd", fn, reads=reads, writes=writes, tag=tag)

    def mm_acc(ps_ap, pairs, reads, psreg, tag=""):
        def fn(e, pairs=tuple(pairs), ps_ap=ps_ap):
            last = None
            n = len(pairs)
            for i, (lh, rh) in enumerate(pairs):
                last = e.matmul(ps_ap, lh, rh, start=(i == 0), stop=(i == n - 1))
            return last
        t_op(fn, reads, [psreg], tag)

    def mm_banks(pairs_per_bank, reads, psregs, first, last_, tag=""):
        def fn(e, ppb=tuple(pairs_per_bank), first=first, last_=last_):
            last = None
            for (ps_ap, lh, rh) in ppb:
                last = e.matmul(ps_ap, lh, rh, start=first, stop=last_)
            return last
        t_op(fn, reads, psregs, tag)

    def pe_tr(ps_ap, in_ap, reads, psreg, tag="tr"):
        k = in_ap.shape[0]
        t_op(lambda e, ps_ap=ps_ap, in_ap=in_ap, k=k: e.transpose(
            ps_ap, in_ap, s_ident[0:k, 0:k]),
             list(reads) + [r_ident], [psreg], tag)

    def vcopy(out, in_, reads, writes, tag="cp"):
        v_op(lambda e, out=out, in_=in_: e.tensor_copy(out, in_), reads, writes, tag)

    def scopy(out, in_, reads, writes, scale=None, tag="scp"):
        if scale is None:
            s_op(lambda e, out=out, in_=in_: e.copy(out, in_), reads, writes, tag)
        else:
            s_op(lambda e, out=out, in_=in_, scale=scale: e.activation(
                out, in_, AF.Copy, scale=scale), reads, writes, tag)

    def tt(op, out, a, b, reads, writes, tag="tt"):
        v_op(lambda e, out=out, a=a, b=b, op=op: e.tensor_tensor(out, a, b, op),
             reads, writes, tag)

    def stt(out, in0, scalar, in1, op0, op1, reads, writes, tag="stt"):
        v_op(lambda e, out=out, in0=in0, scalar=scalar, in1=in1, op0=op0, op1=op1:
             e.scalar_tensor_tensor(out, in0, scalar, in1, op0, op1),
             reads, writes, tag)

    def tss(out, in_, scalar, op, reads, writes, tag="tss"):
        v_op(lambda e, out=out, in_=in_, scalar=scalar, op=op:
             e.tensor_single_scalar(out, in_, scalar, op), reads, writes, tag)

    def tsc2(out, in_, s1, s2, op0, op1, reads, writes, tag="ts2"):
        v_op(lambda e, out=out, in_=in_, s1=s1, s2=s2, op0=op0, op1=op1:
             e.tensor_scalar(out, in_, s1, s2, op0, op1), reads, writes, tag)

    def act(out, in_, func, reads, writes, bias=0.0, scale=1.0, tag="act"):
        s_op(lambda e, out=out, in_=in_, func=func, bias=bias, scale=scale:
             e.activation(out, in_, func, bias=bias, scale=scale),
             reads, writes, tag)

    def scan(out, a, b, init, reads, writes, tag="scan"):
        v_op(lambda e, out=out, a=a, b=b, init=init: e.tensor_tensor_scan(
            out, a, b, init, ALU.mult, ALU.add), reads, writes, tag)

    s_ident = pers.alloc("ident", 128)
    r_ident = p.region("ident")
    p.dma("sync", s_ident, ins["identity"], writes=[r_ident])
    s_parity = pers.alloc("parity", 1)
    r_parity = p.region("parity")
    p.dma("sync", s_parity, ins["parity"], writes=[r_parity])
    s_rs = pers.alloc("res_scale", 1)
    r_rs = p.region("res_scale")
    p.dma("sync", s_rs, ins["res_scale_col"], writes=[r_rs])

    def load_vec(name):
        src = ins[name]
        n = src.shape[1]
        tile = pers.alloc(name, n)
        reg = p.region(name)
        p.dma("sync", tile, src, writes=[reg])
        return tile, reg

    s_ln_g, r_ln_g = load_vec("L0_attn_ln_g")
    s_ln_b, r_ln_b = load_vec("L0_attn_ln_b")
    s_sq_b, r_sq_b = load_vec("L0_attn_sq_b")
    s_srw_b, r_srw_b = load_vec("L0_attn_srw_b")
    s_conv_b, r_conv_b = load_vec("L0_attn_conv_b")
    s_dt_b, r_dt_b = load_vec("L0_attn_dt_b")
    s_Dsk, r_Dsk = load_vec("L0_attn_D")
    s_ln1_w, r_ln1_w = load_vec("L1_attn_ln1_w")
    s_ln1_b, r_ln1_b = load_vec("L1_attn_ln1_b")
    s_ln2_w, r_ln2_w = load_vec("L1_attn_ln2_w")
    s_ln2_b, r_ln2_b = load_vec("L1_attn_ln2_b")
    s_td, r_td = load_vec("L1_attn_time_decay")
    s_tf, r_tf = load_vec("L1_attn_time_first")
    s_tmk, r_tmk = load_vec("L1_attn_time_mix_k")
    s_tmv, r_tmv = load_vec("L1_attn_time_mix_v")
    s_tmr, r_tmr = load_vec("L1_attn_time_mix_r")
    s_ftmk, r_ftmk = load_vec("L1_attn_f_time_mix_k")
    s_ftmr, r_ftmr = load_vec("L1_attn_f_time_mix_r")

    s_convw = pers.alloc("convw", NIC * 4)
    r_convw = p.region("convw")
    p.dma("sync", s_convw, ins["L0_attn_conv_w"], writes=[r_convw])
    s_A = pers.alloc("A", NIC * NST)
    r_A = p.region("A")
    p.dma("sync", s_A, ins["L0_attn_A_log"], writes=[r_A])
    act(s_A, s_A, AF.Exp, [r_A], [r_A], tag="Aexp")
    tss(s_A, s_A, -1.0, ALU.mult, [r_A], [r_A], tag="Aneg")
    s_negw = pers.alloc("negw", NDC)
    r_negw = p.region("negw")
    s_lam = pers.alloc("lam", NDC)
    r_lam = p.region("lam")
    s_Eu = pers.alloc("Eu", NDC)
    r_Eu = p.region("Eu")
    act(s_negw, s_td, AF.Exp, [r_td], [r_negw])
    tss(s_negw, s_negw, -1.0, ALU.mult, [r_negw], [r_negw])
    act(s_lam, s_negw, AF.Exp, [r_negw], [r_lam])
    act(s_Eu, s_tf, AF.Exp, [r_tf], [r_Eu])

    s_iorow = sb("s_iorow", [1, TC], F32)
    r_iorow = p.region("iorow")
    s_ones_row = sb("s_ones_row", [1, 128], F32)
    r_ones_row = p.region("ones_row")
    v_op(lambda e: e.memset(s_ones_row, 1.0), [], [r_ones_row])
    s_ones_col = sb("s_ones_col", [128, 1], F32)
    r_ones_col = p.region("ones_col")
    v_op(lambda e: e.memset(s_ones_col, 1.0), [], [r_ones_col])
    s_onesT = sb("s_onesT", [128, TC], F32)
    r_onesT = p.region("onesT")
    v_op(lambda e: e.memset(s_onesT, 1.0), [], [r_onesT])
    s_wrow = sb("s_wrow", [1, TC], F32)
    r_wrow = p.region("wrow")
    s_eps = sb("s_eps", [128, 1], F32)
    r_eps = p.region("eps")
    v_op(lambda e: e.memset(s_eps, EPS), [], [r_eps])
    s_i127 = sb("s_i127", [128, 1], F32)
    r_i127 = p.region("i127")
    v_op(lambda e: e.memset(s_i127, 1.0 / 127.0), [], [r_i127])

    s_hcar = sb("s_hcar", [128, NIC * NST], F32)
    r_hcar = p.region("hcar")
    s_Pcar = sb("s_Pcar", [128, NIC], F32)
    r_Pcar = p.region("Pcar")
    s_h0eff = sb("s_h0eff", [128, NIC * NST], F32)
    r_h0eff = p.region("h0eff")
    s_acar = sb("s_acar", [128, NDC], F32)
    r_acar = p.region("acar")
    s_bcar = sb("s_bcar", [128, NDC], F32)
    r_bcar = p.region("bcar")
    s_ab0 = sb("s_ab0", [128, 2 * NDC], F32)
    r_ab0 = p.region("ab0")
    s_cc = sb("s_cc", [128, 256], F32)
    r_cc = p.region("cc")
    v_op(lambda e: e.memset(s_hcar, 0.0), [], [r_hcar])
    v_op(lambda e: e.memset(s_Pcar, 0.0), [], [r_Pcar])
    v_op(lambda e: e.memset(s_acar, 0.0), [], [r_acar])
    v_op(lambda e: e.memset(s_bcar, 0.0), [], [r_bcar])

    s_routT = persb.alloc("routT", NDC * NE)
    r_routT = p.region("routT")
    s_rout_raw = sb("s_rout_raw", [8, D], F32)
    r_rout_raw = p.region("rout_raw")
    p.dma("sync", s_rout_raw, ins["L1_ffn_router_w"], writes=[r_rout_raw])
    for dc in range(NDC):
        t_op(lambda e, dc=dc: e.transpose(psb[7][0:128, 0:8],
                                          s_rout_raw[0:8, dc * 128:(dc + 1) * 128],
                                          s_ident[0:8, 0:8]),
             [r_rout_raw, r_ident], [rps[7]], "routT")
        vcopy(s_routT[:, dc * NE:(dc + 1) * NE], psb[7][0:128, 0:8],
              [rps[7]], [r_routT])

    s_prep = sb("s_prep", [128, 4096], BF16)
    r_prep = p.region("prep")
    s_prep2 = sb("s_prep2", [128, 4096], BF16)
    r_prep2 = [p.region("prep2a")]

    def prep_cast(src, dst, rows, cols, rdst, tag):
        if cols <= 4096:
            per = 4096 // cols
            for r0 in range(0, rows, 128 * per):
                nr = min(128 * per, rows - r0)
                npt = nr // 128
                view = s_prep[:, 0:npt * cols].rearrange(
                    "p (c d) -> p c d", d=cols)
                p.dma("gpsimd", view,
                      src[r0:r0 + nr, :].rearrange("(c p) d -> p c d", p=128),
                      reads=[], writes=[r_prep], tag=tag + "l")
                p.dma("gpsimd",
                      dst[r0:r0 + nr, :].rearrange("(c p) d -> p c d", p=128),
                      view, reads=[r_prep], writes=[rdst], tag=tag + "s")
        else:
            half = cols // 2
            for r0 in range(0, rows, 128):
                for c0 in (0, half):
                    view = s_prep[:, 0:half]
                    p.dma("gpsimd", view, src[r0:r0 + 128, c0:c0 + half],
                          reads=[], writes=[r_prep], tag=tag + "l")
                    p.dma("gpsimd", dst[r0:r0 + 128, c0:c0 + half], view,
                          reads=[r_prep], writes=[rdst], tag=tag + "s")

    prep_cast(ins["L0_ffn_w_gate"], wg_bf, H0, D, sreg["wg_bf"], "pg")
    prep_cast(ins["L0_ffn_w_up"], wu_bf, H0, D, sreg["wu_bf"], "pu")
    prep_cast(ins["L0_ffn_w_down"], wd_bf, D, H0, sreg["wd_bf"], "pd")
    prep_cast(ins["L0_attn_out_proj_w"], opw_bf, DI, D, sreg["opw_bf"], "pop")

    _prep_i = [0]

    def prep_transpose(src, rsrc, dst, rdst, rows, cols, tag):
        for c0 in range(0, cols, 128):
            for r0 in range(0, rows, 4096):
                nr = min(4096, rows - r0)
                i = 0
                view = s_prep2[:, 0:nr]
                p.dma("scalar", view, src[r0:r0 + nr, c0:c0 + 128],
                      reads=[rsrc], writes=[r_prep2[i]], tag=tag + "t",
                      transpose=True)
                p.dma("scalar", dst[c0:c0 + 128, r0:r0 + nr], view,
                      reads=[r_prep2[i]], writes=[rdst], tag=tag + "s")

    prep_transpose(wg_bf, sreg["wg_bf"], wgT_scr, sreg["wgT_scr"], H0, D, "tg")
    prep_transpose(wu_bf, sreg["wu_bf"], wuT_scr, sreg["wuT_scr"], H0, D, "tu")
    prep_transpose(wd_bf, sreg["wd_bf"], wdT_scr, sreg["wdT_scr"], D, H0, "td")

    def prep_moe_transpose():
        prep_cast(ins["L1_ffn_w1"], w1_bf, NE * HM, D, sreg["w1_bf"], "pw1")
        prep_cast(ins["L1_ffn_w2"], w2_bf, NE * D, HM, sreg["w2_bf"], "pw2")
        for e_ in range(NE):
            prep_transpose(w1_bf[e_ * HM:(e_ + 1) * HM, :], sreg["w1_bf"],
                           w1T_scr[e_ * D:(e_ + 1) * D, :], sreg["w1T_scr"],
                           HM, D, "t1")
            prep_transpose(w2_bf[e_ * D:(e_ + 1) * D, :], sreg["w2_bf"],
                           w2T_scr[e_ * HM:(e_ + 1) * HM, :], sreg["w2T_scr"],
                           D, HM, "t2")

    ln_sqt = [sb(f"ln_sqt{i}", [128, TC], F32) for i in range(4)]
    ln_rsqt = [p.region(f"ln_sqt{i}") for i in range(4)]
    r_lnsq = p.region("ln_sq_all")
    ln_st = sb("ln_st", [1, 4 * TC], F32)
    r_lst = p.region("ln_st")

    def emit_ln(x_tiles, rx, out_tiles, rout, g=None, b=None, rgb=(),
                banks=(6, 7)):
        bs, bb = psb[banks[0]], psb[banks[1]]
        rbs, rbb = rps[banks[0]], rps[banks[1]]
        rxl = list(rx)

        def msum(e, tiles=tuple(x_tiles), bs=bs):
            last = None
            for i, ti in enumerate(tiles):
                last = e.matmul(bs[0:1, :], s_ones_col, ti,
                                start=(i == 0), stop=(i == NDC - 1))
            return last
        t_op(msum, rxl + [r_ones_col], [rbs], "ln_sum")
        for d0 in range(0, NDC, 4):
            def sq4(e, d0=d0):
                last = None
                for j in range(4):
                    last = e.activation(ln_sqt[j], x_tiles[d0 + j], AF.Square)
                return last
            s_op(sq4, rxl, [r_lnsq], "ln_sq")
            def msq4(e, d0=d0):
                last = None
                for j in range(4):
                    last = e.matmul(bb[0:1, :], s_ones_col, ln_sqt[j],
                                    start=(d0 + j == 0),
                                    stop=(d0 + j == NDC - 1))
                return last
            t_op(msq4, [r_lnsq, r_ones_col], [rbb], "ln_sumsq")
        mean = ln_st[0:1, 0:TC]
        tmp = ln_st[0:1, TC:2 * TC]
        rstd = ln_st[0:1, 2 * TC:3 * TC]
        mr = ln_st[0:1, 3 * TC:4 * TC]
        tss(mean, bs[0:1, :], 1.0 / D, ALU.mult, [rbs], [r_lst], "ln_mean")
        tss(tmp, bb[0:1, :], 1.0 / D, ALU.mult, [rbb], [r_lst], "ln_ms")
        act(mr, mean, AF.Square, [r_lst], [r_lst], tag="ln_msq")
        tt(ALU.subtract, tmp, tmp, mr, [r_lst], [r_lst], "ln_var")
        act(tmp, tmp, AF.Sqrt, [r_lst, r_eps], [r_lst], bias=s_eps[0:1, 0:1],
            tag="ln_std")
        v_op(lambda e: e.reciprocal(rstd, tmp), [r_lst], [r_lst], "ln_rstd")
        tt(ALU.mult, mr, mean, rstd, [r_lst], [r_lst], "ln_mr")
        t_op(lambda e: e.matmul(bs, s_ones_row, rstd, start=True, stop=True),
             [r_lst, r_ones_row], [rbs], "ln_bc_r")
        t_op(lambda e: e.matmul(bb, s_ones_row, mr, start=True, stop=True),
             [r_lst, r_ones_row], [rbb], "ln_bc_m")
        for d0 in range(0, NDC, 4):
            def a14(e, d0=d0):
                last = None
                for j in range(4):
                    last = e.tensor_tensor(ln_sqt[j], x_tiles[d0 + j], bs,
                                           ALU.mult)
                return last
            v_op(a14, rxl + [rbs], [r_lnsq], "ln_a1")
            if g is None:
                def a24(e, d0=d0):
                    last = None
                    for j in range(4):
                        last = e.tensor_tensor(out_tiles[d0 + j], ln_sqt[j],
                                               bb, ALU.subtract)
                    return last
                v_op(a24, [r_lnsq, rbb], [rout], "ln_a2")
            else:
                def a24(e, d0=d0):
                    last = None
                    for j in range(4):
                        last = e.tensor_tensor(ln_sqt[j], ln_sqt[j], bb,
                                               ALU.subtract)
                    return last
                v_op(a24, [r_lnsq, rbb], [r_lnsq], "ln_a2")
                def af4(e, d0=d0):
                    last = None
                    for j in range(4):
                        last = e.tensor_scalar(out_tiles[d0 + j], ln_sqt[j],
                                               g[:, d0 + j:d0 + j + 1],
                                               b[:, d0 + j:d0 + j + 1],
                                               ALU.mult, ALU.add)
                    return last
                v_op(af4, [r_lnsq] + list(rgb), [rout], "ln_aff")

    def load_xtiles(src, rsrc, tiles, rt, t, q="sync", tag="xld"):
        for dc in range(NDC):
            p.dma(q, tiles[dc], src[dc * 128:(dc + 1) * 128, t * TC:(t + 1) * TC],
                  reads=[rsrc], writes=[rt], tag=tag)

    def load_shifted(src, rsrc, tiles, rt, t, bnd, r_bnd, tag="sld"):
        for dc in range(NDC):
            if t == 0:
                p.dma("sync", tiles[dc][:, 1:TC + 1],
                      src[dc * 128:(dc + 1) * 128, 0:TC],
                      reads=[rsrc], writes=[rt], tag=tag)
                vcopy(tiles[dc][:, 0:1], bnd[:, dc:dc + 1], [r_bnd], [rt],
                      tag + "b")
            else:
                p.dma("sync", tiles[dc],
                      src[dc * 128:(dc + 1) * 128, t * TC - 1:t * TC + TC],
                      reads=[rsrc], writes=[rt], tag=tag)

    def phase_P0():
        ing = [af.alloc(f"ing{i}", 128) for i in range(4)]
        r_ing = [p.region(f"ing{i}") for i in range(4)]
        xrow = [af.alloc(f"P0x{i}", TC) for i in range(2)]
        r_xrow = [p.region(f"P0x{i}") for i in range(2)]
        for t in range(NTC):
            for dc in range(NDC):
                xb_, rxb = xrow[dc % 2], r_xrow[dc % 2]
                for tb in range(TC // 128):
                    i = (dc * 4 + tb) % 4
                    bank = 4 + i % 2
                    p.dma("sync", ing[i],
                          ins["x_in"][t * TC + tb * 128:t * TC + (tb + 1) * 128,
                                      dc * 128:(dc + 1) * 128],
                          writes=[r_ing[i]], tag="P0ld")
                    pe_tr(psb[bank][:, 0:128], ing[i], [r_ing[i]], rps[bank])
                    vcopy(xb_[:, tb * 128:(tb + 1) * 128], psb[bank][:, 0:128],
                          [rps[bank]], [rxb])
                p.dma("sync", xfA[dc * 128:(dc + 1) * 128, t * TC:(t + 1) * TC],
                      xb_, reads=[rxb], writes=[sreg["xfA"]], tag="P0st")

    def phase_ffn0(src, rsrc, dst, rdst, pn):
        HHALF = H0 // 2
        NHH = NH0 // 2
        wsl = wa.alloc(pn + "w", NDC * HHALF)
        r_wsl = p.region(pn + "w")
        sgb = [wa.alloc(f"{pn}s{hc}", TC) for hc in range(NH0)]
        r_sgb = p.region(pn + "s")
        xn = [wa.alloc(f"{pn}xn{dc}", TC) for dc in range(NDC)]
        r_xn = p.region(pn + "xn")
        wdT = [wa.alloc(f"{pn}wd{i}", 2048) for i in range(2)]
        r_wdT = [p.region(f"{pn}wd{i}") for i in range(2)]
        xt = [af.alloc(f"{pn}x{dc}", TC) for dc in range(NDC)]
        r_xt = p.region(pn + "x")
        upev = [af.alloc(f"{pn}g{i}", TC) for i in range(2)]
        r_upev = [p.region(f"{pn}g{i}") for i in range(2)]
        xo = [bfa.alloc(f"{pn}o{i}", TC) for i in range(2)]
        r_xo = [p.region(f"{pn}o{i}") for i in range(2)]

        def load_half(w_scr, rw, half, tag):
            p.dma("scalar",
                  wsl.rearrange("p (c h) -> p c h", h=HHALF),
                  w_scr[:, half * HHALF:(half + 1) * HHALF].rearrange(
                      "(c p) h -> p c h", p=128),
                  reads=[rw], writes=[r_wsl], tag=tag)

        for t in range(NTC):
            load_xtiles(src, rsrc, xt, r_xt, t, tag=pn + "xld")
            emit_ln(xt, [r_xt], xn, r_xn)
            for half in range(2):
                load_half(wgT_scr, sreg["wgT_scr"], half, pn + "wg")
                for hj in range(0, NHH, 2):
                    hc = half * NHH + hj
                    b0 = 2 * ((hj // 2) % 2)
                    def s1(e, hj=hj, b0=b0):
                        last = None
                        for j in range(2):
                            for dc in range(NDC):
                                last = e.matmul(
                                    psb[b0 + j],
                                    wsl[:, dc * HHALF + (hj + j) * 128:
                                        dc * HHALF + (hj + j + 1) * 128],
                                    xn[dc], start=(dc == 0), stop=(dc == NDC - 1))
                        return last
                    t_op(s1, [r_wsl, r_xn], [rps[b0], rps[b0 + 1]], pn + "gmm")
                    def g1(e, hc=hc, b0=b0):
                        last = None
                        for j in range(2):
                            last = e.activation(sgb[hc + j], psb[b0 + j],
                                                AF.Silu, scale=s_i127[:, 0:1])
                        return last
                    s_op(g1, [rps[b0], rps[b0 + 1], r_i127], [r_sgb],
                         pn + "silu")
            for half in range(2):
                load_half(wuT_scr, sreg["wuT_scr"], half, pn + "wu")
                for hj in range(0, NHH, 2):
                    hc = half * NHH + hj
                    b0 = 2 * ((hj // 2) % 2)
                    def s1(e, hj=hj, b0=b0):
                        last = None
                        for j in range(2):
                            for dc in range(NDC):
                                last = e.matmul(
                                    psb[b0 + j],
                                    wsl[:, dc * HHALF + (hj + j) * 128:
                                        dc * HHALF + (hj + j + 1) * 128],
                                    xn[dc], start=(dc == 0), stop=(dc == NDC - 1))
                        return last
                    t_op(s1, [r_wsl, r_xn], [rps[b0], rps[b0 + 1]], pn + "umm")
                    def u1(e, b0=b0):
                        e.activation(upev[0], psb[b0], AF.Copy,
                                     scale=s_i127[:, 0:1])
                        return e.activation(upev[1], psb[b0 + 1], AF.Copy,
                                            scale=s_i127[:, 0:1])
                    s_op(u1, [rps[b0], rps[b0 + 1], r_i127],
                         [r_upev[0], r_upev[1]], pn + "ue")
                    def m1(e, hc=hc):
                        e.tensor_tensor(sgb[hc], upev[0], sgb[hc], ALU.mult)
                        return e.tensor_tensor(sgb[hc + 1], upev[1], sgb[hc + 1],
                                               ALU.mult)
                    v_op(m1, [r_upev[0], r_upev[1], r_sgb], [r_sgb], pn + "hm")
            for hc in range(0, NH0, 2):
                sl = (hc // 2) % 2
                p.dma("scalar",
                      wdT[sl].rearrange("p (c d) -> p c d", d=D),
                      wdT_scr[hc * 128:(hc + 2) * 128, :].rearrange(
                          "(c p) d -> p c d", p=128),
                      reads=[sreg["wdT_scr"]], writes=[r_wdT[sl]], tag=pn + "wd")
                def s2(e, hc=hc, sl=sl):
                    last = None
                    for j in range(2):
                        for dc in range(NDC):
                            last = e.matmul(
                                psb[dc],
                                wdT[sl][:, j * D + dc * 128:j * D + (dc + 1) * 128],
                                sgb[hc + j], start=(hc + j == 0),
                                stop=(hc + j == NH0 - 1))
                    return last
                t_op(s2, [r_wdT[sl], r_sgb], [rps[d] for d in range(NDC)],
                     pn + "dmm")
            for dc in range(NDC):
                i = dc % 2
                v_op(lambda e, dc=dc, i=i: e.scalar_tensor_tensor(
                    xo[i], psb[dc], 0.5 / 127.0, xt[dc], ALU.mult, ALU.add),
                    [rps[dc], r_xt], [r_xo[i]], pn + "res")
                p.dma("sync", dst[dc * 128:(dc + 1) * 128, t * TC:(t + 1) * TC],
                      xo[i], reads=[r_xo[i]], writes=[rdst], tag=pn + "st")

    def phase_P2a1(src, rsrc):
        sqw = [wa.alloc(f"A1q{dc}", D) for dc in range(NDC)]
        r_sqw = p.region("A1q")
        for dc in range(NDC):
            p.dma("gpsimd", sqw[dc],
                  ins["L0_attn_sq_w"][dc * 128:(dc + 1) * 128, :],
                  writes=[r_sqw], tag="sqw")
        h1b = [wa.alloc(f"A1hb{dc}", TC) for dc in range(NDC)]
        r_h1b = p.region("A1hb")
        ho = [wa.alloc(f"A1ho{i}", TC) for i in range(2)]
        r_ho = [p.region(f"A1ho{i}") for i in range(2)]
        xt = [af.alloc(f"A1x{dc}", TC) for dc in range(NDC)]
        r_xt = p.region("A1x")
        xa = [af.alloc(f"A1a{dc}", TC) for dc in range(NDC)]
        r_xa = p.region("A1a")
        h1 = [bfa.alloc(f"A1h{dc}", TC) for dc in range(NDC)]
        r_h1 = p.region("A1h")
        atmp = [af.alloc(f"A1t{i}", TC) for i in range(2)]
        r_atmp = p.region("A1t")

        for t in range(NTC):
            load_xtiles(src, rsrc, xt, r_xt, t, tag="A1ld")
            emit_ln(xt, [r_xt], xa, r_xa)
            emit_ln(xa, [r_xa], h1, r_h1, g=s_ln_g, b=s_ln_b,
                    rgb=[r_ln_g, r_ln_b])
            for dc in range(NDC):
                vcopy(h1b[dc], h1[dc], [r_h1], [r_h1b])
            for dc2 in range(NDC):
                bank = dc2 % 2
                mm_acc(psb[bank],
                       [(sqw[dc][:, dc2 * 128:(dc2 + 1) * 128], h1b[dc])
                        for dc in range(NDC)],
                       [r_sqw, r_h1b], rps[bank], "sqmm")
                i = dc2 % 2
                v_op(lambda e, dc2=dc2, bank=bank: e.scalar_tensor_tensor(
                    atmp[bank], psb[bank], s_sq_b[:, dc2:dc2 + 1], h1[dc2],
                    ALU.add, ALU.add),
                    [rps[bank], r_sq_b, r_h1], [r_atmp], "sqadd")
                vcopy(ho[i], atmp[bank], [r_atmp], [r_ho[i]])
                p.dma("sync", h_scr[dc2 * 128:(dc2 + 1) * 128,
                                      t * TC:(t + 1) * TC],
                      ho[i], reads=[r_ho[i]], writes=[sreg["h_scr"]],
                      tag="A1st")

    def phase_P2a2():
        ipw = [wa.alloc(f"A2w{dc}", DI) for dc in range(NDC)]
        r_ipw = p.region("A2w")
        hb = [wa.alloc(f"A2h{dc}", TC) for dc in range(NDC)]
        r_hb = p.region("A2h")
        uo = [af.alloc(f"A2u{i}", TC) for i in range(4)]
        r_uo = [p.region(f"A2u{i}") for i in range(4)]
        for half in range(2):
            for dc in range(NDC):
                p.dma("gpsimd", ipw[dc],
                      ins["L0_attn_in_proj_w"][dc * 128:(dc + 1) * 128,
                                               half * DI:(half + 1) * DI],
                      writes=[r_ipw], tag="ipw")
            for t in range(NTC):
                load_xtiles(h_scr, sreg["h_scr"], hb, r_hb, t, tag="A2hld")
                for oc in range(NIC):
                    i = oc % 4
                    bank = oc % 4
                    mm_acc(psb[bank],
                           [(ipw[dc][:, oc * 128:(oc + 1) * 128], hb[dc])
                            for dc in range(NDC)],
                           [r_ipw, r_hb], rps[bank], "ipmm")
                    vcopy(uo[i], psb[bank], [rps[bank]], [r_uo[i]])
                    if half == 0:
                        dap = u_pre[oc * 128:(oc + 1) * 128,
                                    t * TC:(t + 1) * TC]
                        rd = sreg["u_pre"]
                        if t == NTC - 1:
                            vcopy(s_tail[:, oc * 3:(oc + 1) * 3],
                                  uo[i][:, TC - 3:TC], [r_uo[i]], [r_tail],
                                  "A2tl")
                    else:
                        dap = z_scr[oc * 128:(oc + 1) * 128, t * TC:(t + 1) * TC]
                        rd = sreg["z_scr"]
                    p.dma("sync", dap, uo[i], reads=[r_uo[i]], writes=[rd],
                          tag="A2st")

    def phase_C1():
        p.dma("sync", cc_in["c1"], s_tail, reads=[r_tail],
              writes=[sreg["cin_c1"]], tag="c1a")
        p.collective("AllGather", ALU.bypass, [cc_in["c1"]], [cc_out["c1"]],
                     reads=[sreg["cin_c1"]], writes=[sreg["cout_c1"]])
        p.dma("sync", s_cc[:, 0:48], cc_out["c1"][0:128, :],
              reads=[sreg["cout_c1"]], writes=[r_cc], tag="c1l")
        tss(s_halo[:, 0:48], s_cc[:, 0:48], s_parity[:, 0:1], ALU.mult,
            [r_cc, r_parity], [r_halo], "c1m")

    def phase_P2b_real():
        delta = [af.alloc(f"Bd{ic}", TC) for ic in range(NIC)]
        r_dl = p.region("Bd")
        Pt = [af.alloc(f"BP{i}", TC) for i in range(2)]
        r_Pt = [p.region(f"BP{i}") for i in range(2)]
        spl = [af.alloc(f"Bs{i}", TC) for i in range(2)]
        r_spl = [p.region(f"Bs{i}") for i in range(2)]
        du = [bfa.alloc(f"Bu{ic}", TC) for ic in range(NIC)]
        r_du = p.region("Bu")
        Bb = bfa.alloc("Bbb", TC)
        r_Bb = p.region("Bbb")
        Cb = bfa.alloc("Bcb", TC)
        r_Cb = p.region("Bcb")
        yac = [waf.alloc(f"By{ic}", TC) for ic in range(NIC)]
        r_y = p.region("By")
        upt = [waf.alloc(f"Bt{i}", TC + 3) for i in range(2)]
        r_upt = [p.region(f"Bt{i}") for i in range(2)]
        tA = [waf.alloc(f"BA{i}", TC) for i in range(6)]
        r_tA = [p.region("BAa"), p.region("BAb")]
        tB = [waf.alloc(f"BB{i}", TC) for i in range(6)]
        r_tB = [p.region("BBa"), p.region("BBb")]
        xprj = [s_xprj[:, ic * 96:(ic + 1) * 96] for ic in range(NIC)]
        r_xprj = p.region("Bxp")
        dtw = s_dtw
        r_dtw = p.region("Bdtw")
        dtv_t = waf.alloc("Bdtv", TC)
        dtv = dtv_t[0:64, :]
        r_dtv = p.region("Bdtv")
        bc16_t = waf.alloc("Bbc16", TC)
        bc16 = bc16_t[0:16, :]
        r_bc16 = p.region("Bbc16")

        for ic in range(NIC):
            p.dma("sync", xprj[ic],
                  ins["L0_attn_x_proj_w"][ic * 128:(ic + 1) * 128, :],
                  writes=[r_xprj], tag="xprj")
        p.dma("sync", dtw, ins["L0_attn_dt_w"], writes=[r_dtw], tag="dtw")

        for t in range(NTC):
            for ic in range(NIC):
                iu = ic % 2
                ub, rub = upt[iu], r_upt[iu]
                if t == 0:
                    p.dma("sync", ub[:, 3:TC + 3],
                          u_pre[ic * 128:(ic + 1) * 128, 0:TC],
                          reads=[sreg["u_pre"]], writes=[rub], tag="Buld")
                    vcopy(ub[:, 0:3], s_halo[:, ic * 3:(ic + 1) * 3],
                          [r_halo], [rub], "Bhal")
                else:
                    p.dma("sync", ub,
                          u_pre[ic * 128:(ic + 1) * 128,
                                t * TC - 3:t * TC + TC],
                          reads=[sreg["u_pre"]], writes=[rub], tag="Buld")
                cv = tA[iu * 3]
                rcv = r_tA[iu]
                tss(cv, ub[:, 0:TC], s_convw[:, ic * 4:ic * 4 + 1], ALU.mult,
                    [rub, r_convw], [rcv], "cv0")
                for j in range(1, 4):
                    v_op(lambda e, ub=ub, cv=cv, ic=ic, j=j: e.scalar_tensor_tensor(
                        cv, ub[:, j:j + TC], s_convw[:, ic * 4 + j:ic * 4 + j + 1],
                        cv, ALU.mult, ALU.add),
                        [rub, r_convw, rcv], [rcv], "cvj")
                s_op(lambda e, ic=ic, cv=cv: e.activation(
                    du[ic], cv, AF.Silu, bias=s_conv_b[:, ic:ic + 1]),
                    [rcv, r_conv_b], [r_du], "usilu")
                tss(yac[ic], du[ic], s_Dsk[:, ic:ic + 1], ALU.mult,
                    [r_du, r_Dsk], [r_y], "yD")
            mm_acc(psb[0][0:64, :], [(xprj[ic][:, 0:64], du[ic])
                                     for ic in range(NIC)],
                   [r_xprj, r_du], rps[0], "xdt")
            mm_acc(psb[1][0:16, :], [(xprj[ic][:, 64:80], du[ic])
                                     for ic in range(NIC)],
                   [r_xprj, r_du], rps[1], "xB")
            mm_acc(psb[2][0:16, :], [(xprj[ic][:, 80:96], du[ic])
                                     for ic in range(NIC)],
                   [r_xprj, r_du], rps[2], "xC")
            vcopy(dtv, psb[0][0:64, :], [rps[0]], [r_dtv])
            vcopy(bc16, psb[1][0:16, :], [rps[1]], [r_bc16])
            p.dma("sync", B_scr[:, t * TC:(t + 1) * TC], bc16,
                  reads=[r_bc16], writes=[sreg["B_scr"]], tag="Bst")
            scopy(bc16, psb[2][0:16, :], [rps[2]], [r_bc16], tag="Cev")
            p.dma("sync", C_scr[:, t * TC:(t + 1) * TC], bc16,
                  reads=[r_bc16], writes=[sreg["C_scr"]], tag="Cst")
            for ic in range(NIC):
                bank = 3 + ic % 2
                t_op(lambda e, ic=ic, bank=bank: e.matmul(
                    psb[bank], dtw[:, ic * 128:(ic + 1) * 128], dtv,
                    start=True, stop=True),
                    [r_dtw, r_dtv], [rps[bank]], "dtmm")
                sa_, sb_ = spl[0], spl[1]
                s_op(lambda e, ic=ic, bank=bank, sa_=sa_: e.activation(
                    sa_, psb[bank], AF.Abs, bias=s_dt_b[:, ic:ic + 1]),
                    [rps[bank], r_dt_b], [r_spl[0]], "sp_abs")
                act(sa_, sa_, AF.Exp, [r_spl[0]], [r_spl[0]], scale=-1.0,
                    tag="sp_exp")
                act(sa_, sa_, AF.Ln, [r_spl[0]], [r_spl[0]], bias=1.0,
                    tag="sp_ln")
                s_op(lambda e, ic=ic, bank=bank, sb_=sb_: e.activation(
                    sb_, psb[bank], AF.Relu, bias=s_dt_b[:, ic:ic + 1]),
                    [rps[bank], r_dt_b], [r_spl[1]], "sp_relu")
                tt(ALU.add, delta[ic], sa_, sb_, [r_spl[0], r_spl[1]], [r_dl],
                   "sp_add")
                pt = Pt[ic % 2]
                rpt = r_Pt[ic % 2]
                scan(pt, s_onesT, delta[ic], s_Pcar[:, ic:ic + 1],
                     [r_onesT, r_dl, r_Pcar], [rpt], "Pscan")
                vcopy(s_Pcar[:, ic:ic + 1], pt[:, TC - 1:TC], [rpt], [r_Pcar],
                      "Pcar")
                p.dma("sync", P_scr[ic * 128:(ic + 1) * 128,
                                      t * TC:(t + 1) * TC],
                      pt, reads=[rpt], writes=[sreg["P_scr"]], tag="Pst")
                tt(ALU.mult, du[ic], delta[ic], du[ic], [r_dl, r_du], [r_du],
                   "du")
            for n in range(NST):
                p.dma("sync", Bb,
                      B_scr[n:n + 1, t * TC:(t + 1) * TC].partition_broadcast(128),
                      reads=[sreg["B_scr"]], writes=[r_Bb], tag="Bbc")
                p.dma("sync", Cb,
                      C_scr[n:n + 1, t * TC:(t + 1) * TC].partition_broadcast(128),
                      reads=[sreg["C_scr"]], writes=[r_Cb], tag="Cbc")
                groups = [(0, 3), (3, 3), (6, 3), (9, 3), (12, 3), (15, 1)]
                for gi, (ic0, gn) in enumerate(groups):
                    st_ = gi % 2
                    A = tA[st_ * 3:st_ * 3 + gn]
                    B = tB[st_ * 3:st_ * 3 + gn]
                    rA, rB = r_tA[st_], r_tB[st_]
                    def dAg(e, ic0=ic0, gn=gn, n=n, A=tuple(A)):
                        last = None
                        for j in range(gn):
                            col = (ic0 + j) * NST + n
                            last = e.activation(A[j], delta[ic0 + j], AF.Exp,
                                                scale=s_A[:, col:col + 1])
                        return last
                    s_op(dAg, [r_dl, r_A], [rA], "dA")
                    def dBg(e, ic0=ic0, gn=gn, B=tuple(B)):
                        last = None
                        for j in range(gn):
                            last = e.tensor_tensor(B[j], du[ic0 + j], Bb,
                                                   ALU.mult)
                        return last
                    g_op(dBg, [r_du, r_Bb], [rB], "dBu")
                    def scg(e, ic0=ic0, gn=gn, n=n, A=tuple(A), B=tuple(B)):
                        last = None
                        for j in range(gn):
                            col = (ic0 + j) * NST + n
                            last = e.tensor_tensor_scan(
                                A[j], A[j], B[j], s_hcar[:, col:col + 1],
                                ALU.mult, ALU.add)
                        for j in range(gn):
                            col = (ic0 + j) * NST + n
                            last = e.tensor_copy(s_hcar[:, col:col + 1],
                                                 A[j][:, TC - 1:TC])
                        return last
                    v_op(scg, [rA, rB, r_hcar], [rA, r_hcar], "hscan")
                    def hCg(e, gn=gn, A=tuple(A), B=tuple(B)):
                        last = None
                        for j in range(gn):
                            last = e.tensor_tensor(B[j], A[j], Cb, ALU.mult)
                        return last
                    v_op(hCg, [rA, r_Cb], [rB], "hC")
                    def yag(e, ic0=ic0, gn=gn, B=tuple(B)):
                        last = None
                        for j in range(gn):
                            last = e.tensor_tensor(yac[ic0 + j], yac[ic0 + j],
                                                   B[j], ALU.add)
                        return last
                    g_op(yag, [rB, r_y], [r_y], "yadd")
            for ic in range(NIC):
                p.dma("sync", y_scr[ic * 128:(ic + 1) * 128,
                                      t * TC:(t + 1) * TC],
                      yac[ic], reads=[r_y], writes=[sreg["y_scr"]], tag="yst")

    def phase_C2():
        p.dma("gpsimd", cc_in["c2"], s_hcar, reads=[r_hcar],
              writes=[sreg["cin_c2"]], tag="c2a")
        p.collective("AllGather", ALU.bypass, [cc_in["c2"]], [cc_out["c2"]],
                     reads=[sreg["cin_c2"]], writes=[sreg["cout_c2"]])
        p.dma("sync", s_h0eff, cc_out["c2"][0:128, :],
              reads=[sreg["cout_c2"]], writes=[r_h0eff], tag="c2l")
        tss(s_h0eff, s_h0eff, s_parity[:, 0:1], ALU.mult,
            [r_h0eff, r_parity], [r_h0eff], "c2m")

    s_xprj = sb("s_xprj", [128, NIC * 96], F32)
    s_dtw = sb("s_dtw", [64, DI], F32)
    s_tail = sb("s_tail", [128, 48], F32)
    r_tail = p.region("tail")
    s_halo = sb("s_halo", [128, 48], F32)
    r_halo = p.region("halo")
    s_bcc = sb("s_bcc", [128, 8], F32)
    r_bcc = p.region("bcc")
    s_bnd1 = sb("s_bnd1", [128, 8], F32)
    r_bnd1 = p.region("bnd1")
    s_bnd2 = sb("s_bnd2", [128, 8], F32)
    r_bnd2 = p.region("bnd2")

    def phase_P2c(src, rsrc, dst, rdst):
        Ptl = [af.alloc(f"Cp{ic}", TC) for ic in range(NIC)]
        r_P = p.region("Cp")
        tE = [af.alloc(f"Ce{i}", TC) for i in range(4)]
        r_tE = p.region("Ce")
        ytl = [bfa.alloc(f"Cy{ic}", TC) for ic in range(NIC)]
        r_y = p.region("Cy")
        zt = [bfa.alloc(f"Cz{i}", TC) for i in range(2)]
        r_zt = [p.region(f"Cz{i}") for i in range(2)]
        opw = [wa.alloc(f"Do{i}", NIC * 128) for i in range(2)]
        r_opw = [p.region(f"Do{i}") for i in range(2)]
        srw = [wa.alloc(f"Dw{dc}", D) for dc in range(NDC)]
        r_srw = p.region("Dw")
        ygb = [wa.alloc(f"Dg{ic}", TC) for ic in range(NIC)]
        r_ygb = p.region("Dg")
        tM = [waf.alloc(f"Cm{i}", TC) for i in range(8)]
        r_tM = [p.region("Cma"), p.region("Cmb")]
        Cb = waf.alloc("Ccb", TC)
        r_Cb = p.region("Ccb")
        gtmp = [waf.alloc(f"Dt{i}", TC) for i in range(2)]
        r_gtmp = p.region("Dt")
        ybf = Ptl[0:NDC]
        r_ybf = r_P
        xt2 = ytl[0:NDC]
        r_xt2 = r_y
        xa2 = ytl[NDC:2 * NDC]
        r_xa2 = r_y
        xo = [tE[2], tE[3]]
        r_xo = [r_tE, r_tE]

        for dc in range(NDC):
            p.dma("gpsimd", srw[dc],
                  ins["L0_attn_srw_w"][dc * 128:(dc + 1) * 128, :],
                  writes=[r_srw], tag="srw")

        for t in range(NTC):
            for ic in range(NIC):
                p.dma("sync", Ptl[ic],
                      P_scr[ic * 128:(ic + 1) * 128, t * TC:(t + 1) * TC],
                      reads=[sreg["P_scr"]], writes=[r_P], tag="CPld")
                p.dma("sync", ytl[ic],
                      y_scr[ic * 128:(ic + 1) * 128, t * TC:(t + 1) * TC],
                      reads=[sreg["y_scr"]], writes=[r_y], tag="Cyld")
            for n in range(NST):
                p.dma("sync", Cb,
                      C_scr[n:n + 1, t * TC:(t + 1) * TC].partition_broadcast(128),
                      reads=[sreg["C_scr"]], writes=[r_Cb], tag="CCbc")
                for gi, ic0 in enumerate(range(0, NIC, 4)):
                    st_ = gi % 2
                    M = tM[st_ * 4:st_ * 4 + 4]
                    rM = r_tM[st_]
                    def ce4(e, ic0=ic0, n=n):
                        last = None
                        for j in range(4):
                            col = (ic0 + j) * NST + n
                            last = e.activation(tE[j], Ptl[ic0 + j], AF.Exp,
                                                scale=s_A[:, col:col + 1])
                        return last
                    s_op(ce4, [r_P, r_A], [r_tE], "Cexp")
                    def cm4(e, ic0=ic0, n=n, M=tuple(M)):
                        last = None
                        for j in range(4):
                            col = (ic0 + j) * NST + n
                            last = e.scalar_tensor_tensor(
                                M[j], tE[j], s_h0eff[:, col:col + 1], Cb,
                                ALU.mult, ALU.mult)
                        return last
                    v_op(cm4, [r_tE, r_h0eff, r_Cb], [rM], "Cm")
                    if gi % 2 == 0:
                        def ca4(e, ic0=ic0, M=tuple(M)):
                            last = None
                            for j in range(4):
                                last = e.tensor_tensor(ytl[ic0 + j],
                                                       ytl[ic0 + j], M[j],
                                                       ALU.add)
                            return last
                        g_op(ca4, [rM, r_y], [r_y], "Cadd")
                    else:
                        def ca4(e, ic0=ic0, M=tuple(M)):
                            last = None
                            for j in range(4):
                                last = e.tensor_tensor(ytl[ic0 + j],
                                                       ytl[ic0 + j], M[j],
                                                       ALU.add)
                            return last
                        v_op(ca4, [rM, r_y], [r_y], "Cadd")
            for ic in range(NIC):
                i = ic % 2
                p.dma("sync", zt[i],
                      z_scr[ic * 128:(ic + 1) * 128, t * TC:(t + 1) * TC],
                      reads=[sreg["z_scr"]], writes=[r_zt[i]], tag="Czld")
                act(tE[i], zt[i], AF.Silu, [r_zt[i]], [r_tE], tag="Csz")
                tt(ALU.mult, ygb[ic], ytl[ic], tE[i], [r_y, r_tE], [r_ygb],
                   "Cyg")
            for dc in range(NDC):
                sl = dc % 2
                p.dma("scalar",
                      opw[sl].rearrange("p (c q) -> p c q", q=128),
                      opw_bf[:, dc * 128:(dc + 1) * 128]
                      .rearrange("(c p) q -> p c q", p=128),
                      reads=[sreg["opw_bf"]], writes=[r_opw[sl]], tag="Dow")
                bank = dc % 4
                mm_acc(psb[bank],
                       [(opw[sl][:, ic * 128:(ic + 1) * 128], ygb[ic])
                        for ic in range(NIC)],
                       [r_opw[sl], r_ygb], rps[bank], "Dop")
                scopy(ybf[dc], psb[bank], [rps[bank]], [r_ybf], tag="Dybe")
            def ybbc(e):
                last = None
                for dc in range(NDC):
                    last = e.tensor_copy(ygb[dc], ybf[dc])
                return last
            v_op(ybbc, [r_ybf, r_ygb], [r_ygb], "Dybb")
            load_xtiles(src, rsrc, xt2, r_xt2, t, tag="Dxld")
            emit_ln(xt2, [r_xt2], xa2, r_xa2)
            for dc2 in range(NDC):
                bank = 4 + dc2 % 2
                mm_acc(psb[bank],
                       [(srw[dc][:, dc2 * 128:(dc2 + 1) * 128], ygb[dc])
                        for dc in range(NDC)],
                       [r_srw, r_ygb], rps[bank], "Dsrw")
                tsc2(gtmp[dc2 % 2], psb[bank], s_srw_b[:, dc2:dc2 + 1], None,
                     ALU.add, ALU.bypass, [rps[bank], r_srw_b],
                     [r_gtmp], "Dgb")
                tt(ALU.mult, gtmp[dc2 % 2], ybf[dc2], gtmp[dc2 % 2],
                   [r_ybf, r_gtmp], [r_gtmp], "Dy2")
                xoo, rxo = xo[dc2 % 2], r_xo[dc2 % 2]
                tt(ALU.add, xoo, xt2[dc2], xa2[dc2], [r_xt2, r_xa2], [rxo],
                   "Dxa")
                v_op(lambda e, dc2=dc2, xoo=xoo: e.scalar_tensor_tensor(
                    xoo, gtmp[dc2 % 2], s_rs[:, 0:1], xoo, ALU.mult, ALU.add),
                    [r_gtmp, r_rs, rxo], [rxo], "Dres")
                p.dma("sync", dst[dc2 * 128:(dc2 + 1) * 128,
                                  t * TC:(t + 1) * TC],
                      xoo, reads=[rxo], writes=[rdst], tag="Dst")

    def phase_moepre(src, rsrc, pn):
        xn = [wa.alloc(f"{pn}n{dc}", TC) for dc in range(NDC)]
        r_xn = p.region(pn + "n")
        xt = [af.alloc(f"{pn}x{dc}", TC) for dc in range(NDC)]
        r_xt = p.region(pn + "x")
        rr = af.alloc(pn + "rr", TC)
        r_rr = p.region(pn + "rr")
        lt = af.alloc(pn + "lt", 8)
        r_lt = p.region(pn + "lt")
        st8 = af.alloc(pn + "s8", 8)
        r_st8 = p.region(pn + "s8")
        w8 = af.alloc(pn + "w8", 8)
        r_w8 = p.region(pn + "w8")
        stat = af.alloc(pn + "st", 4)
        r_stat = p.region(pn + "st")

        for t in range(NTC):
            load_xtiles(src, rsrc, xt, r_xt, t, tag=pn + "xld")
            emit_ln(xt, [r_xt], xn, r_xn)
            for dc in range(NDC):
                p.dma("sync", xnm_scr[dc * 128:(dc + 1) * 128,
                                        t * TC:(t + 1) * TC],
                      xn[dc], reads=[r_xn], writes=[sreg["xnm_scr"]],
                      tag=pn + "nst")
            def mr_(e):
                last = None
                for dc in range(NDC):
                    last = e.matmul(psb[2][0:8, :],
                                    s_routT[:, dc * NE:(dc + 1) * NE], xn[dc],
                                    start=(dc == 0), stop=(dc == NDC - 1))
                return last
            t_op(mr_, [r_routT, r_xn], [rps[2]], pn + "rt")
            vcopy(rr[0:8, :], psb[2][0:8, :], [rps[2]], [r_rr])
            for tb in range(TC // 128):
                pe_tr(psb[3][0:128, 0:8], rr[0:8, tb * 128:(tb + 1) * 128],
                      [r_rr], rps[3])
                vcopy(lt, psb[3][0:128, 0:8], [rps[3]], [r_lt])
                m1n = stat[:, 0:1]
                ssum = stat[:, 1:2]
                m2 = stat[:, 2:3]
                v_op(lambda e, m1n=m1n: e.reduce_max(m1n, lt, AX.X, negate=True),
                     [r_lt], [r_stat], "m1")
                s_op(lambda e, m1n=m1n: e.activation(st8, lt, AF.Exp,
                                                     bias=m1n[:, 0:1]),
                     [r_lt, r_stat], [r_st8], "smexp")
                v_op(lambda e, ssum=ssum: e.reduce_sum(ssum, st8, AX.X),
                     [r_st8], [r_stat], "ssum")
                v_op(lambda e, ssum=ssum: e.reciprocal(ssum, ssum),
                     [r_stat], [r_stat], "rsum")
                tss(st8, st8, ssum[:, 0:1], ALU.mult, [r_st8, r_stat], [r_st8],
                    "probs")
                v_op(lambda e, m1n=m1n: e.reduce_max(m1n, st8, AX.X, negate=True),
                     [r_st8], [r_stat], "m1p")
                tss(m1n, m1n, -1.0, ALU.mult, [r_stat], [r_stat], "m1pp")
                tss(w8, st8, m1n[:, 0:1], ALU.is_ge, [r_st8, r_stat], [r_w8],
                    "mask1")
                v_op(lambda e: e.scalar_tensor_tensor(
                    w8, w8, -1e9, st8, ALU.mult, ALU.add),
                    [r_w8, r_st8], [r_w8], "s4")
                v_op(lambda e, m2=m2: e.reduce_max(m2, w8, AX.X),
                     [r_w8], [r_stat], "m2")
                tss(w8, st8, m2[:, 0:1], ALU.is_ge, [r_st8, r_stat], [r_w8],
                    "mask2")
                tt(ALU.mult, w8, w8, st8, [r_w8, r_st8], [r_w8], "wsel")
                tss(w8, w8, 0.5, ALU.mult, [r_w8], [r_w8], "whalf")
                pe_tr(psb[3][0:8, 0:128], w8, [r_w8], rps[3])
                vcopy(rr[0:8, tb * 128:(tb + 1) * 128], psb[3][0:8, 0:128],
                      [rps[3]], [r_rr])
            p.dma("sync", gate_scr[:, t * TC:(t + 1) * TC], rr[0:8, :],
                  reads=[r_rr], writes=[sreg["gate_scr"]], tag=pn + "gst")

    def phase_moe(src, rsrc, dst, rdst, pn):
        HMH = HM // 2
        NHH = NHM // 2
        w1T = wa.alloc(pn + "1", NDC * HMH)
        r_w1T = p.region(pn + "1")
        hp = [wa.alloc(f"{pn}h{hh}", TC) for hh in range(NHH)]
        r_hp = p.region(pn + "h")
        xn = [wa.alloc(f"{pn}n{dc}", TC) for dc in range(NDC)]
        r_xn = p.region(pn + "n")
        w2T = [wa.alloc(f"{pn}2{i}", 2048) for i in range(2)]
        r_w2T = [p.region(f"{pn}2{i}") for i in range(2)]
        osb = [af.alloc(f"{pn}o{dc}", TC) for dc in range(NDC)]
        r_osb = p.region(pn + "o")
        gl = [af.alloc(f"{pn}g{i}", TC) for i in range(2)]
        r_gl = [p.region(f"{pn}g{i}") for i in range(2)]
        wb = af.alloc(pn + "wb", TC)
        r_wb = p.region(pn + "wb")
        xt = [bfa.alloc(f"{pn}x{i}", TC) for i in range(2)]
        r_xtl = [p.region(f"{pn}x{i}") for i in range(2)]
        xo = [bfa.alloc(f"{pn}xo{i}", TC) for i in range(2)]
        r_xo = [p.region(f"{pn}xo{i}") for i in range(2)]

        for sup in range(NTC):
            t0 = sup * TC
            def oz(e):
                last = None
                for dc in range(NDC):
                    last = e.memset(osb[dc], 0.0)
                return last
            v_op(oz, [r_osb], [r_osb], pn + "oz")
            for dc in range(NDC):
                p.dma("sync", xn[dc],
                      xnm_scr[dc * 128:(dc + 1) * 128, t0:t0 + TC],
                      reads=[sreg["xnm_scr"]], writes=[r_xn], tag=pn + "nld")
            for e_ in range(NE):
                p.dma("sync", s_wrow, gate_scr[e_:e_ + 1, t0:t0 + TC],
                      reads=[sreg["gate_scr"]], writes=[r_wrow], tag=pn + "wr")
                t_op(lambda e: e.matmul(psb[6], s_ones_row, s_wrow,
                                        start=True, stop=True),
                     [r_wrow, r_ones_row], [rps[6]], pn + "wbc")
                vcopy(wb, psb[6], [rps[6]], [r_wb])
                for half in range(2):
                    for q in range(2):
                        p.dma("scalar",
                              w1T[:, q * NDC * 1024:(q + 1) * NDC * 1024]
                              .rearrange("p (c h) -> p c h", h=1024),
                              w1T_scr[e_ * D:(e_ + 1) * D,
                                      half * HMH + q * 1024:
                                      half * HMH + (q + 1) * 1024]
                              .rearrange("(c p) h -> p c h", p=128),
                              reads=[sreg["w1T_scr"]], writes=[r_w1T],
                              tag=pn + "w1")
                    for hj in range(0, NHH, 2):
                        q, hq = divmod(hj, 8)
                        b0 = 2 * ((hj // 2) % 2)
                        def s1(e, q=q, hq=hq, b0=b0):
                            last = None
                            for j in range(2):
                                for dc in range(NDC):
                                    base = q * NDC * 1024 + dc * 1024
                                    last = e.matmul(
                                        psb[b0 + j],
                                        w1T[:, base + (hq + j) * 128:
                                            base + (hq + j + 1) * 128],
                                        xn[dc], start=(dc == 0),
                                        stop=(dc == NDC - 1))
                            return last
                        t_op(s1, [r_w1T, r_xn], [rps[b0], rps[b0 + 1]],
                             pn + "s1")
                        def g1(e, b0=b0):
                            last = None
                            for j in range(2):
                                last = e.activation(gl[j], psb[b0 + j],
                                                    AF.Gelu)
                            return last
                        s_op(g1, [rps[b0], rps[b0 + 1]], [r_gl[0], r_gl[1]],
                             pn + "gel")
                        def m1(e, hj=hj):
                            e.tensor_tensor(hp[hj], gl[0], wb, ALU.mult)
                            return e.tensor_tensor(hp[hj + 1], gl[1], wb,
                                                   ALU.mult)
                        v_op(m1, [r_gl[0], r_gl[1], r_wb], [r_hp], pn + "gw")
                    for hj in range(0, NHH, 2):
                        hc = half * NHH + hj
                        sl = (hj // 2) % 2
                        p.dma("scalar",
                              w2T[sl].rearrange("p (c d) -> p c d", d=D),
                              w2T_scr[e_ * HM + hc * 128:
                                      e_ * HM + (hc + 2) * 128, :]
                              .rearrange("(c p) d -> p c d", p=128),
                              reads=[sreg["w2T_scr"]], writes=[r_w2T[sl]],
                              tag=pn + "w2")
                        def s2(e, hj=hj, sl=sl):
                            last = None
                            for j in range(2):
                                for dc in range(NDC):
                                    last = e.matmul(
                                        psb[dc],
                                        w2T[sl][:, j * D + dc * 128:
                                                j * D + (dc + 1) * 128],
                                        hp[hj + j], start=(hj + j == 0),
                                        stop=(hj + j == NHH - 1))
                            return last
                        t_op(s2, [r_w2T[sl], r_hp],
                             [rps[d] for d in range(NDC)], pn + "s2")
                    for d0 in range(0, NDC, 4):
                        def oa4(e, d0=d0):
                            last = None
                            for j in range(4):
                                last = e.scalar_tensor_tensor(
                                    osb[d0 + j], psb[d0 + j], 1.0,
                                    osb[d0 + j], ALU.mult, ALU.add)
                            return last
                        v_op(oa4, [rps[d] for d in range(d0, d0 + 4)] + [r_osb],
                             [r_osb], pn + "oa")
            for dc in range(NDC):
                i = dc % 2
                p.dma("sync", xt[i], src[dc * 128:(dc + 1) * 128, t0:t0 + TC],
                      reads=[rsrc], writes=[r_xtl[i]], tag=pn + "xld")
                tt(ALU.add, xo[i], xt[i], osb[dc], [r_xtl[i], r_osb],
                   [r_xo[i]], pn + "res")
                p.dma("sync", dst[dc * 128:(dc + 1) * 128, t0:t0 + TC],
                      xo[i], reads=[r_xo[i]], writes=[rdst], tag=pn + "st")

    def phase_r1(src, rsrc):
        xt = [af.alloc(f"R1x{dc}", TC) for dc in range(NDC)]
        r_xt = p.region("R1x")
        xa = [af.alloc(f"R1a{dc}", TC) for dc in range(NDC)]
        r_xa = p.region("R1a")
        xn1 = [bfa.alloc(f"R1n{dc}", TC) for dc in range(NDC)]
        r_xn1 = p.region("R1n")
        for t in range(NTC):
            load_xtiles(src, rsrc, xt, r_xt, t, tag="R1ld")
            emit_ln(xt, [r_xt], xa, r_xa)
            emit_ln(xa, [r_xa], xn1, r_xn1, g=s_ln1_w, b=s_ln1_b,
                    rgb=[r_ln1_w, r_ln1_b])
            for dc in range(NDC):
                if t == NTC - 1:
                    vcopy(s_bcc[:, dc:dc + 1], xn1[dc][:, TC - 1:TC],
                          [r_xn1], [r_bcc], "R1bc")
                p.dma("sync", xn1_scr[dc * 128:(dc + 1) * 128,
                                        t * TC:(t + 1) * TC],
                      xn1[dc], reads=[r_xn1], writes=[sreg["xn1_scr"]],
                      tag="R1st")

    def phase_cshift(bnd_tile, r_bnd, ccname):
        p.dma("sync", cc_in[ccname], s_bcc, reads=[r_bcc],
              writes=[sreg["cin_" + ccname]], tag=ccname + "a")
        p.collective("AllGather", ALU.bypass, [cc_in[ccname]], [cc_out[ccname]],
                     reads=[sreg["cin_" + ccname]],
                     writes=[sreg["cout_" + ccname]])
        p.dma("sync", s_cc[:, 0:NDC], cc_out[ccname][0:128, :],
              reads=[sreg["cout_" + ccname]], writes=[r_cc], tag=ccname + "l")
        tss(bnd_tile, s_cc[:, 0:NDC], s_parity[:, 0:1], ALU.mult,
            [r_cc, r_parity], [r_bnd], ccname + "m")

    def phase_r1b():
        wsl = [wa.alloc(f"Kw{dc}", D) for dc in range(NDC)]
        r_wsl = p.region("Kw")
        mx = [wa.alloc(f"Km{dc}", TC) for dc in range(NDC)]
        r_mx = p.region("Km")
        xnn = [af.alloc(f"Kx{dc}", TC + 1) for dc in range(NDC)]
        r_xn = p.region("Kx")
        dd = af.alloc("Kd", TC)
        r_dd = p.region("Kd")
        ko = [af.alloc(f"Ko{i}", TC) for i in range(4)]
        r_ko = [p.region(f"Ko{i}") for i in range(4)]

        for (wname, sv, rv, dst_s, dname, sig) in [
                ("L1_attn_w_key", s_tmk, r_tmk, k_scr, "k_scr", False),
                ("L1_attn_w_value", s_tmv, r_tmv, v_scr, "v_scr", False),
                ("L1_attn_w_recept", s_tmr, r_tmr, sr_scr, "sr_scr", True)]:
            for dc in range(NDC):
                p.dma("gpsimd", wsl[dc],
                      ins[wname][dc * 128:(dc + 1) * 128, :],
                      writes=[r_wsl], tag="Kwl")
            for t in range(NTC):
                load_shifted(xn1_scr, sreg["xn1_scr"], xnn, r_xn, t,
                             s_bnd1, r_bnd1, "Kxl")
                for dc in range(NDC):
                    cur = xnn[dc][:, 1:TC + 1]
                    lx = xnn[dc][:, 0:TC]
                    tt(ALU.subtract, dd, cur, lx, [r_xn], [r_dd], "Kmd")
                    v_op(lambda e, dc=dc, lx=lx, sv=sv: e.scalar_tensor_tensor(
                        mx[dc], dd, sv[:, dc:dc + 1], lx, ALU.mult, ALU.add),
                        [r_dd, rv, r_xn], [r_mx], "Kmx")
                for dc2 in range(NDC):
                    i = dc2 % 4
                    bank = dc2 % 4
                    mm_acc(psb[bank],
                           [(wsl[dc][:, dc2 * 128:(dc2 + 1) * 128], mx[dc])
                            for dc in range(NDC)],
                           [r_wsl, r_mx], rps[bank], "Kmm")
                    if sig:
                        act(ko[i], psb[bank], AF.Sigmoid, [rps[bank]],
                            [r_ko[i]], tag="Ksg")
                    else:
                        scopy(ko[i], psb[bank], [rps[bank]], [r_ko[i]],
                              tag="Kev")
                    p.dma("sync", dst_s[dc2 * 128:(dc2 + 1) * 128,
                                          t * TC:(t + 1) * TC],
                          ko[i], reads=[r_ko[i]], writes=[sreg[dname]],
                          tag="Kst")

    def phase_r2a():
        kt = [af.alloc(f"Qk{i}", TC) for i in range(2)]
        r_kt = [p.region(f"Qk{i}") for i in range(2)]
        vt = [af.alloc(f"Qv{i}", TC) for i in range(2)]
        r_vt = [p.region(f"Qv{i}") for i in range(2)]
        gt = [af.alloc(f"Qg{i}", TC) for i in range(2)]
        r_gt = [p.region(f"Qg{i}") for i in range(2)]
        gv = [af.alloc(f"Qgv{i}", TC) for i in range(2)]
        r_gv = [p.region(f"Qgv{i}") for i in range(2)]
        lamb = af.alloc("Ql", TC)
        r_lamb = p.region("Ql")
        ao = [af.alloc(f"Qa{i}", TC) for i in range(2)]
        r_ao = [p.region(f"Qa{i}") for i in range(2)]
        bo = [af.alloc(f"Qb{i}", TC) for i in range(2)]
        r_bo = [p.region(f"Qb{i}") for i in range(2)]
        for t in range(NTC):
            for dc in range(NDC):
                i = dc % 2
                p.dma("sync", kt[i],
                      k_scr[dc * 128:(dc + 1) * 128, t * TC:(t + 1) * TC],
                      reads=[sreg["k_scr"]], writes=[r_kt[i]], tag="Qkl")
                p.dma("sync", vt[i],
                      v_scr[dc * 128:(dc + 1) * 128, t * TC:(t + 1) * TC],
                      reads=[sreg["v_scr"]], writes=[r_vt[i]], tag="Qvl")
                act(gt[i], kt[i], AF.Exp, [r_kt[i]], [r_gt[i]], tag="Qg")
                tt(ALU.mult, gv[i], gt[i], vt[i], [r_gt[i], r_vt[i]],
                   [r_gv[i]], "Qgv")
                vcopy(lamb, s_lam[:, dc:dc + 1].broadcast_to((128, TC)),
                      [r_lam], [r_lamb], "Qlb")
                scan(ao[i], lamb, gv[i], s_acar[:, dc:dc + 1],
                     [r_lamb, r_gv[i], r_acar], [r_ao[i]], "Qas")
                vcopy(s_acar[:, dc:dc + 1], ao[i][:, TC - 1:TC], [r_ao[i]],
                      [r_acar], "Qac")
                scan(bo[i], lamb, gt[i], s_bcar[:, dc:dc + 1],
                     [r_lamb, r_gt[i], r_bcar], [r_bo[i]], "Qbs")
                vcopy(s_bcar[:, dc:dc + 1], bo[i][:, TC - 1:TC], [r_bo[i]],
                      [r_bcar], "Qbc")
                p.dma("sync", al_scr[dc * 128:(dc + 1) * 128,
                                       t * TC:(t + 1) * TC],
                      ao[i], reads=[r_ao[i]], writes=[sreg["al_scr"]],
                      tag="Qast")
                p.dma("sync", bl_scr[dc * 128:(dc + 1) * 128,
                                       t * TC:(t + 1) * TC],
                      bo[i], reads=[r_bo[i]], writes=[sreg["bl_scr"]],
                      tag="Qbst")

    def phase_c4():
        p.dma("gpsimd", cc_in["c4"][:, 0:NDC], s_acar, reads=[r_acar],
              writes=[sreg["cin_c4"]], tag="c4a")
        p.dma("gpsimd", cc_in["c4"][:, NDC:2 * NDC], s_bcar, reads=[r_bcar],
              writes=[sreg["cin_c4"]], tag="c4b")
        p.collective("AllGather", ALU.bypass, [cc_in["c4"]], [cc_out["c4"]],
                     reads=[sreg["cin_c4"]], writes=[sreg["cout_c4"]])
        p.dma("sync", s_ab0, cc_out["c4"][0:128, :],
              reads=[sreg["cout_c4"]], writes=[r_ab0], tag="c4l")
        tss(s_ab0, s_ab0, s_parity[:, 0:1], ALU.mult, [r_ab0, r_parity],
            [r_ab0], "c4m")

    def phase_r2b(src, rsrc):
        wo = [wa.alloc(f"Ww{dc}", D) for dc in range(NDC)]
        r_wo = p.region("Ww")
        for dc in range(NDC):
            p.dma("gpsimd", wo[dc],
                  ins["L1_attn_w_output"][dc * 128:(dc + 1) * 128, :],
                  writes=[r_wo], tag="Wwl")
        wkb = [wa.alloc(f"Wk{dc}", TC) for dc in range(NDC)]
        r_wkb = p.region("Wk")
        xn2 = [bfa.alloc(f"Wn{dc}", TC) for dc in range(NDC)]
        r_xn2 = p.region("Wn")
        kt = [af.alloc(f"Wkt{i}", TC) for i in range(2)]
        r_kt = [p.region(f"Wkt{i}") for i in range(2)]
        vt = [af.alloc(f"Wvt{i}", TC) for i in range(2)]
        r_vt = [p.region(f"Wvt{i}") for i in range(2)]
        alt = [af.alloc(f"Wa{i}", TC) for i in range(2)]
        r_alt = [p.region(f"Wa{i}") for i in range(2)]
        blt = [af.alloc(f"Wb{i}", TC) for i in range(2)]
        r_blt = [p.region(f"Wb{i}") for i in range(2)]
        srt = [af.alloc(f"Wr{i}", TC) for i in range(2)]
        r_srt = [p.region(f"Wr{i}") for i in range(2)]
        iob = af.alloc("Wio", TC)
        r_iob = p.region("Wio")
        lamt = af.alloc("Wlm", TC)
        r_lamt = p.region("Wlm")
        gt = af.alloc("Wg", TC)
        r_gt = p.region("Wg")
        gv = af.alloc("Wgv", TC)
        r_gv = p.region("Wgv")
        num = af.alloc("Wnu", TC)
        r_num = p.region("Wnu")
        den = af.alloc("Wde", TC)
        r_den = p.region("Wde")
        xbt = [bfa.alloc(f"Wxb{dc}", TC) for dc in range(NDC)]
        r_xb = p.region("Wxb")
        xa = [waf.alloc(f"Wxa{dc}", TC) for dc in range(NDC)]
        r_xa = p.region("Wxa")
        xt8 = [waf.alloc(f"Wx8{dc}", TC) for dc in range(NDC)]
        r_xt8 = p.region("Wx8")

        for t in range(NTC):
            p.dma("sync", s_iorow, ins["iota_row"][0:1, t * TC:(t + 1) * TC],
                  writes=[r_iorow], tag="Wio")
            t_op(lambda e: e.matmul(psb[6], s_ones_row, s_iorow,
                                    start=True, stop=True),
                 [r_iorow, r_ones_row], [rps[6]], "Wioc")
            vcopy(iob, psb[6], [rps[6]], [r_iob])
            for dc in range(NDC):
                i = dc % 2
                p.dma("sync", kt[i],
                      k_scr[dc * 128:(dc + 1) * 128, t * TC:(t + 1) * TC],
                      reads=[sreg["k_scr"]], writes=[r_kt[i]], tag="Wkl")
                p.dma("sync", vt[i],
                      v_scr[dc * 128:(dc + 1) * 128, t * TC:(t + 1) * TC],
                      reads=[sreg["v_scr"]], writes=[r_vt[i]], tag="Wvl")
                if t == 0:
                    v_op(lambda e, i=i: e.memset(alt[i][:, 0:1], 0.0),
                         [], [r_alt[i]], "Waz")
                    p.dma("sync", alt[i][:, 1:TC],
                          al_scr[dc * 128:(dc + 1) * 128, 0:TC - 1],
                          reads=[sreg["al_scr"]], writes=[r_alt[i]], tag="Wal")
                    v_op(lambda e, i=i: e.memset(blt[i][:, 0:1], 0.0),
                         [], [r_blt[i]], "Wbz")
                    p.dma("sync", blt[i][:, 1:TC],
                          bl_scr[dc * 128:(dc + 1) * 128, 0:TC - 1],
                          reads=[sreg["bl_scr"]], writes=[r_blt[i]], tag="Wbl")
                else:
                    p.dma("sync", alt[i],
                          al_scr[dc * 128:(dc + 1) * 128,
                                 t * TC - 1:t * TC + TC - 1],
                          reads=[sreg["al_scr"]], writes=[r_alt[i]], tag="Wal")
                    p.dma("sync", blt[i],
                          bl_scr[dc * 128:(dc + 1) * 128,
                                 t * TC - 1:t * TC + TC - 1],
                          reads=[sreg["bl_scr"]], writes=[r_blt[i]], tag="Wbl")
                p.dma("sync", srt[i],
                      sr_scr[dc * 128:(dc + 1) * 128, t * TC:(t + 1) * TC],
                      reads=[sreg["sr_scr"]], writes=[r_srt[i]], tag="Wsl")
                act(gt, kt[i], AF.Exp, [r_kt[i]], [r_gt], tag="Wge")
                tt(ALU.mult, gv, gt, vt[i], [r_gt, r_vt[i]], [r_gv], "Wgv")
                s_op(lambda e, dc=dc: e.activation(
                    lamt, iob, AF.Exp, scale=s_negw[:, dc:dc + 1]),
                    [r_iob, r_negw], [r_lamt], "Wlam")
                v_op(lambda e, dc=dc, i=i: e.scalar_tensor_tensor(
                    alt[i], lamt, s_ab0[:, dc:dc + 1], alt[i], ALU.mult, ALU.add),
                    [r_lamt, r_ab0, r_alt[i]], [r_alt[i]], "Wat")
                v_op(lambda e, dc=dc, i=i: e.scalar_tensor_tensor(
                    blt[i], lamt, s_ab0[:, NDC + dc:NDC + dc + 1], blt[i],
                    ALU.mult, ALU.add),
                    [r_lamt, r_ab0, r_blt[i]], [r_blt[i]], "Wbt")
                v_op(lambda e, dc=dc, i=i: e.scalar_tensor_tensor(
                    num, gv, s_Eu[:, dc:dc + 1], alt[i], ALU.mult, ALU.add),
                    [r_gv, r_Eu, r_alt[i]], [r_num], "Wnum")
                v_op(lambda e, dc=dc, i=i: e.scalar_tensor_tensor(
                    den, gt, s_Eu[:, dc:dc + 1], blt[i], ALU.mult, ALU.add),
                    [r_gt, r_Eu, r_blt[i]], [r_den], "Wden")
                v_op(lambda e: e.reciprocal(den, den), [r_den], [r_den], "Wrd")
                tt(ALU.mult, num, num, den, [r_num, r_den], [r_num], "Wwkv")
                if debug_r2b:
                    p.dma("gpsimd", P_scr[dc * 128:(dc + 1) * 128,
                                          t * TC:(t + 1) * TC],
                          alt[i], reads=[r_alt[i]], writes=[sreg["P_scr"]],
                          tag="DBGa")
                    p.dma("gpsimd", P_scr[1024 + dc * 128:1024 + (dc + 1) * 128,
                                          t * TC:(t + 1) * TC],
                          blt[i], reads=[r_blt[i]], writes=[sreg["P_scr"]],
                          tag="DBGb")
                    p.dma("gpsimd", y_scr[dc * 128:(dc + 1) * 128,
                                          t * TC:(t + 1) * TC],
                          num, reads=[r_num], writes=[sreg["y_scr"]],
                          tag="DBGn")
                    p.dma("gpsimd", y_scr[1024 + dc * 128:1024 + (dc + 1) * 128,
                                          t * TC:(t + 1) * TC],
                          lamt, reads=[r_lamt], writes=[sreg["y_scr"]],
                          tag="DBGl")
                tt(ALU.mult, wkb[dc], num, srt[i], [r_num, r_srt[i]], [r_wkb],
                   "Wsr")
            load_xtiles(src, rsrc, xt8, r_xt8, t, tag="Wxl")
            emit_ln(xt8, [r_xt8], xa, r_xa)
            for dc2 in range(NDC):
                bank = dc2 % 4
                mm_acc(psb[bank],
                       [(wo[dc][:, dc2 * 128:(dc2 + 1) * 128], wkb[dc])
                        for dc in range(NDC)],
                       [r_wo, r_wkb], rps[bank], "Wout")
                v_op(lambda e, dc2=dc2, bank=bank: e.tensor_tensor(
                    xbt[dc2], xa[dc2], psb[bank], ALU.add),
                    [r_xa, rps[bank]], [r_xb], "Wxb")
                p.dma("sync", xb_scr[dc2 * 128:(dc2 + 1) * 128,
                                       t * TC:(t + 1) * TC],
                      xbt[dc2], reads=[r_xb], writes=[sreg["xb_scr"]],
                      tag="Wxbs")
            emit_ln(xbt, [r_xb], xn2, r_xn2, g=s_ln2_w, b=s_ln2_b,
                    rgb=[r_ln2_w, r_ln2_b])
            for dc in range(NDC):
                if t == NTC - 1:
                    vcopy(s_bcc[:, dc:dc + 1], xn2[dc][:, TC - 1:TC],
                          [r_xn2], [r_bcc], "Wn2b")
                p.dma("sync", xn2_scr[dc * 128:(dc + 1) * 128,
                                        t * TC:(t + 1) * TC],
                      xn2[dc], reads=[r_xn2], writes=[sreg["xn2_scr"]],
                      tag="Wn2s")

    def phase_r3a():
        frc = [wa.alloc(f"Ff{dc}", D) for dc in range(NDC)]
        r_frc = p.region("Ff")
        for dc in range(NDC):
            p.dma("gpsimd", frc[dc],
                  ins["L1_attn_f_w_recept"][dc * 128:(dc + 1) * 128, :],
                  writes=[r_frc], tag="Ffl")
        mfr = [wa.alloc(f"Fm{dc}", TC) for dc in range(NDC)]
        r_mfr = p.region("Fm")
        xnn = [af.alloc(f"Fx{dc}", TC + 1) for dc in range(NDC)]
        r_xn = p.region("Fx")
        dd = af.alloc("Fd", TC)
        r_dd = p.region("Fd")
        so = [af.alloc(f"Fo{i}", TC) for i in range(4)]
        r_so = [p.region(f"Fo{i}") for i in range(4)]
        for t in range(NTC):
            load_shifted(xn2_scr, sreg["xn2_scr"], xnn, r_xn, t,
                         s_bnd2, r_bnd2, "Fxl")
            for dc in range(NDC):
                cur = xnn[dc][:, 1:TC + 1]
                lx = xnn[dc][:, 0:TC]
                tt(ALU.subtract, dd, cur, lx, [r_xn], [r_dd], "Fmd")
                v_op(lambda e, dc=dc, lx=lx: e.scalar_tensor_tensor(
                    mfr[dc], dd, s_ftmr[:, dc:dc + 1], lx, ALU.mult, ALU.add),
                    [r_dd, r_ftmr, r_xn], [r_mfr], "Fmx")
            for dc2 in range(NDC):
                i = dc2 % 4
                bank = dc2 % 4
                mm_acc(psb[bank],
                       [(frc[dc][:, dc2 * 128:(dc2 + 1) * 128], mfr[dc])
                        for dc in range(NDC)],
                       [r_frc, r_mfr], rps[bank], "Fmm")
                act(so[i], psb[bank], AF.Sigmoid, [rps[bank]], [r_so[i]],
                    tag="Fsg")
                p.dma("sync", sg_scr[dc2 * 128:(dc2 + 1) * 128,
                                       t * TC:(t + 1) * TC],
                      so[i], reads=[r_so[i]], writes=[sreg["sg_scr"]],
                      tag="Fst")

    def phase_r3b(src, rsrc, dst, rdst):
        HRH = HR // 2
        NHH2 = NHR // 2
        fwk = [wa.alloc(f"Gk{dc}", HRH) for dc in range(NDC)]
        r_fwk = p.region("Gk")
        fq = [wa.alloc(f"Gq{hh}", TC) for hh in range(NHH2)]
        r_fq = p.region("Gq")
        mfk = [wa.alloc(f"Gm{dc}", TC) for dc in range(NDC)]
        r_mfk = p.region("Gm")
        fwv = [wa.alloc(f"Gv{i}", D) for i in range(2)]
        r_fwv = [p.region(f"Gv{i}") for i in range(2)]
        xnn = [af.alloc(f"Gx{dc}", TC + 1) for dc in range(NDC)]
        r_xn = p.region("Gx")
        dd = af.alloc("Gd", TC)
        r_dd = p.region("Gd")
        vk = [af.alloc(f"Gw{dc}", TC) for dc in range(NDC)]
        r_vk = p.region("Gw")
        sgt = [af.alloc(f"Gs{i}", TC) for i in range(2)]
        r_sg = [p.region(f"Gs{i}") for i in range(2)]
        xt = [bfa.alloc(f"Gxt{i}", TC) for i in range(2)]
        r_xt = [p.region(f"Gxt{i}") for i in range(2)]
        xbt = [bfa.alloc(f"Gxb{i}", TC) for i in range(2)]
        r_xbt = [p.region(f"Gxb{i}") for i in range(2)]
        xo = [bfa.alloc(f"Go{i}", TC) for i in range(2)]
        r_xo = [p.region(f"Go{i}") for i in range(2)]
        rtmp = [waf.alloc(f"Gt{i}", TC) for i in range(2)]
        r_rtmp = p.region("Gt")

        for t in range(NTC):
            load_shifted(xn2_scr, sreg["xn2_scr"], xnn, r_xn, t,
                         s_bnd2, r_bnd2, "Gxl")
            for dc in range(NDC):
                cur = xnn[dc][:, 1:TC + 1]
                lx = xnn[dc][:, 0:TC]
                tt(ALU.subtract, dd, cur, lx, [r_xn], [r_dd], "Gmd")
                v_op(lambda e, dc=dc, lx=lx: e.scalar_tensor_tensor(
                    mfk[dc], dd, s_ftmk[:, dc:dc + 1], lx, ALU.mult, ALU.add),
                    [r_dd, r_ftmk, r_xn], [r_mfk], "Gmx")
            for half in range(2):
                for dc in range(NDC):
                    p.dma("gpsimd", fwk[dc],
                          ins["L1_attn_f_w_key"][dc * 128:(dc + 1) * 128,
                                                 half * HRH:(half + 1) * HRH],
                          writes=[r_fwk], tag="Gkl")
                for hh in range(NHH2):
                    bank = hh % 2
                    mm_acc(psb[bank],
                           [(fwk[dc][:, hh * 128:(hh + 1) * 128], mfk[dc])
                            for dc in range(NDC)],
                           [r_fwk, r_mfk], rps[bank], "Gfk")
                    tss(rtmp[bank], psb[bank], 0.0, ALU.max,
                        [rps[bank]], [r_rtmp], "Grl")
                    s_op(lambda e, hh=hh, bank=bank: e.activation(
                        fq[hh], rtmp[bank], AF.Square),
                        [r_rtmp], [r_fq], "Gsq")
                for hh in range(NHH2):
                    hc = half * NHH2 + hh
                    sl = hh % 2
                    p.dma("gpsimd", fwv[sl],
                          ins["L1_attn_f_w_value"][hc * 128:(hc + 1) * 128, :],
                          writes=[r_fwv[sl]], tag="Gvl")
                    mm_banks([(psb[dc], fwv[sl][:, dc * 128:(dc + 1) * 128],
                               fq[hh]) for dc in range(NDC)],
                             [r_fwv[sl], r_fq], [rps[d] for d in range(NDC)],
                             hh == 0, hh == NHH2 - 1, "Gvk")
                for dc in range(NDC):
                    if half == 0:
                        scopy(vk[dc], psb[dc], [rps[dc]], [r_vk], tag="Gve")
                    else:
                        v_op(lambda e, dc=dc: e.scalar_tensor_tensor(
                            vk[dc], psb[dc], 1.0, vk[dc], ALU.mult, ALU.add),
                            [rps[dc], r_vk], [r_vk], "Gva")
            for dc in range(NDC):
                i = dc % 2
                p.dma("sync", sgt[i],
                      sg_scr[dc * 128:(dc + 1) * 128, t * TC:(t + 1) * TC],
                      reads=[sreg["sg_scr"]], writes=[r_sg[i]], tag="Gsl")
                p.dma("sync", xt[i],
                      src[dc * 128:(dc + 1) * 128, t * TC:(t + 1) * TC],
                      reads=[rsrc], writes=[r_xt[i]], tag="Gxtl")
                p.dma("sync", xbt[i],
                      xb_scr[dc * 128:(dc + 1) * 128, t * TC:(t + 1) * TC],
                      reads=[sreg["xb_scr"]], writes=[r_xbt[i]], tag="Gxbl")
                tt(ALU.mult, rtmp[i], sgt[i], vk[dc], [r_sg[i], r_vk],
                   [r_rtmp], "Gc")
                tt(ALU.add, rtmp[i], rtmp[i], xbt[i],
                   [r_rtmp, r_xbt[i]], [r_rtmp], "Ga1")
                tt(ALU.add, xo[i], rtmp[i], xt[i], [r_rtmp, r_xt[i]],
                   [r_xo[i]], "Ga2")
                p.dma("sync", dst[dc * 128:(dc + 1) * 128, t * TC:(t + 1) * TC],
                      xo[i], reads=[r_xo[i]], writes=[rdst], tag="Gst")

    def phase_PZ(src, rsrc):
        xr = [af.alloc(f"Zx{i}", TC) for i in range(2)]
        r_xr = [p.region(f"Zx{i}") for i in range(2)]
        eg = [af.alloc(f"Ze{i}", 128) for i in range(4)]
        r_eg = [p.region(f"Ze{i}") for i in range(4)]
        for t in range(NTC):
            for dc in range(NDC):
                i = dc % 2
                p.dma("sync", xr[i],
                      src[dc * 128:(dc + 1) * 128, t * TC:(t + 1) * TC],
                      reads=[rsrc], writes=[r_xr[i]], tag="Zld")
                for tb in range(TC // 128):
                    j = (dc * 4 + tb) % 4
                    bank = 4 + j % 2
                    pe_tr(psb[bank][:, 0:128], xr[i][:, tb * 128:(tb + 1) * 128],
                          [r_xr[i]], rps[bank])
                    vcopy(eg[j], psb[bank][:, 0:128], [rps[bank]], [r_eg[j]])
                    p.dma("sync",
                          out_ap[t * TC + tb * 128:t * TC + (tb + 1) * 128,
                                 dc * 128:(dc + 1) * 128],
                          eg[j], reads=[r_eg[j]], writes=[sreg["out"]],
                          tag="Zst")

    schedule = [
        ("P0", phase_P0, "A"),
        ("P1", lambda: phase_ffn0(xfA, sreg["xfA"], xfB, sreg["xfB"], "P1"), "B"),
        ("P2a1", lambda: phase_P2a1(xfB, sreg["xfB"]), "B"),
        ("P2a2", phase_P2a2, "B"),
        ("prepMoE", prep_moe_transpose, "B"),
        ("C1", phase_C1, "B"),
        ("P2b", phase_P2b_real, "B"),
        ("C2", phase_C2, "B"),
        ("P2c2", lambda: phase_P2c(xfB, sreg["xfB"], xfA, sreg["xfA"]), "A"),
        ("P3", lambda: phase_ffn0(xfA, sreg["xfA"], xfB, sreg["xfB"], "P3"), "B"),
        ("P4pre", lambda: phase_moepre(xfB, sreg["xfB"], "M4"), "B"),
        ("P4", lambda: phase_moe(xfB, sreg["xfB"], xfA, sreg["xfA"], "Q4"), "A"),
        ("P5r1", lambda: phase_r1(xfA, sreg["xfA"]), "A"),
        ("C3", lambda: phase_cshift(s_bnd1, r_bnd1, "c3"), "A"),
        ("P5r1b", phase_r1b, "A"),
        ("P5r2a", phase_r2a, "A"),
        ("C4", phase_c4, "A"),
        ("P5r2b", lambda: phase_r2b(xfA, sreg["xfA"]), "A"),
        ("C5", lambda: phase_cshift(s_bnd2, r_bnd2, "c5"), "A"),
        ("P5r3a", phase_r3a, "A"),
        ("P5r3b", lambda: phase_r3b(xfA, sreg["xfA"], xfB, sreg["xfB"]), "B"),
        ("P6pre", lambda: phase_moepre(xfB, sreg["xfB"], "M6"), "B"),
        ("P6", lambda: phase_moe(xfB, sreg["xfB"], xfA, sreg["xfA"], "Q6"), "A"),
    ]

    cur = "A"
    for name, fn, after in schedule:
        fn()
        phase_reset()
        cur = after
        if upto is not None and name == upto:
            break

    phase_PZ(xfA if cur == "A" else xfB, sreg["xfA" if cur == "A" else "xfB"])
    phase_reset()

    dump_names = []
    for nm in dumps:
        tsrc = scr_tensors[nm]
        shp = list(tsrc.shape)
        dap = nc.dram_tensor("dbg_" + nm, shp, tsrc.dtype,
                             kind="ExternalOutput").ap()
        rows = shp[0]
        for r0 in range(0, rows, 128):
            nr = min(128, rows - r0)
            p.dma("gpsimd", dap[r0:r0 + nr, :], tsrc[r0:r0 + nr, :],
                  reads=[sreg[nm]], writes=[], tag="dump")
        dump_names.append("dbg_" + nm)

    p.barrier()
    p.build()
    return nc, stack, dump_names


def make_in_maps(inputs):
    x = np.ascontiguousarray(inputs["L0_attn_x"])
    weights = {}
    for k, v in inputs.items():
        if k in ("L0_attn_x", "L0_attn_res_scale"):
            continue
        a = np.ascontiguousarray(v)
        weights[k] = a
    def col128(v):
        v = np.asarray(v).reshape(-1)
        n = v.shape[0] // 128
        return np.ascontiguousarray(v.reshape(n, 128).T)

    for k in ("L0_attn_ln_g", "L0_attn_ln_b", "L0_attn_sq_b", "L0_attn_srw_b",
              "L1_attn_ln1_w", "L1_attn_ln1_b", "L1_attn_ln2_w",
              "L1_attn_ln2_b", "L1_attn_time_decay", "L1_attn_time_first",
              "L1_attn_time_mix_k", "L1_attn_time_mix_v", "L1_attn_time_mix_r",
              "L1_attn_f_time_mix_k", "L1_attn_f_time_mix_r",
              "L0_attn_conv_b", "L0_attn_dt_b", "L0_attn_D"):
        weights[k] = col128(weights[k])
    cw = np.asarray(weights["L0_attn_conv_w"])
    weights["L0_attn_conv_w"] = np.ascontiguousarray(
        cw.reshape(NIC, 128, 4).transpose(1, 0, 2).reshape(128, NIC * 4))
    al = np.asarray(weights["L0_attn_A_log"])
    weights["L0_attn_A_log"] = np.ascontiguousarray(
        al.reshape(NIC, 128, NST).transpose(1, 0, 2).reshape(128, NIC * NST))
    weights["L1_ffn_w1"] = np.ascontiguousarray(
        inputs["L1_ffn_w1"]).reshape(NE * HM, D)
    weights["L1_ffn_w2"] = np.ascontiguousarray(
        inputs["L1_ffn_w2"]).reshape(NE * D, HM)
    ident = np.eye(128, dtype=np.float32)
    iota = np.ascontiguousarray(np.arange(T, dtype=np.float32)[None, :])
    rs = np.ascontiguousarray(np.broadcast_to(
        np.asarray(inputs["L0_attn_res_scale"], np.float32).reshape(1, 1),
        (128, 1)))
    maps = []
    for c in range(8):
        b, h = c // 2, c % 2
        m = dict(weights)
        m["x_in"] = np.ascontiguousarray(x[b, h * T:(h + 1) * T, :])
        m["parity"] = np.full((128, 1), float(h), np.float32)
        m["iota_row"] = iota
        m["identity"] = ident
        m["res_scale_col"] = rs
        maps.append(m)
    return maps


_CACHE = {}


def kernel(**inputs):
    if "full" not in _CACHE:
        _CACHE["full"] = build_program()
    nc, stack, _ = _CACHE["full"]
    maps = make_in_maps(inputs)
    res = run_bass_kernel_spmd(nc, maps, core_ids=list(range(8)))
    out = np.empty((4, 2 * T, D), dtype=np.float32)
    for c in range(8):
        b, h = c // 2, c % 2
        out[b, h * T:(h + 1) * T, :] = res.results[c]["out"]
    return out
